# version 01
# baseline (speedup 1.0000x reference)
import numpy as np
import ml_dtypes
import concourse.bass as bass
import concourse.mybir as mybir
from concourse.bass_core_rust import IndirectOffsetOnAxis as IOA

F32 = mybir.dt.float32
BF16 = mybir.dt.bfloat16
I32 = mybir.dt.int32
AF = mybir.ActivationFunctionType
ALU = mybir.AluOpType
AXX = mybir.AxisListType.X
BF = ml_dtypes.bfloat16

D, DC, NH, DH = 2048, 16, 16, 128
T, NT, HAL, TE, NTE = 1024, 8, 512, 1536, 12
SCALE = float(np.float32(DH) ** -0.5)
BIGNEG = -1e9
E, CAP = 16, 1024
ISH, ISC = 2816, 22
IPAD = 1536
GUARD = float(2 ** 20)
LISTW = 16400


class EngProxy:

    CHAIN = {"tensor_scalar", "tensor_tensor", "tensor_copy", "tensor_reduce",
             "reduce_max", "reduce_sum", "reciprocal", "memset", "activation",
             "iota", "scalar_tensor_tensor", "tensor_scalar_min",
             "tensor_scalar_add", "tensor_scalar_mul", "tensor_scalar_max",
             "tensor_single_scalar", "affine_select", "bn_stats", "bn_aggr",
             "tensor_add", "tensor_sub", "tensor_mul", "tensor_max",
             "partition_broadcast", "memzero"}

    def __init__(self, eng, X=None, chname=None):
        self._eng = eng
        self.last = None
        self._X = X
        self._chname = chname
        if X is not None and chname is not None:
            X.sem(chname)

    def __getattr__(self, name):
        f = getattr(self._eng, name)
        if not callable(f):
            return f
        chain = (name in EngProxy.CHAIN) and self._chname is not None

        def g(*a, **k):
            if chain and self._X.c[self._chname] > 0:
                self._eng.wait_ge(self._X.sem(self._chname), self._X.c[self._chname])
            r = f(*a, **k)
            if hasattr(r, "then_inc"):
                self.last = r
                if chain:
                    self._X.inc(r, self._chname, 1)
            return r
        return g


class Ctx:
    def __init__(self, nc):
        self.nc = nc
        self.c = {}
        self.h = {}

    def sem(self, name):
        if name not in self.h:
            self.h[name] = self.nc.alloc_semaphore(name)
            self.c[name] = 0
        return self.h[name]

    def inc(self, inst, name, n=1):
        inst.then_inc(self.sem(name), n)
        self.c[name] += n

    def tick(self, eng, name):
        if getattr(eng, "_chname", None) == name:
            return
        self.inc(eng.last, name, 1)

    def dma(self, eng, out, in_, name):
        self.inc(eng.dma_start(out=out, in_=in_), name, 16)

    def dmat(self, out, in_, name):
        self.inc(self.nc.sync.dma_start(out=out, in_=in_, transpose=True), name, 16)

    def wait(self, eng, name):
        eng.wait_ge(self.sem(name), self.c[name])


def build_nc(stage="full", dbg=()):
    nc = bass.Bass("TRN2", target_bir_lowering=False, num_devices=8,
                   detect_race_conditions=False)
    X = Ctx(nc)
    sync = EngProxy(nc.sync)
    gps = EngProxy(nc.gpsimd, X, "gp")
    vec = EngProxy(nc.vector, X, "dv")
    sca = EngProxy(nc.scalar, X, "ac")
    ten = EngProxy(nc.tensor)

    def di(name, shape, dtype=F32):
        return nc.dram_tensor(name, list(shape), dtype, kind="ExternalInput").ap()

    xh = di("xh", [TE, D])
    wqT = di("wqT", [NH, 128, DC, DH])
    wkT = di("wkT", [NH, 128, DC, DH])
    wvT = di("wvT", [4, 128, DC, 512])
    woT = di("woT", [4, 128, DC, 512])
    gateT = di("gateT", [128, DC, E])
    wsg = di("wsg", [ISC, 128, DC, 128])
    wsu = di("wsu", [ISC, 128, DC, 128])
    wsdA = di("wsdA", [4, 128, 16, 512])
    wsdB = di("wsdB", [4, 128, 6, 512])
    wg = di("wg", [2, 3, 128, DC, 512])
    wu = di("wu", [2, 3, 128, DC, 512])
    wd = di("wd", [2, 4, 128, 12, 512])
    cosI = di("cosI", [128, TE])
    sinIS = di("sinIS", [128, TE])
    kmaski = di("kmask", [128, NTE])
    cidv = di("cidv", [1, 8])
    cidohc = di("cidohc", [8, 1])
    eohi = di("eoh", [E, 2])

    y_out = nc.dram_tensor("y", [T, D], F32, kind="ExternalOutput").ap()
    dbg_out = {}
    dbg_shapes = {"dbg_xnT": [D, TE], "dbg_q": [DH, T], "dbg_k": [DH, TE],
                  "dbg_xn2T": [D, T], "dbg_gu": [128, T], "dbg_g0": [128, T],
                  "dbg_aout": [D, T], "dbg_route": [T, 16], "dbg_lists": [1, LISTW],
                  "dbg_slot": [128, 32], "dbg_eo": [2 * CAP, D], "dbg_cnt": [8, 16],
                  "dbg_v": [TE, 512]}
    for nm in dbg:
        dbg_out[nm] = nc.dram_tensor(nm, dbg_shapes[nm], F32, kind="ExternalOutput").ap()

    lists_pay = nc.dram_tensor("lists_pay", [1, LISTW], F32, kind="Internal").ap()
    lists_all = nc.dram_tensor("lists_all", [8, LISTW], F32, kind="Internal").ap()
    xn2_hbm = nc.dram_tensor("xn2_hbm", [T, D], BF16, kind="Internal").ap()
    xnall = nc.dram_tensor("xnall", [8 * T, D], BF16, kind="Internal").ap()
    eo_hbm = nc.dram_tensor("eo_hbm", [2 * CAP, D], BF16, kind="Internal").ap()
    eoall = nc.dram_tensor("eoall", [16 * CAP, D], BF16, kind="Internal").ap()

    swp_np = np.zeros((128, 128), BF)
    for m in range(128):
        swp_np[m ^ 1, m] = 1
    tq_ = np.arange(128)[None, :]
    tk_ = np.arange(128)[:, None]
    cN = {
        "idc": np.eye(128, dtype=BF),
        "swp": swp_np,
        "mdiag": np.where(tk_ <= tq_, 0, BIGNEG).astype(BF),
        "mfar": np.where(tk_ > tq_, 0, BIGNEG).astype(BF),
        "triu": np.triu(np.ones((128, 128)), 1).astype(BF),
    }
    cN_ap = {k: nc.inline_tensor(v, k).ap() for k, v in cN.items()}
    onescol_b = nc.inline_tensor(np.ones((128, 1), BF), "onescol_b").ap()
    onesrow_f = nc.inline_tensor(np.ones((1, 128), np.float32), "onesrow_f").ap()
    iot16_c = nc.inline_tensor(
        np.broadcast_to(np.arange(16, dtype=np.float32), (128, 16)).copy(), "iot16").ap()
    iotmB_c = nc.inline_tensor(
        (np.broadcast_to(np.arange(16, dtype=np.float32), (128, 16)) - 4e4).copy(), "iotmB").ap()
    iotcol_c = nc.inline_tensor(np.arange(128, dtype=np.float32)[:, None].copy(), "iotcol").ap()
    eye8_f = nc.inline_tensor(np.eye(8, dtype=np.float32), "eye8").ap()
    eps_f = nc.inline_tensor(np.full((128, 1), 1e-5, np.float32), "epsc").ap()
    triu8_f = nc.inline_tensor(np.triu(np.ones((8, 8), np.float32), 1), "triu8").ap()

    A = nc.alloc_sbuf_tensor
    AT = nc.alloc_sbuf_tensor_at

    def addr(h):
        return nc.lookup_mloc(h).addr

    x_res = A("x_res", [128, NT, D], F32)
    fmA = A("fmA", [128, DC, TE], BF16)
    fmB = A("fmB", [128, 22, 1024], BF16)
    wslA = A("wslA", [128, 8192], BF16)
    wslB = A("wslB", [128, 8192], BF16)
    pool = A("pool", [128, 19008], mybir.dt.uint8)
    pb_ = addr(pool)
    fb_ = addr(fmB)
    vgrp = AT("vgrp", [128, NTE, 512], BF16, offset=fb_ + 16 * 2048)
    gstg = AT("gstg", [128, 2, D], BF16, offset=fb_ + 16 * 2048)
    tabs = AT("tabs", [128, 2, TE], BF16, offset=addr(wslB) + 8704)
    consts = AT("consts", [128, 5, 128], BF16, offset=addr(wslB) + 14848)
    stg = AT("stg", [128, D], F32, offset=pb_ + 0)
    qkraw = AT("qkraw", [128, 2560], BF16, offset=pb_ + 0)
    zrow = AT("zrow", [1, 512], F32, offset=pb_ + 5120)
    stgb = AT("stgb", [128, 2, D], BF16, offset=pb_ + 8192)
    ropeA = AT("ropeA", [128, 512], F32, offset=pb_ + 8192)
    ropeB = AT("ropeB", [128, 512], F32, offset=pb_ + 10240)
    Pt = AT("Pt", [128, 2, 512], BF16, offset=pb_ + 12288)
    bcz = AT("bcz", [128, 512], F32, offset=pb_ + 14336)
    sm = pb_ + 16384
    stats = AT("stats", [128, 16], F32, offset=sm + 0)
    kmk = AT("kmk", [128, NTE], F32, offset=sm + 64)
    rt = AT("rt", [128, 8, 16], F32, offset=sm + 128)
    tokr = AT("tokr", [128, NT, 8], F32, offset=sm + 640)
    slotm = AT("slotm", [128, 40], F32, offset=sm + 896)
    icol = AT("icol", [128, 8], I32, offset=sm + 1056)
    sids = AT("sids", [128, 8], F32, offset=sm + 1088)
    onesc = AT("onesc", [128, 1], BF16, offset=sm + 1120)
    onesrf = AT("onesrf", [1, 128], F32, offset=sm + 1152)
    iot16 = AT("iot16s", [128, 16], F32, offset=sm + 1664)
    iotmB = AT("iotmBs", [128, 16], F32, offset=sm + 1728)
    iotcol = AT("iotcols", [128, 1], F32, offset=sm + 1792)
    eye8 = AT("eye8s", [8, 8], F32, offset=sm + 1824)
    triu8 = AT("triu8s", [8, 8], F32, offset=sm + 1856)
    cid8 = AT("cid8s", [1, 8], F32, offset=sm + 1888)
    cidoh = AT("cidohs", [8, 1], F32, offset=sm + 1920)
    eoh = AT("eohs", [E, 2], F32, offset=sm + 1952)
    ccols = AT("ccolss", [128, 4], F32, offset=sm + 1984)
    crow = AT("crows", [1, 4, 16], F32, offset=sm + 2016)
    cnt8 = AT("cnt8s", [8, 16], F32, offset=sm + 2272)
    cnt8T = AT("cnt8Ts", [16, 8], F32, offset=sm + 2336)
    cume = AT("cumes", [8, 16], F32, offset=sm + 2368)
    col8 = AT("col8s", [8, 1], F32, offset=sm + 2432)
    row8 = AT("row8s", [1, 8], F32, offset=sm + 2464)
    row16 = AT("row16s", [1, 16], F32, offset=sm + 2496)
    rts = AT("rtss", [128, 16], BF16, offset=sm + 2560)
    epsc = AT("epscs", [128, 1], F32, offset=sm + 2592)

    PB = [nc.alloc_psum_tensor(f"pb{i}", [128, 512], F32) for i in range(6)]
    TBa = nc.alloc_psum_tensor("tba", [128, 512], BF16)
    TBb = nc.alloc_psum_tensor("tbb", [128, 512], BF16)

    IDC = consts[:, 0, :]
    SWP = consts[:, 1, :]
    MDIAG = consts[:, 2, :]
    MFAR = consts[:, 3, :]
    TRIU = consts[:, 4, :]

    def slab(dst, src_ap, name="lw"):
        X.wait(gps, "pe")
        X.dma(gps, dst, src_ap, name)

    def ts(out, in0, s1, s2=None, op0=ALU.mult, op1=None, eng=None):
        e = eng or vec
        if s2 is None and op1 is None:
            e.tensor_scalar(out, in0, s1, None, op0=op0)
        else:
            e.tensor_scalar(out, in0, s1, s2, op0=op0, op1=op1)

    def tt(out, a, b, op=ALU.mult, eng=None):
        (eng or vec).tensor_tensor(out, a, b, op=op)

    def ln_tile(xt, out_bf, sq_scratch):
        mcol, mneg, ss, yv, s0, r0, t1, rstd = (stats[:, i:i + 1] for i in range(8))
        vec.reduce_sum(mcol, xt, axis=AXX)
        ts(mcol, mcol, 1.0 / D)
        ts(mneg, mcol, -1.0)
        X.tick(vec, "dv")
        X.wait(sca, "dv")
        sca.activation(out=sq_scratch, in_=xt, func=AF.Square, bias=mneg, scale=1.0,
                       accum_out=ss)
        sca.activation(out=s0, in_=ss, func=AF.Sqrt, bias=epsc[:, :], scale=1.0 / D)
        X.tick(sca, "ac")
        X.wait(vec, "ac")
        ts(yv, ss, 1.0 / D, 1e-5, op0=ALU.mult, op1=ALU.add)
        vec.reciprocal(r0, s0)
        tt(t1, yv, r0, ALU.mult)
        tt(t1, s0, t1, ALU.add)
        ts(t1, t1, 0.5)
        vec.reciprocal(rstd, t1)
        ts(out_bf, xt, mcol, rstd, op0=ALU.subtract, op1=ALU.mult)
        X.tick(vec, "dv")

    def store_x(tag="st"):
        X.wait(sync, "dv")
        for tt_ in range(NT):
            X.dma(sync, y_out[tt_ * 128:(tt_ + 1) * 128, :], x_res[:, tt_, :], tag)
        X.wait(sync, tag)

    X.dma(sync, consts[:, 0, :], cN_ap["idc"][:, :], "lc")
    X.dma(sync, consts[:, 1, :], cN_ap["swp"][:, :], "lc")
    X.dma(sync, consts[:, 2, :], cN_ap["mdiag"][:, :], "lc")
    X.dma(sync, consts[:, 3, :], cN_ap["mfar"][:, :], "lc")
    X.dma(sync, consts[:, 4, :], cN_ap["triu"][:, :], "lc")
    X.dma(sync, onesc[:, :], onescol_b[:, :], "lc")
    X.dma(sync, onesrf[:, :], onesrow_f[:, :], "lc")
    X.dma(sync, iot16[:, :], iot16_c[:, :], "lc")
    X.dma(sync, iotmB[:, :], iotmB_c[:, :], "lc")
    X.dma(sync, iotcol[:, :], iotcol_c[:, :], "lc")
    X.dma(sync, eye8[:, :], eye8_f[:, :], "lc")
    X.dma(sync, triu8[:, :], triu8_f[:, :], "lc")
    X.dma(sync, cid8[:, :], cidv[:, :], "lc")
    X.dma(sync, cidoh[:, :], cidohc[:, :], "lc")
    X.dma(sync, eoh[:, :], eohi[:, :], "lc")
    X.dma(sync, kmk[:, :], kmaski[:, :], "lc")
    X.dma(sync, epsc[:, :], eps_f[:, :], "lc")
    X.dma(gps, tabs[:, 0, :], cosI[:, :], "lcg")
    X.dma(gps, tabs[:, 1, :], sinIS[:, :], "lcg")
    X.wait(ten, "lc")
    ten.matmul(PB[0][:, 0:3], onesrf[:, :], cid8[0:1, 0:3], start=True, stop=True)
    X.tick(ten, "pe")
    X.wait(vec, "pe")
    ts(ccols[:, 0:3], PB[0][:, 0:3], 1024.0)
    X.tick(vec, "dv")

    for te in range(NTE):
        own = te >= 4
        xt = x_res[:, te - 4, :] if own else stg[:, :]
        X.dma(sync, xt, xh[te * 128:(te + 1) * 128, :], "lx")
        X.wait(vec, "lx")
        X.wait(sca, "lx")
        X.wait(vec, "tr")
        X.wait(sca, "tr")
        xnst = stgb[:, te % 2, :]
        ln_tile(xt, xnst, stgb[:, (te + 1) % 2, :])
        X.wait(sync, "dv")
        for dch in range(DC):
            X.dmat(fmA[:, dch, te * 128:(te + 1) * 128],
                   xnst[:, dch * 128:(dch + 1) * 128], "tr")
        X.wait(sync, "tr")

    if "dbg_xnT" in dbg_out:
        X.wait(sync, "tr")
        X.wait(sync, "dv")
        X.wait(vec, "tr")
        for dch in range(DC):
            vec.tensor_copy(stg[:, 0:TE], fmA[:, dch, :])
            X.tick(vec, "dv")
            X.wait(sync, "dv")
            X.dma(sync, dbg_out["dbg_xnT"][dch * 128:(dch + 1) * 128, :], stg[:, 0:TE], "st")
            X.wait(vec, "st")

    if stage == "p1":
        store_x()
        return nc

    aout = fmB
    X.wait(ten, "tr")
    X.wait(ten, "lc")
    X.wait(vec, "lcg")
    X.wait(vec, "lc")
    X.wait(sca, "lc")
    wv_sb = wslA[:, 0:8192].rearrange("p (a b) -> p a b", a=DC)
    wq_sb = wslB[:, 0:2048].rearrange("p (a b) -> p a b", a=DC)
    wk_sb = wslB[:, 2048:4096].rearrange("p (a b) -> p a b", a=DC)
    for hg in range(4):
        slab(wv_sb[:, :, :], wvT[hg, :, :, :])
        X.wait(ten, "lw")
        for te in range(NTE):
            ps = PB[te % 2]
            X.wait(ten, "dv")
            for dch in range(DC):
                ten.matmul(ps[:, :], fmA[:, dch, te * 128:(te + 1) * 128],
                           wv_sb[:, dch, :], start=(dch == 0), stop=(dch == DC - 1))
            X.tick(ten, "pe")
            X.wait(vec, "pe")
            vec.tensor_copy(vgrp[:, te, :], ps[:, :])
            X.tick(vec, "dv")
        X.wait(ten, "dv")
        if stage == "p2v" and hg == 0:
            store_x()
            return nc
        if "dbg_v" in dbg_out and hg == 0:
            for te in range(NTE):
                vec.tensor_copy(stg[:, 0:512], vgrp[:, te, :])
                X.tick(vec, "dv")
                X.wait(sync, "dv")
                X.dma(sync, dbg_out["dbg_v"][te * 128:(te + 1) * 128, :], stg[:, 0:512], "st")
            X.wait(vec, "st")
        for hh in range(4):
            h = hg * 4 + hh
            slab(wq_sb[:, :, :], wqT[h, :, :, :])
            slab(wk_sb[:, :, :], wkT[h, :, :, :])
            X.wait(ten, "lw")
            X.wait(ten, "dv")
            X.wait(ten, "ac")
            for tg in range(2):
                for dch in range(DC):
                    ten.matmul(PB[tg][:, :], wq_sb[:, dch, :],
                               fmA[:, dch, 512 + tg * 512:1024 + tg * 512],
                               start=(dch == 0), stop=(dch == DC - 1))
            for sg in range(3):
                for dch in range(DC):
                    ten.matmul(PB[2 + sg][:, :], wk_sb[:, dch, :],
                               fmA[:, dch, sg * 512:(sg + 1) * 512],
                               start=(dch == 0), stop=(dch == DC - 1))
            X.tick(ten, "pe")
            X.wait(vec, "pe")
            for tg in range(2):
                vec.tensor_copy(qkraw[:, tg * 512:(tg + 1) * 512], PB[tg][:, :])
            for sg in range(3):
                vec.tensor_copy(qkraw[:, 1024 + sg * 512:1024 + (sg + 1) * 512],
                                PB[2 + sg][:, :])
            X.tick(vec, "dv")
            X.wait(ten, "dv")
            for (isq, nch) in ((True, 2), (False, 3)):
                for ch in range(nch):
                    lo = 512 + ch * 512 if isq else ch * 512
                    src = qkraw[:, ch * 512:(ch + 1) * 512] if isq \
                        else qkraw[:, 1024 + ch * 512:1024 + (ch + 1) * 512]
                    ps = PB[5]
                    ten.matmul(ps[:, :], SWP, src, start=True, stop=True)
                    X.tick(ten, "pe")
                    X.wait(vec, "pe")
                    tt(ropeA[:, :], src, tabs[:, 0, lo:lo + 512], ALU.mult)
                    tt(ropeB[:, :], ps[:, :], tabs[:, 1, lo:lo + 512], ALU.mult)
                    tt(src, ropeA[:, :], ropeB[:, :], ALU.add)
                    X.tick(vec, "dv")
                    X.wait(ten, "dv")
            if stage == "p2q" and h == 0:
                store_x()
                return nc
            if "dbg_q" in dbg_out and h == 0:
                X.wait(sync, "dv")
                X.dma(gps, dbg_out["dbg_q"][:, :], qkraw[:, 0:1024], "stg_")
            if "dbg_k" in dbg_out and h == 0:
                X.wait(sync, "dv")
                X.dma(gps, dbg_out["dbg_k"][:, :], qkraw[:, 1024:2560], "stg_")
            for g in range(2):
                pvps, zps, bcps = PB[2], PB[3], PB[4]
                rorder = [3, 4, 0, 1, 2, 5, 6, 7]
                for ri, r in enumerate(rorder):
                    jj = 4 * g + r
                    s0, s1 = max(0, r - 4), min(3, r) + 1
                    lo, hi = s0 * 128, s1 * 128
                    st_ = PB[ri % 2]
                    has_far, has_diag = r <= 3, r >= 4
                    ten.matmul(st_[:, lo:hi], qkraw[:, 1024 + jj * 128:1024 + (jj + 1) * 128],
                               qkraw[:, g * 512 + lo:g * 512 + hi],
                               start=True, stop=not (has_far or has_diag))
                    if has_far:
                        ten.matmul(st_[:, r * 128:(r + 1) * 128], IDC, MFAR,
                                   start=False, stop=not has_diag)
                    if has_diag:
                        ten.matmul(st_[:, (r - 4) * 128:(r - 3) * 128], IDC, MDIAG,
                                   start=False, stop=True)
                    X.tick(ten, "pe")
                    X.wait(sca, "pe")
                    sca.activation(out=Pt[:, ri % 2, lo:hi], in_=st_[:, lo:hi],
                                   func=AF.Exp, bias=kmk[:, jj:jj + 1], scale=SCALE)
                    X.tick(sca, "ac")
                    X.wait(ten, "ac")
                    last = ri == len(rorder) - 1
                    ten.matmul(zps[0:1, lo:hi], onesc[:, :], Pt[:, ri % 2, lo:hi],
                               start=(ri == 0), stop=last)
                    ten.matmul(pvps[:, lo:hi], vgrp[:, jj, hh * 128:(hh + 1) * 128],
                               Pt[:, ri % 2, lo:hi], start=(ri == 0), stop=last)
                X.tick(ten, "pe")
                X.wait(vec, "pe")
                vec.tensor_copy(zrow[0:1, :], zps[0:1, :])
                vec.reciprocal(zrow[0:1, :], zrow[0:1, :])
                X.tick(vec, "dv")
                X.wait(ten, "dv")
                ten.matmul(bcps[:, :], onesrf[:, :], zrow[0:1, :], start=True, stop=True)
                X.tick(ten, "pe")
                X.wait(vec, "pe")
                vec.tensor_copy(bcz[:, :], bcps[:, :])
                tt(aout[:, h, g * 512:(g + 1) * 512], pvps[:, :], bcz[:, :], ALU.mult)
                X.tick(vec, "dv")
                X.wait(ten, "dv")

    if stage == "p2a":
        store_x()
        return nc
    if "dbg_aout" in dbg_out:
        for dch in range(DC):
            vec.tensor_copy(stg[:, 0:1024], aout[:, dch, :])
            X.tick(vec, "dv")
            X.wait(sync, "dv")
            X.dma(sync, dbg_out["dbg_aout"][dch * 128:(dch + 1) * 128, :],
                  stg[:, 0:1024], "st")
        X.wait(vec, "st")

    wo_sb = wslA[:, 0:8192].rearrange("p (a b) -> p a b", a=DC)
    for ocg in range(4):
        slab(wo_sb[:, :, :], woT[ocg, :, :, :])
        X.wait(ten, "lw")
        X.wait(ten, "dv")
        for tt_ in range(NT):
            ps = PB[tt_ % 2]
            X.wait(ten, "dv")
            for dch in range(DC):
                ten.matmul(ps[:, :], aout[:, dch, tt_ * 128:(tt_ + 1) * 128],
                           wo_sb[:, dch, :], start=(dch == 0), stop=(dch == DC - 1))
            X.tick(ten, "pe")
            X.wait(vec, "pe")
            xs = x_res[:, tt_, ocg * 512:(ocg + 1) * 512]
            tt(xs, xs, ps[:, :], ALU.add)
            X.tick(vec, "dv")
        X.wait(ten, "dv")

    if stage == "attn":
        store_x()
        return nc

    for tt_ in range(NT):
        X.wait(vec, "dv")
        X.wait(vec, "sx")
        X.wait(vec, "tr2")
        X.wait(sca, "sx")
        X.wait(sca, "tr2")
        xnst = stgb[:, tt_ % 2, :]
        ln_tile(x_res[:, tt_, :], xnst, stgb[:, (tt_ + 1) % 2, :])
        X.wait(sync, "dv")
        X.dma(sync, xn2_hbm[tt_ * 128:(tt_ + 1) * 128, :], xnst, "sx")
        for dch in range(DC):
            X.dmat(fmA[:, dch, tt_ * 128:(tt_ + 1) * 128],
                   xnst[:, dch * 128:(dch + 1) * 128], "tr2")


    gt_sb = wslB[:, 4096:4352].rearrange("p (a b) -> p a b", a=DC)
    X.wait(gps, "pe")
    X.dma(gps, gt_sb[:, :, :], gateT[:, :, :], "lw")
    vec.memset(stg[:, 0:128], 0.0)
    vec.memset(crow[0:1, 0, :], 0.0)
    X.tick(vec, "dv")
    X.wait(sync, "dv")
    X.dma(sync, lists_pay[0:1, 0:16384].rearrange("a (p f) -> (a p) f", p=128),
          stg[:, 0:128], "sl")
    X.wait(ten, "lw")
    X.wait(ten, "tr2")
    X.wait(ten, "dv")
    X.wait(gps, "lc")
    X.wait(vec, "lc")
    for tt_ in range(NT):
        X.wait(vec, "slg")
        X.wait(ten, "dv")
        lps = PB[2][:, 0:16]
        for dch in range(DC):
            ten.matmul(lps, fmA[:, dch, tt_ * 128:(tt_ + 1) * 128],
                       gt_sb[:, dch, :], start=(dch == 0), stop=(dch == DC - 1))
        X.tick(ten, "pe")
        X.wait(vec, "pe")
        lg = rt[:, 0, :]
        vec.tensor_copy(lg, lps)
        m1, idx1, m2, idx2, w1c, w2c, p1c, p2c = (tokr[:, tt_, i:i + 1] for i in range(8))
        sc1, sc2 = stats[:, 8:9], stats[:, 9:10]
        vec.reduce_max(m1, lg, axis=AXX)
        E1, E2, scr = rt[:, 1, :], rt[:, 2, :], rt[:, 3, :]
        ts(scr, lg, m1, None, op0=ALU.is_ge)
        tt(scr, scr, iotmB[:, :], ALU.mult)
        ts(scr, scr, 4e4, op0=ALU.add)
        vec.tensor_reduce(idx1, scr, axis=AXX, op=ALU.min)
        ts(E1, iot16[:, :], idx1, None, op0=ALU.is_equal)
        ts(scr, E1, 1e6)
        tt(scr, lg, scr, ALU.subtract)
        vec.reduce_max(m2, scr, axis=AXX)
        ts(rt[:, 4, :], scr, m2, None, op0=ALU.is_ge)
        tt(rt[:, 4, :], rt[:, 4, :], iotmB[:, :], ALU.mult)
        ts(rt[:, 4, :], rt[:, 4, :], 4e4, op0=ALU.add)
        vec.tensor_reduce(idx2, rt[:, 4, :], axis=AXX, op=ALU.min)
        ts(E2, iot16[:, :], idx2, None, op0=ALU.is_equal)
        ts(sc1, m1, -1.0)
        X.tick(vec, "dv")
        X.wait(sca, "dv")
        sca.activation(out=rt[:, 4, :], in_=lg, func=AF.Exp, bias=sc1, scale=1.0,
                       accum_out=sc2)
        X.tick(sca, "ac")
        X.wait(vec, "ac")
        vec.reciprocal(w1c, sc2)
        tt(sc1, m2, m1, ALU.subtract)
        X.tick(vec, "dv")
        X.wait(sca, "dv")
        sca.activation(out=sc2, in_=sc1, func=AF.Exp)
        X.tick(sca, "ac")
        X.wait(vec, "ac")
        tt(w2c, sc2, w1c, ALU.mult)
        tt(rts[:, :], E1, E2, ALU.add)
        X.tick(vec, "dv")
        X.wait(ten, "dv")
        cps = PB[3][:, 0:16]
        ten.matmul(cps, TRIU, rts[:, :], start=True, stop=False)
        ten.matmul(cps, onesrf[:, :], crow[0:1, 0, :], start=False, stop=True)
        ten.matmul(PB[4][0:1, 0:16], onesc[:, :], rts[:, :], start=True, stop=True)
        X.tick(ten, "pe")
        X.wait(vec, "pe")
        Cc = rt[:, 5, :]
        vec.tensor_copy(Cc, cps)
        vec.tensor_copy(crow[0:1, 1, :], PB[4][0:1, 0:16])
        vec.tensor_tensor(crow[0:1, 0, :], crow[0:1, 0, :], crow[0:1, 1, :],
                          op=ALU.add)
        tt(rt[:, 6, :], Cc, E1, ALU.mult)
        vec.reduce_sum(p1c, rt[:, 6, :], axis=AXX)
        tt(rt[:, 6, :], Cc, E2, ALU.mult)
        vec.reduce_sum(p2c, rt[:, 6, :], axis=AXX)
        gidc = stats[:, 10:11]
        ts(gidc, ccols[:, 0:1], 1.0, float(128 * tt_), op0=ALU.mult, op1=ALU.add)
        tt(gidc, gidc, iotcol[:, :], ALU.add)
        for kk, (idxc, posc) in enumerate(((idx1, p1c), (idx2, p2c))):
            sp = stats[:, 11 + kk:12 + kk]
            ts(sp, idxc, 1024.0)
            tt(sp, sp, posc, ALU.add)
            ovg = stats[:, 13 + kk:14 + kk]
            ts(ovg, posc, 1024.0, None, op0=ALU.is_ge)
            ts(ovg, ovg, GUARD)
            tt(sp, sp, ovg, ALU.add)
            vec.tensor_copy(icol[:, kk:kk + 1], sp)
        X.tick(vec, "dv")
        X.wait(gps, "dv")
        X.wait(gps, "sl")
        for kk in range(2):
            X.inc(gps.indirect_dma_start(
                out=lists_pay[:, :], out_offset=IOA(ap=icol[:, kk:kk + 1], axis=1),
                in_=gidc, in_offset=None,
                bounds_check=16383, oob_is_err=False), "slg", 16)
        if "dbg_route" in dbg_out:
            vec.tensor_copy(stg[:, 0:8], tokr[:, tt_, :])
            X.tick(vec, "dv")
            X.wait(sync, "dv")
            X.dma(sync, dbg_out["dbg_route"][tt_ * 128:(tt_ + 1) * 128, 0:8],
                  stg[:, 0:8], "st")
    X.wait(vec, "pe")
    X.wait(sync, "dv")
    X.dma(sync, lists_pay[0:1, 16384:16400], crow[0:1, 0, :], "sl")
    if "dbg_lists" in dbg_out:
        X.wait(sync, "sl")
        X.wait(vec, "sl")
        vec.memset(stats[:, 15:16], 0.0)
        X.dma(sync, dbg_out["dbg_lists"][:, :], lists_pay[:, :], "st")

    X.wait(gps, "sl")
    X.wait(gps, "slg")
    X.wait(gps, "sx")
    X.inc(gps.collective_compute(
        "AllGather", ALU.bypass, ins=[lists_pay[:, :]], outs=[lists_all[:, :]],
        replica_groups=[list(range(8))]), "cc", 1)
    X.inc(gps.collective_compute(
        "AllGather", ALU.bypass, ins=[xn2_hbm[:, :]], outs=[xnall[:, :]],
        replica_groups=[list(range(8))]), "cc", 1)

    wsgv = wslA[:, 0:2048].rearrange("p (a b) -> p a b", a=DC)
    wsuv = wslA[:, 2048:4096].rearrange("p (a b) -> p a b", a=DC)
    for is_ in range(ISC):
        X.wait(gps, "pe")
        X.dma(gps, wsgv[:, :, :], wsg[is_, :, :, :], "lw")
        X.dma(gps, wsuv[:, :, :], wsu[is_, :, :, :], "lw")
        X.wait(ten, "lw")
        X.wait(ten, "ac")
        X.wait(ten, "dv")
        for tg in range(2):
            for dch in range(DC):
                ten.matmul(PB[0 + tg][:, :], wsgv[:, dch, :],
                           fmA[:, dch, tg * 512:(tg + 1) * 512],
                           start=(dch == 0), stop=(dch == DC - 1))
                ten.matmul(PB[2 + tg][:, :], wsuv[:, dch, :],
                           fmA[:, dch, tg * 512:(tg + 1) * 512],
                           start=(dch == 0), stop=(dch == DC - 1))
        X.tick(ten, "pe")
        X.wait(sca, "pe")
        X.wait(vec, "pe")
        if "dbg_g0" in dbg_out and is_ == 0:
            vec.tensor_copy(stg[:, 0:512], PB[0][:, :])
            vec.tensor_copy(stg[:, 512:1024], PB[1][:, :])
            X.wait(sync, "dv")
            X.dma(sync, dbg_out["dbg_g0"][:, :], stg[:, 0:1024], "st")
            X.wait(vec, "st")
        if "dbg_slot" in dbg_out and is_ == 0:
            vec.tensor_copy(stg[:, 0:16], wsgv[:, 0, 0:16])
            vec.tensor_copy(stg[:, 16:32], wsgv[:, 1, 0:16])
            X.wait(sync, "dv")
            X.dma(sync, dbg_out["dbg_slot"][:, :], stg[:, 0:32], "st")
            X.wait(vec, "st")
        for tg in range(2):
            sca.activation(out=stgb[:, 0, tg * 512:(tg + 1) * 512], in_=PB[tg][:, :],
                           func=AF.Silu)
            vec.tensor_copy(stgb[:, 1, tg * 512:(tg + 1) * 512], PB[2 + tg][:, :])
        X.tick(sca, "ac")
        X.wait(vec, "ac")
        tt(fmB[:, is_, :], stgb[:, 0, 0:1024], stgb[:, 1, 0:1024], ALU.mult)
        X.tick(vec, "dv")
        if stage.startswith("p7a") and is_ == int(stage[3:] or 0):
            vec.tensor_copy(stg[:, 0:1024], fmB[:, 0, :])
            vec.tensor_copy(stg[:, 1024:2048], fmB[:, min(is_, 1), :])
            X.wait(sync, "dv")
            for tt_ in range(NT):
                X.dma(sync, y_out[tt_ * 128:(tt_ + 1) * 128, :],
                      stg[:, :] if tt_ == 0 else x_res[:, tt_, :], "st")
            X.wait(sync, "st")
            return nc
    if "dbg_xn2T" in dbg_out:
        for dch in range(DC):
            vec.tensor_copy(stg[:, 0:1024], fmA[:, dch, 0:1024])
            X.wait(sync, "dv")
            X.dma(sync, dbg_out["dbg_xn2T"][dch * 128:(dch + 1) * 128, :],
                  stg[:, 0:1024], "st")
            X.wait(vec, "st")
    if "dbg_gu" in dbg_out:
        vec.tensor_copy(stg[:, 0:1024], fmB[:, 0, :])
        X.wait(sync, "dv")
        X.dma(sync, dbg_out["dbg_gu"][:, :], stg[:, 0:1024], "st")
        X.wait(vec, "st")
    wdvA = wslA[:, 0:16 * 512].rearrange("p (a b) -> p a b", a=16)
    wdvB = wslB[:, 0:6 * 512].rearrange("p (a b) -> p a b", a=6)
    for ocg in range(4):
        X.wait(gps, "pe")
        X.dma(gps, wdvA[:, :, :], wsdA[ocg, :, :, :], "lw")
        X.dma(gps, wdvB[:, :, :], wsdB[ocg, :, :, :], "lw")
        X.wait(ten, "lw")
        X.wait(ten, "dv")
        for tt_ in range(NT):
            ps = PB[4 + tt_ % 2]
            X.wait(ten, "dv")
            for is_ in range(ISC):
                wtile = wdvA[:, is_, :] if is_ < 16 else wdvB[:, is_ - 16, :]
                ten.matmul(ps[:, :], fmB[:, is_, tt_ * 128:(tt_ + 1) * 128],
                           wtile, start=(is_ == 0), stop=(is_ == ISC - 1))
            X.tick(ten, "pe")
            X.wait(vec, "pe")
            xs = x_res[:, tt_, ocg * 512:(ocg + 1) * 512]
            tt(xs, xs, ps[:, :], ALU.add)
            X.tick(vec, "dv")
        X.wait(ten, "dv")

    if stage == "shared":
        store_x()
        return nc

    X.wait(sync, "cc")
    X.wait(vec, "cc")
    X.wait(gps, "cc")
    X.dma(sync, cnt8[:, :], lists_all[:, 16384:16400], "lg")
    X.wait(ten, "lg")
    ten.transpose(PB[0][0:16, 0:8], cnt8[:, :], eye8[:, :])
    ten.matmul(PB[1][0:8, 0:16], triu8[:, :], cnt8[:, :], start=True, stop=True)
    X.tick(ten, "pe")
    X.wait(vec, "pe")
    vec.tensor_copy(cnt8T[:, :], PB[0][0:16, 0:8])
    vec.tensor_copy(cume[:, :], PB[1][0:8, 0:16])
    X.tick(vec, "dv")
    X.wait(ten, "dv")
    ten.matmul(PB[2][0:1, 0:16], cidoh[:, :], cume[:, :], start=True, stop=True)
    X.tick(ten, "pe")
    X.wait(vec, "pe")
    vec.tensor_copy(row16[:, :], PB[2][0:1, 0:16])
    X.tick(vec, "dv")
    X.wait(ten, "dv")
    ten.matmul(PB[2][:, 0:16], onesrf[:, :], row16[:, :], start=True, stop=True)
    X.tick(ten, "pe")
    X.wait(vec, "pe")
    vec.tensor_copy(rt[:, 7, :], PB[2][:, 0:16])
    gps.iota(slotm[:, 8:16], pattern=[[128, 8]], base=0, channel_multiplier=1,
             allow_small_or_imprecise_dtypes=True)
    X.tick(gps, "gp")
    X.tick(vec, "dv")

    for o in range(2):
        X.wait(ten, "dv")
        X.wait(ten, "gp")
        ten.matmul(PB[3][0:8, 0:1], cnt8T[:, :], eoh[:, o:o + 1], start=True, stop=True)
        X.tick(ten, "pe")
        X.wait(vec, "pe")
        vec.tensor_copy(col8[:, :], PB[3][0:8, 0:1])
        X.tick(vec, "dv")
        X.wait(ten, "dv")
        ten.matmul(PB[3][0:1, 0:8], col8[:, :], triu8[:, :], start=True, stop=True)
        X.tick(ten, "pe")
        X.wait(vec, "pe")
        vec.tensor_copy(row8[:, :], PB[3][0:1, 0:8])
        X.tick(vec, "dv")
        X.wait(ten, "dv")
        ten.matmul(PB[3][:, 0:8], onesrf[:, :], row8[:, :], start=True, stop=True)
        X.tick(ten, "pe")
        X.wait(vec, "pe")
        bnd = slotm[:, 0:8]
        vec.tensor_copy(bnd, PB[3][:, 0:8])
        sio = slotm[:, 8:16]
        cidx = slotm[:, 16:24]
        base = slotm[:, 24:32]
        vec.memset(cidx, 0.0)
        vec.memset(base, 0.0)
        for c in range(1, 8):
            ge = slotm[:, 32:40]
            ts(ge, sio, bnd[:, c:c + 1], None, op0=ALU.is_ge)
            tt(cidx, cidx, ge, ALU.add)
            ts(ge, ge, bnd[:, c:c + 1], None, op0=ALU.mult)
            tt(base, base, ge, ALU.max)
        fl = slotm[:, 32:40]
        ts(fl, cidx, float(LISTW))
        tt(fl, fl, sio, ALU.add)
        tt(fl, fl, base, ALU.subtract)
        ts(fl, fl, ccols[:, 1 + o:2 + o], None, op0=ALU.add)
        vec.tensor_copy(icol[:, 0:8], fl)
        X.tick(vec, "dv")
        if "dbg_slot" in dbg_out and o == 0:
            vec.tensor_copy(stg[:, 0:32], slotm[:, 0:32])
            X.tick(vec, "dv")
            X.wait(sync, "dv")
            X.dma(sync, dbg_out["dbg_slot"][:, 0:32], stg[:, 0:32], "st")
        X.wait(gps, "dv")
        for col in range(8):
            X.inc(gps.indirect_dma_start(
                out=sids[:, col:col + 1], out_offset=None,
                in_=lists_all.rearrange("a b -> (a b)")[None, :],
                in_offset=IOA(ap=icol[:, col:col + 1], axis=1)), "lgg", 16)
        X.wait(vec, "lgg")
        vec.tensor_copy(icol[:, 0:8], sids[:, :])
        X.tick(vec, "dv")
        X.wait(gps, "dv")
        for col in range(8):
            X.inc(gps.indirect_dma_start(
                out=gstg[:, col % 2, :], out_offset=None,
                in_=xnall[:, :],
                in_offset=IOA(ap=icol[:, col:col + 1], axis=0)), "lgg", 16)
            X.wait(sync, "lgg")
            for dch in range(DC):
                X.dmat(fmB[:, dch, col * 128:(col + 1) * 128],
                       gstg[:, col % 2, dch * 128:(dch + 1) * 128], "trb")
            if col % 2 == 1:
                X.wait(gps, "trb")
        X.wait(ten, "trb")
        wg_sb = wslA[:, 0:8192].rearrange("p (a b) -> p a b", a=DC)
        wu_sb = wslB[:, 0:8192].rearrange("p (a b) -> p a b", a=DC)
        for igrp in range(3):
            X.wait(gps, "pe")
            X.dma(gps, wg_sb[:, :, :], wg[o, igrp, :, :, :], "lw")
            X.dma(gps, wu_sb[:, :, :], wu[o, igrp, :, :, :], "lw")
            X.wait(ten, "lw")
            for ic in range(4):
                X.wait(ten, "ac")
                X.wait(ten, "dv")
                for tg in range(2):
                    for dch in range(DC):
                        ten.matmul(PB[tg][:, :], wg_sb[:, dch, ic * 128:(ic + 1) * 128],
                                   fmB[:, dch, tg * 512:(tg + 1) * 512],
                                   start=(dch == 0), stop=(dch == DC - 1))
                        ten.matmul(PB[2 + tg][:, :], wu_sb[:, dch, ic * 128:(ic + 1) * 128],
                                   fmB[:, dch, tg * 512:(tg + 1) * 512],
                                   start=(dch == 0), stop=(dch == DC - 1))
                X.tick(ten, "pe")
                X.wait(sca, "pe")
                X.wait(vec, "pe")
                for tg in range(2):
                    sca.activation(out=stgb[:, 0, tg * 512:(tg + 1) * 512],
                                   in_=PB[tg][:, :], func=AF.Silu)
                    vec.tensor_copy(stgb[:, 1, tg * 512:(tg + 1) * 512], PB[2 + tg][:, :])
                X.tick(sca, "ac")
                X.wait(vec, "ac")
                tt(fmA[:, igrp * 4 + ic, 0:1024], stgb[:, 0, 0:1024], stgb[:, 1, 0:1024],
                   ALU.mult)
                X.tick(vec, "dv")
        wdx = wslA[:, 0:12 * 512].rearrange("p (a b) -> p a b", a=12)
        for ocg in range(4):
            X.wait(gps, "pe")
            X.dma(gps, wdx[:, :, :], wd[o, ocg, :, :, :], "lw")
            X.wait(ten, "lw")
            X.wait(ten, "dv")
            for rs in range(8):
                ps = PB[4 + rs % 2]
                X.wait(ten, "dv")
                for is_ in range(12):
                    ten.matmul(ps[:, :], fmA[:, is_, rs * 128:(rs + 1) * 128],
                               wdx[:, is_, :], start=(is_ == 0), stop=(is_ == 11))
                X.tick(ten, "pe")
                X.wait(vec, "pe")
                vec.tensor_copy(stgb[:, rs % 2, 0:512], ps[:, :])
                X.tick(vec, "dv")
                X.wait(sync, "dv")
                X.dma(sync, eo_hbm[o * 1024 + rs * 128:o * 1024 + (rs + 1) * 128,
                                   ocg * 512:(ocg + 1) * 512], stgb[:, rs % 2, 0:512], "se")
            X.wait(vec, "se")
    if "dbg_eo" in dbg_out:
        X.wait(sync, "se")
        for rr in range(16):
            X.dma(sync, gstg[:, 0, :], eo_hbm[rr * 128:(rr + 1) * 128, :], "lg")
            X.wait(vec, "lg")
            vec.tensor_copy(stg[:, :], gstg[:, 0, :])
            X.tick(vec, "dv")
            X.wait(sync, "dv")
            X.dma(sync, dbg_out["dbg_eo"][rr * 128:(rr + 1) * 128, :], stg[:, :], "st")
        X.wait(vec, "st")
    if "dbg_cnt" in dbg_out:
        X.wait(sync, "dv")
        vec.tensor_copy(stg[0:8, 0:16], cnt8[:, :])
        X.tick(vec, "dv")
        X.wait(sync, "dv")
        X.dma(sync, dbg_out["dbg_cnt"][:, :], stg[0:8, 0:16], "st")
        X.wait(vec, "st")

    X.wait(gps, "se")
    X.inc(gps.collective_compute(
        "AllGather", ALU.bypass, ins=[eo_hbm[:, :]], outs=[eoall[:, :]],
        replica_groups=[list(range(8))]), "cc", 1)
    X.wait(gps, "cc")
    X.wait(vec, "cc")
    for tt_ in range(NT):
        for kk in range(2):
            idxc = tokr[:, tt_, 1 + 2 * kk:2 + 2 * kk]
            posc = tokr[:, tt_, 6 + kk:7 + kk]
            ek = rt[:, 0, :]
            ts(ek, iot16[:, :], idxc, None, op0=ALU.is_equal)
            tt(ek, ek, rt[:, 7, :], ALU.mult)
            offc = stats[:, 8:9]
            vec.reduce_sum(offc, ek, axis=AXX)
            gp_ = stats[:, 9:10]
            tt(gp_, posc, offc, ALU.add)
            vec.tensor_scalar_min(gp_, gp_, 1023.0)
            ts(stats[:, 10:11], idxc, 1024.0)
            tt(gp_, gp_, stats[:, 10:11], ALU.add)
            vec.tensor_copy(icol[:, kk:kk + 1], gp_)
            X.tick(vec, "dv")
            X.wait(gps, "dv")
            X.inc(gps.indirect_dma_start(
                out=gstg[:, kk, :], out_offset=None,
                in_=eoall[:, :],
                in_offset=IOA(ap=icol[:, kk:kk + 1], axis=0)), "lgg", 16)
        X.wait(vec, "lgg")
        for kk in range(2):
            wc = tokr[:, tt_, 4 + kk:5 + kk]
            vec.tensor_copy(stg[:, :], gstg[:, kk, :])
            vec.scalar_tensor_tensor(x_res[:, tt_, :], stg[:, :], wc,
                                     x_res[:, tt_, :], op0=ALU.mult, op1=ALU.add)
        X.tick(vec, "dv")
    store_x()
    return nc


def rope_tabs_for_core(c):
    s0 = (c % 2) * 1024
    pos = np.maximum(np.arange(TE) + s0 - 512, 0).astype(np.float32)
    inv = (1.0 / (10000.0 ** (np.arange(0, DH, 2, dtype=np.float32) / np.float32(DH)))).astype(np.float32)
    fr = (pos[None, :] * inv[:, None]).astype(np.float32)
    cos, sin = np.cos(fr).astype(np.float32), np.sin(fr).astype(np.float32)
    cosI = np.empty((128, TE), np.float32)
    sinIS = np.empty((128, TE), np.float32)
    cosI[0::2] = cos
    cosI[1::2] = cos
    sinIS[0::2] = -sin
    sinIS[1::2] = sin
    return cosI, sinIS


def prep_inputs(L0_attn_x, L0_attn_Wq, L0_attn_Wk, L0_attn_Wv, L0_attn_Wo,
                L0_ffn_gate_w, L0_ffn_w_gate, L0_ffn_w_up, L0_ffn_w_down,
                L0_ffn_ws_gate, L0_ffn_ws_up, L0_ffn_ws_down):
    f = np.ascontiguousarray

    def pmaj(w, width):
        K = w.shape[0]
        return w.reshape(K // 128, 128, width).transpose(1, 0, 2)

    WqT = np.stack([pmaj(L0_attn_Wq.T[:, h * DH:(h + 1) * DH], DH) for h in range(NH)])
    WkT = np.stack([pmaj(L0_attn_Wk.T[:, h * DH:(h + 1) * DH], DH) for h in range(NH)])
    WvT = np.stack([pmaj(L0_attn_Wv.T[:, g * 512:(g + 1) * 512], 512) for g in range(4)])
    WoT = np.stack([pmaj(L0_attn_Wo.T[:, g * 512:(g + 1) * 512], 512) for g in range(4)])
    gT = pmaj(f(L0_ffn_gate_w.T), E)
    wsg = np.stack([pmaj(L0_ffn_ws_gate[:, i * 128:(i + 1) * 128], 128) for i in range(ISC)])
    wsu = np.stack([pmaj(L0_ffn_ws_up[:, i * 128:(i + 1) * 128], 128) for i in range(ISC)])
    wsdA = np.stack([pmaj(L0_ffn_ws_down[0:2048, g * 512:(g + 1) * 512], 512) for g in range(4)])
    wsdB = np.stack([pmaj(L0_ffn_ws_down[2048:ISH, g * 512:(g + 1) * 512], 512) for g in range(4)])
    wgp = np.zeros((E, D, IPAD), np.float32)
    wgp[:, :, :1408] = L0_ffn_w_gate
    wup = np.zeros((E, D, IPAD), np.float32)
    wup[:, :, :1408] = L0_ffn_w_up
    wdp = np.zeros((E, IPAD, D), np.float32)
    wdp[:, :1408, :] = L0_ffn_w_down
    wg_s = np.stack([np.stack([pmaj(wgp[e][:, g * 512:(g + 1) * 512], 512) for g in range(3)])
                     for e in range(E)])
    wu_s = np.stack([np.stack([pmaj(wup[e][:, g * 512:(g + 1) * 512], 512) for g in range(3)])
                     for e in range(E)])
    wd_s = np.stack([np.stack([pmaj(wdp[e][:, g * 512:(g + 1) * 512], 512) for g in range(4)])
                     for e in range(E)])

    in_maps = []
    for c in range(8):
        b, hf = c // 2, c % 2
        own = L0_attn_x[b, hf * 1024:(hf + 1) * 1024, :]
        if hf == 0:
            halo = np.zeros((HAL, D), np.float32)
        else:
            halo = L0_attn_x[b, 512:1024, :]
        xh = np.concatenate([halo, own], 0).astype(np.float32)
        cosI, sinIS = rope_tabs_for_core(c)
        km = np.zeros((128, NTE), np.float32)
        if hf == 0:
            km[:, 0:4] = BIGNEG
        cidv = np.zeros((1, 8), np.float32)
        cidv[0, 0], cidv[0, 1], cidv[0, 2] = c, 2 * c, 2 * c + 1
        cidohc = np.zeros((8, 1), np.float32)
        cidohc[c, 0] = 1
        eoh = np.zeros((E, 2), np.float32)
        eoh[2 * c, 0] = 1
        eoh[2 * c + 1, 1] = 1
        in_maps.append(dict(
            xh=xh, wqT=f(WqT), wkT=f(WkT), wvT=f(WvT), woT=f(WoT), gateT=f(gT),
            wsg=f(wsg), wsu=f(wsu), wsdA=f(wsdA), wsdB=f(wsdB),
            wg=f(wg_s[2 * c:2 * c + 2]), wu=f(wu_s[2 * c:2 * c + 2]),
            wd=f(wd_s[2 * c:2 * c + 2]),
            cosI=cosI, sinIS=sinIS, kmask=km, cidv=cidv, cidohc=cidohc, eoh=eoh))
    return in_maps


_NC_CACHE = {}


def kernel(**inputs):
    from concourse.bass_utils import run_bass_kernel_spmd
    in_maps = prep_inputs(**inputs)
    if "nc" not in _NC_CACHE:
        _NC_CACHE["nc"] = build_nc(stage="full")
    nc = _NC_CACHE["nc"]
    res = run_bass_kernel_spmd(nc, in_maps, core_ids=list(range(8)))
    x = inputs["L0_attn_x"]
    out = np.zeros((4, 2048, 2048), np.float32)
    for c in range(8):
        b, hf = c // 2, c % 2
        out[b, hf * 1024:(hf + 1) * 1024, :] = res.results[c]["y"]
    return out


# version 02
# speedup vs baseline: 1.1382x; 1.1382x over previous
import numpy as np
import ml_dtypes
import concourse.bass as bass
import concourse.mybir as mybir
from concourse.bass_core_rust import IndirectOffsetOnAxis as IOA

F32 = mybir.dt.float32
BF16 = mybir.dt.bfloat16
I32 = mybir.dt.int32
AF = mybir.ActivationFunctionType
ALU = mybir.AluOpType
AXX = mybir.AxisListType.X
BF = ml_dtypes.bfloat16

D, DC, NH, DH = 2048, 16, 16, 128
T, NT, HAL, TE, NTE = 1024, 8, 512, 1536, 12
SCALE = float(np.float32(DH) ** -0.5)
BIGNEG = -1e9
E, CAP = 16, 1024
ISH, ISC = 2816, 22
IPAD = 1536
GUARD = float(2 ** 20)
LISTW = 16400


class EngProxy:

    CHAIN = {"tensor_scalar", "tensor_tensor", "tensor_copy", "tensor_reduce",
             "reduce_max", "reduce_sum", "reciprocal", "memset", "activation",
             "iota", "scalar_tensor_tensor", "tensor_scalar_min",
             "tensor_scalar_add", "tensor_scalar_mul", "tensor_scalar_max",
             "tensor_single_scalar", "affine_select", "bn_stats", "bn_aggr",
             "tensor_add", "tensor_sub", "tensor_mul", "tensor_max",
             "partition_broadcast", "memzero"}

    def __init__(self, eng, X=None, chname=None):
        self._eng = eng
        self.last = None
        self._X = X
        self._chname = chname
        if X is not None and chname is not None:
            X.sem(chname)

    def __getattr__(self, name):
        f = getattr(self._eng, name)
        if not callable(f):
            return f
        chain = (name in EngProxy.CHAIN) and self._chname is not None

        def g(*a, **k):
            if chain and self._X.c[self._chname] > 0:
                self._eng.wait_ge(self._X.sem(self._chname), self._X.c[self._chname])
            r = f(*a, **k)
            if hasattr(r, "then_inc"):
                self.last = r
                if chain:
                    self._X.inc(r, self._chname, 1)
            return r
        return g


class Ctx:
    def __init__(self, nc):
        self.nc = nc
        self.c = {}
        self.h = {}

    def sem(self, name):
        if name not in self.h:
            self.h[name] = self.nc.alloc_semaphore(name)
            self.c[name] = 0
        return self.h[name]

    def inc(self, inst, name, n=1):
        inst.then_inc(self.sem(name), n)
        self.c[name] += n

    def tick(self, eng, name):
        if getattr(eng, "_chname", None) == name:
            return
        self.inc(eng.last, name, 1)

    def dma(self, eng, out, in_, name):
        self.inc(eng.dma_start(out=out, in_=in_), name, 16)

    def dmat(self, out, in_, name):
        self.inc(self.nc.sync.dma_start(out=out, in_=in_, transpose=True), name, 16)

    def wait(self, eng, name):
        eng.wait_ge(self.sem(name), self.c[name])

    def wait_val(self, eng, name, val):
        if val > 0:
            eng.wait_ge(self.sem(name), val)


def build_nc(stage="full", dbg=()):
    nc = bass.Bass("TRN2", target_bir_lowering=False, num_devices=8,
                   detect_race_conditions=False)
    X = Ctx(nc)
    sync = EngProxy(nc.sync)
    gps = EngProxy(nc.gpsimd, X, "gp")
    vec = EngProxy(nc.vector, X, "dv")
    sca = EngProxy(nc.scalar, X, "ac")
    ten = EngProxy(nc.tensor)

    def di(name, shape, dtype=F32):
        return nc.dram_tensor(name, list(shape), dtype, kind="ExternalInput").ap()

    xh = di("xh", [TE, D])
    wqT = di("wqT", [NH, 128, DC, DH])
    wkT = di("wkT", [NH, 128, DC, DH])
    wvT = di("wvT", [4, 128, DC, 512])
    woT = di("woT", [4, 128, DC, 512])
    gateT = di("gateT", [128, DC, E])
    wsg = di("wsg", [ISC, 128, DC, 128])
    wsu = di("wsu", [ISC, 128, DC, 128])
    wsdA = di("wsdA", [4, 128, 16, 512])
    wsdB = di("wsdB", [4, 128, 6, 512])
    wg = di("wg", [2, 3, 128, DC, 512])
    wu = di("wu", [2, 3, 128, DC, 512])
    wd = di("wd", [2, 4, 128, 12, 512])
    cosI = di("cosI", [128, TE])
    sinIS = di("sinIS", [128, TE])
    kmaski = di("kmask", [128, NTE])
    cidv = di("cidv", [1, 8])
    cidohc = di("cidohc", [8, 1])
    eohi = di("eoh", [E, 2])

    y_out = nc.dram_tensor("y", [T, D], F32, kind="ExternalOutput").ap()
    dbg_out = {}
    dbg_shapes = {"dbg_xnT": [D, TE], "dbg_q": [DH, T], "dbg_k": [DH, TE],
                  "dbg_xn2T": [D, T], "dbg_gu": [128, T], "dbg_g0": [128, T],
                  "dbg_aout": [D, T], "dbg_route": [T, 16], "dbg_lists": [1, LISTW],
                  "dbg_slot": [128, 32], "dbg_eo": [2 * CAP, D], "dbg_cnt": [8, 16],
                  "dbg_v": [TE, 512], "dbg_gu_all": [22 * 128, 1024],
                  "dbg_wdv": [128, 22 * 512]}
    for nm in dbg:
        dbg_out[nm] = nc.dram_tensor(nm, dbg_shapes[nm], F32, kind="ExternalOutput").ap()

    lists_pay = nc.dram_tensor("lists_pay", [1, LISTW], F32, kind="Internal").ap()
    lists_all = nc.dram_tensor("lists_all", [8, LISTW], F32, kind="Internal").ap()
    xn2_hbm = nc.dram_tensor("xn2_hbm", [T, D], BF16, kind="Internal").ap()
    xnall = nc.dram_tensor("xnall", [8 * T, D], BF16, kind="Internal").ap()
    eo_hbm = nc.dram_tensor("eo_hbm", [2 * CAP, D], BF16, kind="Internal").ap()
    eoall = nc.dram_tensor("eoall", [16 * CAP, D], BF16, kind="Internal").ap()

    swp_np = np.zeros((128, 128), BF)
    for m in range(128):
        swp_np[m ^ 1, m] = 1
    tq_ = np.arange(128)[None, :]
    tk_ = np.arange(128)[:, None]
    cN = {
        "idc": np.eye(128, dtype=BF),
        "swp": swp_np,
        "mdiag": np.where(tk_ <= tq_, 0, BIGNEG).astype(BF),
        "mfar": np.where(tk_ > tq_, 0, BIGNEG).astype(BF),
        "triu": np.triu(np.ones((128, 128)), 1).astype(BF),
    }
    cN_ap = {k: nc.inline_tensor(v, k).ap() for k, v in cN.items()}
    onescol_b = nc.inline_tensor(np.ones((128, 1), BF), "onescol_b").ap()
    onesrow_f = nc.inline_tensor(np.ones((1, 128), np.float32), "onesrow_f").ap()
    iot16_c = nc.inline_tensor(
        np.broadcast_to(np.arange(16, dtype=np.float32), (128, 16)).copy(), "iot16").ap()
    iotmB_c = nc.inline_tensor(
        (np.broadcast_to(np.arange(16, dtype=np.float32), (128, 16)) - 4e4).copy(), "iotmB").ap()
    iotcol_c = nc.inline_tensor(np.arange(128, dtype=np.float32)[:, None].copy(), "iotcol").ap()
    eye8_f = nc.inline_tensor(np.eye(8, dtype=np.float32), "eye8").ap()
    eps_f = nc.inline_tensor(np.full((128, 1), 1e-5, np.float32), "epsc").ap()
    triu8_f = nc.inline_tensor(np.triu(np.ones((8, 8), np.float32), 1), "triu8").ap()

    A = nc.alloc_sbuf_tensor
    AT = nc.alloc_sbuf_tensor_at

    def addr(h):
        return nc.lookup_mloc(h).addr

    x_res = A("x_res", [128, NT, D], F32)
    fmA = A("fmA", [128, DC, TE], BF16)
    fmB = A("fmB", [128, 22, 1024], BF16)
    wslA = A("wslA", [128, 8192], BF16)
    wslB = A("wslB", [128, 8192], BF16)
    pool = A("pool", [128, 19008], mybir.dt.uint8)
    pb_ = addr(pool)
    fb_ = addr(fmB)
    vgrp = AT("vgrp", [128, NTE, 512], BF16, offset=fb_ + 16 * 2048)
    gstg = AT("gstg", [128, 2, D], BF16, offset=fb_ + 16 * 2048)
    tabs = AT("tabs", [128, 2, TE], BF16, offset=addr(wslB) + 8704)
    consts = AT("consts", [128, 5, 128], BF16, offset=addr(wslB) + 14848)
    stg = AT("stg", [128, D], F32, offset=pb_ + 0)
    qkraw = AT("qkraw", [128, 2560], BF16, offset=pb_ + 0)
    zrow = AT("zrow", [1, 512], F32, offset=pb_ + 5120)
    stgb = AT("stgb", [128, 2, D], BF16, offset=pb_ + 8192)
    ropeA = AT("ropeA", [128, 512], F32, offset=pb_ + 8192)
    ropeB = AT("ropeB", [128, 512], F32, offset=pb_ + 10240)
    Pt = AT("Pt", [128, 2, 512], BF16, offset=pb_ + 12288)
    bcz = AT("bcz", [128, 512], F32, offset=pb_ + 14336)
    sm = pb_ + 16384
    stats = AT("stats", [128, 16], F32, offset=sm + 0)
    kmk = AT("kmk", [128, NTE], F32, offset=sm + 64)
    rt = AT("rt", [128, 8, 16], F32, offset=sm + 128)
    tokr = AT("tokr", [128, NT, 8], F32, offset=sm + 640)
    slotm = AT("slotm", [128, 40], F32, offset=sm + 896)
    icol = AT("icol", [128, 8], I32, offset=sm + 1056)
    sids = AT("sids", [128, 8], F32, offset=sm + 1088)
    onesc = AT("onesc", [128, 1], BF16, offset=sm + 1120)
    onesrf = AT("onesrf", [1, 128], F32, offset=sm + 1152)
    iot16 = AT("iot16s", [128, 16], F32, offset=sm + 1664)
    iotmB = AT("iotmBs", [128, 16], F32, offset=sm + 1728)
    iotcol = AT("iotcols", [128, 1], F32, offset=sm + 1792)
    eye8 = AT("eye8s", [8, 8], F32, offset=sm + 1824)
    triu8 = AT("triu8s", [8, 8], F32, offset=sm + 1856)
    cid8 = AT("cid8s", [1, 8], F32, offset=sm + 1888)
    cidoh = AT("cidohs", [8, 1], F32, offset=sm + 1920)
    eoh = AT("eohs", [E, 2], F32, offset=sm + 1952)
    ccols = AT("ccolss", [128, 4], F32, offset=sm + 1984)
    crow = AT("crows", [1, 4, 16], F32, offset=sm + 2016)
    cnt8 = AT("cnt8s", [8, 16], F32, offset=sm + 2272)
    cnt8T = AT("cnt8Ts", [16, 8], F32, offset=sm + 2336)
    cume = AT("cumes", [8, 16], F32, offset=sm + 2368)
    col8 = AT("col8s", [8, 1], F32, offset=sm + 2432)
    row8 = AT("row8s", [1, 8], F32, offset=sm + 2464)
    row16 = AT("row16s", [1, 16], F32, offset=sm + 2496)
    rts = AT("rtss", [128, 16], BF16, offset=sm + 2560)
    epsc = AT("epscs", [128, 1], F32, offset=sm + 2592)

    PB = [nc.alloc_psum_tensor(f"pb{i}", [128, 512], F32) for i in range(6)]
    TBa = nc.alloc_psum_tensor("tba", [128, 512], BF16)
    TBb = nc.alloc_psum_tensor("tbb", [128, 512], BF16)

    IDC = consts[:, 0, :]
    SWP = consts[:, 1, :]
    MDIAG = consts[:, 2, :]
    MFAR = consts[:, 3, :]
    TRIU = consts[:, 4, :]

    def slab(dst, src_ap, name="lw", rel="peW"):
        X.wait(gps, rel)
        X.dma(gps, dst, src_ap, name)

    def ts(out, in0, s1, s2=None, op0=ALU.mult, op1=None, eng=None):
        e = eng or vec
        if s2 is None and op1 is None:
            e.tensor_scalar(out, in0, s1, None, op0=op0)
        else:
            e.tensor_scalar(out, in0, s1, s2, op0=op0, op1=op1)

    def tt(out, a, b, op=ALU.mult, eng=None):
        (eng or vec).tensor_tensor(out, a, b, op=op)

    def ln_tile(xt, out_bf, sq_scratch):
        mneg, ss, s0, rstd = (stats[:, i:i + 1] for i in range(4))
        vec.reduce_sum(mneg, xt, axis=AXX, negate=True)
        ts(mneg, mneg, 1.0 / D)
        X.wait(sca, "dv")
        sca.activation(out=sq_scratch, in_=xt, func=AF.Square, bias=mneg, scale=1.0,
                       accum_out=ss)
        sca.activation(out=s0, in_=ss, func=AF.Sqrt, bias=epsc[:, :], scale=1.0 / D)
        X.wait(vec, "ac")
        vec.reciprocal(rstd, s0)
        ts(out_bf, xt, mneg, rstd, op0=ALU.add, op1=ALU.mult)

    def store_x(tag="st"):
        X.wait(sync, "dv")
        for tt_ in range(NT):
            X.dma(sync, y_out[tt_ * 128:(tt_ + 1) * 128, :], x_res[:, tt_, :], tag)
        X.wait(sync, tag)

    X.dma(sync, consts[:, 0, :], cN_ap["idc"][:, :], "lc")
    X.dma(sync, consts[:, 1, :], cN_ap["swp"][:, :], "lc")
    X.dma(sync, consts[:, 2, :], cN_ap["mdiag"][:, :], "lc")
    X.dma(sync, consts[:, 3, :], cN_ap["mfar"][:, :], "lc")
    X.dma(sync, consts[:, 4, :], cN_ap["triu"][:, :], "lc")
    X.dma(sync, onesc[:, :], onescol_b[:, :], "lc")
    X.dma(sync, onesrf[:, :], onesrow_f[:, :], "lc")
    X.dma(sync, iot16[:, :], iot16_c[:, :], "lc")
    X.dma(sync, iotmB[:, :], iotmB_c[:, :], "lc")
    X.dma(sync, iotcol[:, :], iotcol_c[:, :], "lc")
    X.dma(sync, eye8[:, :], eye8_f[:, :], "lc")
    X.dma(sync, triu8[:, :], triu8_f[:, :], "lc")
    X.dma(sync, cid8[:, :], cidv[:, :], "lc")
    X.dma(sync, cidoh[:, :], cidohc[:, :], "lc")
    X.dma(sync, eoh[:, :], eohi[:, :], "lc")
    X.dma(sync, kmk[:, :], kmaski[:, :], "lc")
    X.dma(sync, epsc[:, :], eps_f[:, :], "lc")
    X.dma(gps, tabs[:, 0, :], cosI[:, :], "lcg")
    X.dma(gps, tabs[:, 1, :], sinIS[:, :], "lcg")
    X.wait(ten, "lc")
    ten.matmul(PB[0][:, 0:3], onesrf[:, :], cid8[0:1, 0:3], start=True, stop=True)
    X.tick(ten, "pe")
    X.wait(vec, "pe")
    ts(ccols[:, 0:3], PB[0][:, 0:3], 1024.0)
    X.tick(vec, "dv")

    for te in range(NTE):
        own = te >= 4
        xt = x_res[:, te - 4, :] if own else stg[:, :]
        X.dma(sync, xt, xh[te * 128:(te + 1) * 128, :], "lx")
        X.wait(vec, "lx")
        X.wait(sca, "lx")
        X.wait(vec, "tr")
        X.wait(sca, "tr")
        xnst = stgb[:, te % 2, :]
        ln_tile(xt, xnst, stgb[:, (te + 1) % 2, :])
        X.wait(sync, "dv")
        for dch in range(DC):
            X.dmat(fmA[:, dch, te * 128:(te + 1) * 128],
                   xnst[:, dch * 128:(dch + 1) * 128], "tr")
        X.wait(sync, "tr")

    if "dbg_xnT" in dbg_out:
        X.wait(sync, "tr")
        X.wait(sync, "dv")
        X.wait(vec, "tr")
        for dch in range(DC):
            vec.tensor_copy(stg[:, 0:TE], fmA[:, dch, :])
            X.tick(vec, "dv")
            X.wait(sync, "dv")
            X.dma(sync, dbg_out["dbg_xnT"][dch * 128:(dch + 1) * 128, :], stg[:, 0:TE], "st")
            X.wait(vec, "st")

    if stage == "p1":
        store_x()
        return nc

    aout = fmB
    X.wait(ten, "tr")
    X.wait(ten, "lc")
    X.wait(vec, "lcg")
    X.wait(vec, "lc")
    X.wait(sca, "lc")
    wv_sb = wslA[:, 0:8192].rearrange("p (a b) -> p a b", a=DC)
    wq_sb = wslB[:, 0:2048].rearrange("p (a b) -> p a b", a=DC)
    wk_sb = wslB[:, 2048:4096].rearrange("p (a b) -> p a b", a=DC)
    X.dma(gps, wv_sb[:, :, :], wvT[0, :, :, :], "lwv")
    X.dma(gps, wq_sb[:, :, :], wqT[0, :, :, :], "lwq")
    X.dma(gps, wk_sb[:, :, :], wkT[0, :, :, :], "lwq")
    for hg in range(4):
        X.wait(ten, "lwv")
        dvmarks = []
        for te in range(NTE):
            ps = PB[te % 2]
            if te >= 2:
                X.wait_val(ten, "dv", dvmarks[te - 2])
            for dch in range(DC):
                ten.matmul(ps[:, :], fmA[:, dch, te * 128:(te + 1) * 128],
                           wv_sb[:, dch, :], start=(dch == 0), stop=(dch == DC - 1))
            X.tick(ten, "pe")
            X.wait(vec, "pe")
            vec.tensor_copy(vgrp[:, te, :], ps[:, :])
            dvmarks.append(X.c["dv"])
        if hg < 3:
            X.wait_val(gps, "pe", X.c["pe"])
            X.dma(gps, wv_sb[:, :, :], wvT[hg + 1, :, :, :], "lwv")
        X.wait(ten, "dv")
        if stage == "p2v" and hg == 0:
            store_x()
            return nc
        for hh in range(4):
            h = hg * 4 + hh
            X.wait(ten, "lwq")
            X.wait(ten, "dv")
            X.wait(ten, "ac")
            for tg in range(2):
                for dch in range(DC):
                    ten.matmul(PB[tg][:, :], wq_sb[:, dch, :],
                               fmA[:, dch, 512 + tg * 512:1024 + tg * 512],
                               start=(dch == 0), stop=(dch == DC - 1))
            for sg in range(3):
                for dch in range(DC):
                    ten.matmul(PB[2 + sg][:, :], wk_sb[:, dch, :],
                               fmA[:, dch, sg * 512:(sg + 1) * 512],
                               start=(dch == 0), stop=(dch == DC - 1))
            X.tick(ten, "pe")
            if h < 15:
                X.wait_val(gps, "pe", X.c["pe"])
                X.dma(gps, wq_sb[:, :, :], wqT[h + 1, :, :, :], "lwq")
                X.dma(gps, wk_sb[:, :, :], wkT[h + 1, :, :, :], "lwq")
            X.wait(vec, "pe")
            for tg in range(2):
                vec.tensor_copy(qkraw[:, tg * 512:(tg + 1) * 512], PB[tg][:, :])
            for sg in range(3):
                vec.tensor_copy(qkraw[:, 1024 + sg * 512:1024 + (sg + 1) * 512],
                                PB[2 + sg][:, :])
            chunks = [(512 + c * 512, qkraw[:, c * 512:(c + 1) * 512]) for c in range(2)]                 + [(c * 512, qkraw[:, 1024 + c * 512:1024 + (c + 1) * 512]) for c in range(3)]
            pemarks = []
            dvm = []
            X.wait(ten, "dv")
            for ch, (lo, src) in enumerate(chunks):
                ps = PB[4 + ch % 2]
                if ch >= 2:
                    X.wait_val(ten, "dv", dvm[ch - 2])
                ten.matmul(ps[:, :], SWP, src, start=True, stop=True)
                X.tick(ten, "pe")
                pemarks.append(X.c["pe"])
                X.wait_val(vec, "pe", pemarks[ch])
                tt(ropeA[:, :], src, tabs[:, 0, lo:lo + 512], ALU.mult)
                tt(ropeB[:, :], ps[:, :], tabs[:, 1, lo:lo + 512], ALU.mult)
                tt(src, ropeA[:, :], ropeB[:, :], ALU.add)
                dvm.append(X.c["dv"])
            if stage == "p2q" and h == 0:
                store_x()
                return nc
            X.wait(ten, "dv")
            for g in range(2):
                pvps, zps, bcps = PB[2], PB[3], PB[4]
                rorder = [3, 4, 0, 1, 2, 5, 6, 7]
                geom = []
                for r in rorder:
                    s0, s1 = max(0, r - 4), min(3, r) + 1
                    geom.append((4 * g + r, s0 * 128, (min(3, r) + 1) * 128, r))
                pemarks = []
                acmarks = []
                for ri in range(len(rorder) + 1):
                    if ri < len(rorder):
                        jj, lo, hi, r = geom[ri]
                        st_ = PB[ri % 2]
                        has_far, has_diag = r <= 3, r >= 4
                        ten.matmul(st_[:, lo:hi],
                                   qkraw[:, 1024 + jj * 128:1024 + (jj + 1) * 128],
                                   qkraw[:, g * 512 + lo:g * 512 + hi],
                                   start=True, stop=not (has_far or has_diag))
                        if has_far:
                            ten.matmul(st_[:, r * 128:(r + 1) * 128], IDC, MFAR,
                                       start=False, stop=not has_diag)
                        if has_diag:
                            ten.matmul(st_[:, (r - 4) * 128:(r - 3) * 128], IDC, MDIAG,
                                       start=False, stop=True)
                        X.tick(ten, "pe")
                        pemarks.append(X.c["pe"])
                        X.wait_val(sca, "pe", pemarks[ri])
                        sca.activation(out=Pt[:, ri % 2, lo:hi], in_=st_[:, lo:hi],
                                       func=AF.Exp, bias=kmk[:, jj:jj + 1], scale=SCALE)
                        acmarks.append(X.c["ac"])
                    if ri >= 1:
                        pj = ri - 1
                        jj, lo, hi, r = geom[pj]
                        X.wait_val(ten, "ac", acmarks[pj])
                        last = pj == len(rorder) - 1
                        ten.matmul(zps[0:1, lo:hi], onesc[:, :], Pt[:, pj % 2, lo:hi],
                                   start=(pj == 0), stop=last)
                        ten.matmul(pvps[:, lo:hi],
                                   vgrp[:, jj, hh * 128:(hh + 1) * 128],
                                   Pt[:, pj % 2, lo:hi], start=(pj == 0), stop=last)
                X.tick(ten, "pe")
                X.wait(vec, "pe")
                vec.tensor_copy(zrow[0:1, :], zps[0:1, :])
                vec.reciprocal(zrow[0:1, :], zrow[0:1, :])
                X.tick(vec, "dv")
                X.wait(ten, "dv")
                ten.matmul(bcps[:, :], onesrf[:, :], zrow[0:1, :], start=True, stop=True)
                X.tick(ten, "pe")
                X.wait(vec, "pe")
                vec.tensor_copy(bcz[:, :], bcps[:, :])
                tt(aout[:, h, g * 512:(g + 1) * 512], pvps[:, :], bcz[:, :], ALU.mult)
                X.tick(vec, "dv")
                X.wait(ten, "dv")

    if stage == "p2a":
        store_x()
        return nc
    if "dbg_aout" in dbg_out:
        for dch in range(DC):
            vec.tensor_copy(stg[:, 0:1024], aout[:, dch, :])
            X.tick(vec, "dv")
            X.wait(sync, "dv")
            X.dma(sync, dbg_out["dbg_aout"][dch * 128:(dch + 1) * 128, :],
                  stg[:, 0:1024], "st")
        X.wait(vec, "st")

    wo_a = wslA[:, 0:8192].rearrange("p (a b) -> p a b", a=DC)
    wo_b = wslB[:, 0:8192].rearrange("p (a b) -> p a b", a=DC)
    X.wait(gps, "pe")
    X.wait(gps, "dv")
    X.dma(gps, wo_a[:, :, :], woT[0, :, :, :], "lwoA")
    X.dma(gps, wo_b[:, :, :], woT[1, :, :, :], "lwoB")
    lwo_need = [("lwoA", 16), ("lwoB", 16), None, None]
    for ocg in range(4):
        wo_sb = wo_a if ocg % 2 == 0 else wo_b
        X.wait_val(ten, *lwo_need[ocg])
        X.wait(ten, "dv")
        dvmarks = []
        for tt_ in range(NT):
            ps = PB[tt_ % 2]
            if tt_ >= 2:
                X.wait_val(ten, "dv", dvmarks[tt_ - 2])
            for dch in range(DC):
                ten.matmul(ps[:, :], aout[:, dch, tt_ * 128:(tt_ + 1) * 128],
                           wo_sb[:, dch, :], start=(dch == 0), stop=(dch == DC - 1))
            X.tick(ten, "pe")
            if tt_ == NT - 1 and ocg + 2 < 4:
                nm2 = "lwoA" if ocg % 2 == 0 else "lwoB"
                X.wait_val(gps, "pe", X.c["pe"])
                X.dma(gps, wo_sb[:, :, :], woT[ocg + 2, :, :, :], nm2)
                lwo_need[ocg + 2] = (nm2, X.c[nm2])
            X.wait(vec, "pe")
            xs = x_res[:, tt_, ocg * 512:(ocg + 1) * 512]
            tt(xs, xs, ps[:, :], ALU.add)
            dvmarks.append(X.c["dv"])
        X.wait(ten, "dv")

    if stage == "attn":
        store_x()
        return nc

    for tt_ in range(NT):
        X.wait(vec, "dv")
        X.wait(vec, "sx")
        X.wait(vec, "tr2")
        X.wait(sca, "sx")
        X.wait(sca, "tr2")
        xnst = stgb[:, tt_ % 2, :]
        ln_tile(x_res[:, tt_, :], xnst, stgb[:, (tt_ + 1) % 2, :])
        X.wait(sync, "dv")
        X.dma(sync, xn2_hbm[tt_ * 128:(tt_ + 1) * 128, :], xnst, "sx")
        for dch in range(DC):
            X.dmat(fmA[:, dch, tt_ * 128:(tt_ + 1) * 128],
                   xnst[:, dch * 128:(dch + 1) * 128], "tr2")


    gt_sb = wslB[:, 4096:4352].rearrange("p (a b) -> p a b", a=DC)
    X.wait(gps, "pe")
    X.dma(gps, gt_sb[:, :, :], gateT[:, :, :], "lw")
    X.dma(gps, consts[:, 4, :], cN_ap["triu"][:, :], "lcg")
    vec.memset(stg[:, 0:128], 0.0)
    vec.memset(crow[0:1, 0, :], 0.0)
    X.tick(vec, "dv")
    X.wait(sync, "dv")
    X.dma(sync, lists_pay[0:1, 0:16384].rearrange("a (p f) -> (a p) f", p=128),
          stg[:, 0:128], "sl")
    X.wait(ten, "lw")
    X.wait(ten, "lcg")
    X.wait(ten, "tr2")
    X.wait(ten, "dv")
    X.wait(gps, "lc")
    X.wait(vec, "lc")
    for tt_ in range(NT):
        X.wait(vec, "slg")
        X.wait(ten, "dv")
        lps = PB[2][:, 0:16]
        for dch in range(DC):
            ten.matmul(lps, fmA[:, dch, tt_ * 128:(tt_ + 1) * 128],
                       gt_sb[:, dch, :], start=(dch == 0), stop=(dch == DC - 1))
        X.tick(ten, "pe")
        X.wait(vec, "pe")
        lg = rt[:, 0, :]
        vec.tensor_copy(lg, lps)
        m1, idx1, m2, idx2, w1c, w2c, p1c, p2c = (tokr[:, tt_, i:i + 1] for i in range(8))
        sc1, sc2 = stats[:, 8:9], stats[:, 9:10]
        vec.reduce_max(m1, lg, axis=AXX)
        E1, E2, scr = rt[:, 1, :], rt[:, 2, :], rt[:, 3, :]
        ts(scr, lg, m1, None, op0=ALU.is_ge)
        tt(scr, scr, iotmB[:, :], ALU.mult)
        ts(scr, scr, 4e4, op0=ALU.add)
        vec.tensor_reduce(idx1, scr, axis=AXX, op=ALU.min)
        ts(E1, iot16[:, :], idx1, None, op0=ALU.is_equal)
        ts(scr, E1, 1e6)
        tt(scr, lg, scr, ALU.subtract)
        vec.reduce_max(m2, scr, axis=AXX)
        ts(rt[:, 4, :], scr, m2, None, op0=ALU.is_ge)
        tt(rt[:, 4, :], rt[:, 4, :], iotmB[:, :], ALU.mult)
        ts(rt[:, 4, :], rt[:, 4, :], 4e4, op0=ALU.add)
        vec.tensor_reduce(idx2, rt[:, 4, :], axis=AXX, op=ALU.min)
        ts(E2, iot16[:, :], idx2, None, op0=ALU.is_equal)
        ts(sc1, m1, -1.0)
        X.tick(vec, "dv")
        X.wait(sca, "dv")
        sca.activation(out=rt[:, 4, :], in_=lg, func=AF.Exp, bias=sc1, scale=1.0,
                       accum_out=sc2)
        X.tick(sca, "ac")
        X.wait(vec, "ac")
        vec.reciprocal(w1c, sc2)
        tt(sc1, m2, m1, ALU.subtract)
        X.tick(vec, "dv")
        X.wait(sca, "dv")
        sca.activation(out=sc2, in_=sc1, func=AF.Exp)
        X.tick(sca, "ac")
        X.wait(vec, "ac")
        tt(w2c, sc2, w1c, ALU.mult)
        tt(rts[:, :], E1, E2, ALU.add)
        X.tick(vec, "dv")
        X.wait(ten, "dv")
        cps = PB[3][:, 0:16]
        ten.matmul(cps, TRIU, rts[:, :], start=True, stop=False)
        ten.matmul(cps, onesrf[:, :], crow[0:1, 0, :], start=False, stop=True)
        ten.matmul(PB[4][0:1, 0:16], onesc[:, :], rts[:, :], start=True, stop=True)
        X.tick(ten, "pe")
        X.wait(vec, "pe")
        Cc = rt[:, 5, :]
        vec.tensor_copy(Cc, cps)
        vec.tensor_copy(crow[0:1, 1, :], PB[4][0:1, 0:16])
        vec.tensor_tensor(crow[0:1, 0, :], crow[0:1, 0, :], crow[0:1, 1, :],
                          op=ALU.add)
        tt(rt[:, 6, :], Cc, E1, ALU.mult)
        vec.reduce_sum(p1c, rt[:, 6, :], axis=AXX)
        tt(rt[:, 6, :], Cc, E2, ALU.mult)
        vec.reduce_sum(p2c, rt[:, 6, :], axis=AXX)
        gidc = stats[:, 10:11]
        ts(gidc, ccols[:, 0:1], 1.0, float(128 * tt_), op0=ALU.mult, op1=ALU.add)
        tt(gidc, gidc, iotcol[:, :], ALU.add)
        for kk, (idxc, posc) in enumerate(((idx1, p1c), (idx2, p2c))):
            sp = stats[:, 11 + kk:12 + kk]
            ts(sp, idxc, 1024.0)
            tt(sp, sp, posc, ALU.add)
            ovg = stats[:, 13 + kk:14 + kk]
            ts(ovg, posc, 1024.0, None, op0=ALU.is_ge)
            ts(ovg, ovg, GUARD)
            tt(sp, sp, ovg, ALU.add)
            vec.tensor_copy(icol[:, kk:kk + 1], sp)
        X.tick(vec, "dv")
        X.wait(gps, "dv")
        X.wait(gps, "sl")
        for kk in range(2):
            X.inc(gps.indirect_dma_start(
                out=lists_pay[:, :], out_offset=IOA(ap=icol[:, kk:kk + 1], axis=1),
                in_=gidc, in_offset=None,
                bounds_check=16383, oob_is_err=False), "slg", 16)
        if "dbg_route" in dbg_out:
            vec.tensor_copy(stg[:, 0:8], tokr[:, tt_, :])
            X.tick(vec, "dv")
            X.wait(sync, "dv")
            X.dma(sync, dbg_out["dbg_route"][tt_ * 128:(tt_ + 1) * 128, 0:8],
                  stg[:, 0:8], "st")
    X.wait(vec, "pe")
    X.wait(sync, "dv")
    X.dma(sync, lists_pay[0:1, 16384:16400], crow[0:1, 0, :], "sl")
    if "dbg_lists" in dbg_out:
        X.wait(sync, "sl")
        X.wait(vec, "sl")
        vec.memset(stats[:, 15:16], 0.0)
        X.dma(sync, dbg_out["dbg_lists"][:, :], lists_pay[:, :], "st")

    X.wait(gps, "sl")
    X.wait(gps, "slg")
    X.wait(gps, "sx")
    X.inc(gps.collective_compute(
        "AllGather", ALU.bypass, ins=[lists_pay[:, :]], outs=[lists_all[:, :]],
        replica_groups=[list(range(8))]), "cc", 1)
    X.inc(gps.collective_compute(
        "AllGather", ALU.bypass, ins=[xn2_hbm[:, :]], outs=[xnall[:, :]],
        replica_groups=[list(range(8))]), "cc", 1)

    wsv = [(wslA[:, 0:2048].rearrange("p (a b) -> p a b", a=DC),
            wslA[:, 2048:4096].rearrange("p (a b) -> p a b", a=DC)),
           (wslA[:, 4096:6144].rearrange("p (a b) -> p a b", a=DC),
            wslA[:, 6144:8192].rearrange("p (a b) -> p a b", a=DC))]
    X.wait(gps, "pe")
    X.dma(gps, wsv[0][0][:, :, :], wsg[0, :, :, :], "lwshA")
    X.dma(gps, wsv[0][1][:, :, :], wsu[0, :, :, :], "lwshA")
    lwsh_need = [("lwshA", 32)]
    pewm = [0] * ISC
    for is_ in range(ISC):
        wsgv, wsuv = wsv[is_ % 2]
        X.wait_val(ten, *lwsh_need[is_])
        X.wait(ten, "ac")
        X.wait(ten, "dv")
        for tg in range(2):
            for dch in range(DC):
                ten.matmul(PB[0 + tg][:, :], wsgv[:, dch, :],
                           fmA[:, dch, tg * 512:(tg + 1) * 512],
                           start=(dch == 0), stop=(dch == DC - 1))
                ten.matmul(PB[2 + tg][:, :], wsuv[:, dch, :],
                           fmA[:, dch, tg * 512:(tg + 1) * 512],
                           start=(dch == 0), stop=(dch == DC - 1))
        X.tick(ten, "pe")
        pewm[is_] = X.c["pe"]
        if is_ + 1 < ISC:
            j = is_ + 1
            if j >= 2:
                X.wait_val(gps, "pe", pewm[j - 2])
            nm2 = "lwshA" if j % 2 == 0 else "lwshB"
            X.dma(gps, wsv[j % 2][0][:, :, :], wsg[j, :, :, :], nm2)
            X.dma(gps, wsv[j % 2][1][:, :, :], wsu[j, :, :, :], nm2)
            lwsh_need.append((nm2, X.c[nm2]))
        X.wait(sca, "pe")
        X.wait(vec, "pe")
        if "dbg_g0" in dbg_out and is_ == 0:
            vec.tensor_copy(stg[:, 0:512], PB[0][:, :])
            vec.tensor_copy(stg[:, 512:1024], PB[1][:, :])
            X.wait(sync, "dv")
            X.dma(sync, dbg_out["dbg_g0"][:, :], stg[:, 0:1024], "st")
            X.wait(vec, "st")
        if "dbg_slot" in dbg_out and is_ == 0:
            vec.tensor_copy(stg[:, 0:16], wsgv[:, 0, 0:16])
            vec.tensor_copy(stg[:, 16:32], wsgv[:, 1, 0:16])
            X.wait(sync, "dv")
            X.dma(sync, dbg_out["dbg_slot"][:, :], stg[:, 0:32], "st")
            X.wait(vec, "st")
        for tg in range(2):
            sca.activation(out=stgb[:, 0, tg * 512:(tg + 1) * 512], in_=PB[tg][:, :],
                           func=AF.Silu)
            vec.tensor_copy(stgb[:, 1, tg * 512:(tg + 1) * 512], PB[2 + tg][:, :])
        X.tick(sca, "ac")
        X.wait(vec, "ac")
        tt(fmB[:, is_, :], stgb[:, 0, 0:1024], stgb[:, 1, 0:1024], ALU.mult)
        X.tick(vec, "dv")
        if stage.startswith("p7a") and is_ == int(stage[3:] or 0):
            vec.tensor_copy(stg[:, 0:1024], fmB[:, 0, :])
            vec.tensor_copy(stg[:, 1024:2048], fmB[:, min(is_, 1), :])
            X.wait(sync, "dv")
            for tt_ in range(NT):
                X.dma(sync, y_out[tt_ * 128:(tt_ + 1) * 128, :],
                      stg[:, :] if tt_ == 0 else x_res[:, tt_, :], "st")
            X.wait(sync, "st")
            return nc
    if "dbg_xn2T" in dbg_out:
        for dch in range(DC):
            vec.tensor_copy(stg[:, 0:1024], fmA[:, dch, 0:1024])
            X.wait(sync, "dv")
            X.dma(sync, dbg_out["dbg_xn2T"][dch * 128:(dch + 1) * 128, :],
                  stg[:, 0:1024], "st")
            X.wait(vec, "st")
    if "dbg_gu" in dbg_out:
        vec.tensor_copy(stg[:, 0:1024], fmB[:, 0, :])
        X.wait(sync, "dv")
        X.dma(sync, dbg_out["dbg_gu"][:, :], stg[:, 0:1024], "st")
        X.wait(vec, "st")
    if "dbg_gu_all" in dbg_out:
        for ch in range(22):
            X.wait(vec, "dv")
            vec.tensor_copy(stg[:, 0:1024], fmB[:, ch, :])
            X.wait(sync, "dv")
            X.dma(sync, dbg_out["dbg_gu_all"][ch * 128:(ch + 1) * 128, :],
                  stg[:, 0:1024], "st")
            X.wait(vec, "st")
    wdvA = wslA[:, 0:16 * 512].rearrange("p (a b) -> p a b", a=16)
    wdvB = wslB[:, 0:6 * 512].rearrange("p (a b) -> p a b", a=6)
    for ocg in range(4):
        X.wait(gps, "pe")
        X.dma(gps, wdvA[:, :, :], wsdA[ocg, :, :, :], "lw")
        X.dma(gps, wdvB[:, :, :], wsdB[ocg, :, :, :], "lw")
        if "dbg_wdv" in dbg_out and ocg == 0:
            X.wait(vec, "lw")
            for ii in range(22):
                wt = wdvA[:, ii, :] if ii < 16 else wdvB[:, ii - 16, :]
                vec.tensor_copy(stg[:, 0:512], wt)
                X.wait(sync, "dv")
                X.dma(sync, dbg_out["dbg_wdv"][:, ii * 512:(ii + 1) * 512],
                      stg[:, 0:512], "st")
                X.wait(vec, "st")
        X.wait(ten, "lw")
        X.wait(ten, "dv")
        for tt_ in range(NT):
            ps = PB[4 + tt_ % 2]
            X.wait(ten, "dv")
            for is_ in range(ISC):
                wtile = wdvA[:, is_, :] if is_ < 16 else wdvB[:, is_ - 16, :]
                ten.matmul(ps[:, :], fmB[:, is_, tt_ * 128:(tt_ + 1) * 128],
                           wtile, start=(is_ == 0), stop=(is_ == ISC - 1))
            X.tick(ten, "pe")
            X.wait(vec, "pe")
            xs = x_res[:, tt_, ocg * 512:(ocg + 1) * 512]
            tt(xs, xs, ps[:, :], ALU.add)
            X.tick(vec, "dv")
        X.wait(ten, "dv")

    if stage == "shared":
        store_x()
        return nc

    X.wait(sync, "cc")
    X.wait(vec, "cc")
    X.wait(gps, "cc")
    X.dma(sync, cnt8[:, :], lists_all[:, 16384:16400], "lg")
    X.wait(ten, "lg")
    ten.transpose(PB[0][0:16, 0:8], cnt8[:, :], eye8[:, :])
    ten.matmul(PB[1][0:8, 0:16], triu8[:, :], cnt8[:, :], start=True, stop=True)
    X.tick(ten, "pe")
    X.wait(vec, "pe")
    vec.tensor_copy(cnt8T[:, :], PB[0][0:16, 0:8])
    vec.tensor_copy(cume[:, :], PB[1][0:8, 0:16])
    X.tick(vec, "dv")
    X.wait(ten, "dv")
    ten.matmul(PB[2][0:1, 0:16], cidoh[:, :], cume[:, :], start=True, stop=True)
    X.tick(ten, "pe")
    X.wait(vec, "pe")
    vec.tensor_copy(row16[:, :], PB[2][0:1, 0:16])
    X.tick(vec, "dv")
    X.wait(ten, "dv")
    ten.matmul(PB[2][:, 0:16], onesrf[:, :], row16[:, :], start=True, stop=True)
    X.tick(ten, "pe")
    X.wait(vec, "pe")
    vec.tensor_copy(rt[:, 7, :], PB[2][:, 0:16])
    gps.iota(slotm[:, 8:16], pattern=[[128, 8]], base=0, channel_multiplier=1,
             allow_small_or_imprecise_dtypes=True)
    X.tick(gps, "gp")
    X.tick(vec, "dv")

    for o in range(2):
        X.wait(ten, "dv")
        X.wait(ten, "gp")
        ten.matmul(PB[3][0:8, 0:1], cnt8T[:, :], eoh[:, o:o + 1], start=True, stop=True)
        X.tick(ten, "pe")
        X.wait(vec, "pe")
        vec.tensor_copy(col8[:, :], PB[3][0:8, 0:1])
        X.tick(vec, "dv")
        X.wait(ten, "dv")
        ten.matmul(PB[3][0:1, 0:8], col8[:, :], triu8[:, :], start=True, stop=True)
        X.tick(ten, "pe")
        X.wait(vec, "pe")
        vec.tensor_copy(row8[:, :], PB[3][0:1, 0:8])
        X.tick(vec, "dv")
        X.wait(ten, "dv")
        ten.matmul(PB[3][:, 0:8], onesrf[:, :], row8[:, :], start=True, stop=True)
        X.tick(ten, "pe")
        X.wait(vec, "pe")
        bnd = slotm[:, 0:8]
        vec.tensor_copy(bnd, PB[3][:, 0:8])
        sio = slotm[:, 8:16]
        cidx = slotm[:, 16:24]
        base = slotm[:, 24:32]
        vec.memset(cidx, 0.0)
        vec.memset(base, 0.0)
        for c in range(1, 8):
            ge = slotm[:, 32:40]
            ts(ge, sio, bnd[:, c:c + 1], None, op0=ALU.is_ge)
            tt(cidx, cidx, ge, ALU.add)
            ts(ge, ge, bnd[:, c:c + 1], None, op0=ALU.mult)
            tt(base, base, ge, ALU.max)
        fl = slotm[:, 32:40]
        ts(fl, cidx, float(LISTW))
        tt(fl, fl, sio, ALU.add)
        tt(fl, fl, base, ALU.subtract)
        ts(fl, fl, ccols[:, 1 + o:2 + o], None, op0=ALU.add)
        vec.tensor_copy(icol[:, 0:8], fl)
        X.tick(vec, "dv")
        if "dbg_slot" in dbg_out and o == 0:
            vec.tensor_copy(stg[:, 0:32], slotm[:, 0:32])
            X.tick(vec, "dv")
            X.wait(sync, "dv")
            X.dma(sync, dbg_out["dbg_slot"][:, 0:32], stg[:, 0:32], "st")
        X.wait(gps, "dv")
        for col in range(8):
            X.inc(gps.indirect_dma_start(
                out=sids[:, col:col + 1], out_offset=None,
                in_=lists_all.rearrange("a b -> (a b)")[None, :],
                in_offset=IOA(ap=icol[:, col:col + 1], axis=1)), "lgg", 16)
        X.wait(vec, "lgg")
        vec.tensor_copy(icol[:, 0:8], sids[:, :])
        X.tick(vec, "dv")
        X.wait(gps, "dv")
        for col in range(8):
            X.inc(gps.indirect_dma_start(
                out=gstg[:, col % 2, :], out_offset=None,
                in_=xnall[:, :],
                in_offset=IOA(ap=icol[:, col:col + 1], axis=0)), "lgg", 16)
            X.wait(sync, "lgg")
            for dch in range(DC):
                X.dmat(fmB[:, dch, col * 128:(col + 1) * 128],
                       gstg[:, col % 2, dch * 128:(dch + 1) * 128], "trb")
            if col % 2 == 1:
                X.wait(gps, "trb")
        X.wait(ten, "trb")
        wg_sb = wslA[:, 0:8192].rearrange("p (a b) -> p a b", a=DC)
        wu_sb = wslB[:, 0:8192].rearrange("p (a b) -> p a b", a=DC)
        for igrp in range(3):
            X.wait(gps, "pe")
            X.dma(gps, wg_sb[:, :, :], wg[o, igrp, :, :, :], "lw")
            X.dma(gps, wu_sb[:, :, :], wu[o, igrp, :, :, :], "lw")
            X.wait(ten, "lw")
            for ic in range(4):
                X.wait(ten, "ac")
                X.wait(ten, "dv")
                for tg in range(2):
                    for dch in range(DC):
                        ten.matmul(PB[tg][:, :], wg_sb[:, dch, ic * 128:(ic + 1) * 128],
                                   fmB[:, dch, tg * 512:(tg + 1) * 512],
                                   start=(dch == 0), stop=(dch == DC - 1))
                        ten.matmul(PB[2 + tg][:, :], wu_sb[:, dch, ic * 128:(ic + 1) * 128],
                                   fmB[:, dch, tg * 512:(tg + 1) * 512],
                                   start=(dch == 0), stop=(dch == DC - 1))
                X.tick(ten, "pe")
                X.wait(sca, "pe")
                X.wait(vec, "pe")
                for tg in range(2):
                    sca.activation(out=stgb[:, 0, tg * 512:(tg + 1) * 512],
                                   in_=PB[tg][:, :], func=AF.Silu)
                    vec.tensor_copy(stgb[:, 1, tg * 512:(tg + 1) * 512], PB[2 + tg][:, :])
                X.tick(sca, "ac")
                X.wait(vec, "ac")
                tt(fmA[:, igrp * 4 + ic, 0:1024], stgb[:, 0, 0:1024], stgb[:, 1, 0:1024],
                   ALU.mult)
                X.tick(vec, "dv")
        wdx = wslA[:, 0:12 * 512].rearrange("p (a b) -> p a b", a=12)
        for ocg in range(4):
            X.wait(gps, "pe")
            X.dma(gps, wdx[:, :, :], wd[o, ocg, :, :, :], "lw")
            X.wait(ten, "lw")
            X.wait(ten, "dv")
            for rs in range(8):
                ps = PB[4 + rs % 2]
                X.wait(ten, "dv")
                for is_ in range(12):
                    ten.matmul(ps[:, :], fmA[:, is_, rs * 128:(rs + 1) * 128],
                               wdx[:, is_, :], start=(is_ == 0), stop=(is_ == 11))
                X.tick(ten, "pe")
                X.wait(vec, "pe")
                vec.tensor_copy(stgb[:, rs % 2, 0:512], ps[:, :])
                X.tick(vec, "dv")
                X.wait(sync, "dv")
                X.dma(sync, eo_hbm[o * 1024 + rs * 128:o * 1024 + (rs + 1) * 128,
                                   ocg * 512:(ocg + 1) * 512], stgb[:, rs % 2, 0:512], "se")
            X.wait(vec, "se")
    if "dbg_eo" in dbg_out:
        X.wait(sync, "se")
        for rr in range(16):
            X.dma(sync, gstg[:, 0, :], eo_hbm[rr * 128:(rr + 1) * 128, :], "lg")
            X.wait(vec, "lg")
            vec.tensor_copy(stg[:, :], gstg[:, 0, :])
            X.tick(vec, "dv")
            X.wait(sync, "dv")
            X.dma(sync, dbg_out["dbg_eo"][rr * 128:(rr + 1) * 128, :], stg[:, :], "st")
        X.wait(vec, "st")
    if "dbg_cnt" in dbg_out:
        X.wait(sync, "dv")
        vec.tensor_copy(stg[0:8, 0:16], cnt8[:, :])
        X.tick(vec, "dv")
        X.wait(sync, "dv")
        X.dma(sync, dbg_out["dbg_cnt"][:, :], stg[0:8, 0:16], "st")
        X.wait(vec, "st")

    X.wait(gps, "se")
    X.inc(gps.collective_compute(
        "AllGather", ALU.bypass, ins=[eo_hbm[:, :]], outs=[eoall[:, :]],
        replica_groups=[list(range(8))]), "cc", 1)
    X.wait(gps, "cc")
    X.wait(vec, "cc")
    for tt_ in range(NT):
        for kk in range(2):
            idxc = tokr[:, tt_, 1 + 2 * kk:2 + 2 * kk]
            posc = tokr[:, tt_, 6 + kk:7 + kk]
            ek = rt[:, 0, :]
            ts(ek, iot16[:, :], idxc, None, op0=ALU.is_equal)
            tt(ek, ek, rt[:, 7, :], ALU.mult)
            offc = stats[:, 8:9]
            vec.reduce_sum(offc, ek, axis=AXX)
            gp_ = stats[:, 9:10]
            tt(gp_, posc, offc, ALU.add)
            vec.tensor_scalar_min(gp_, gp_, 1023.0)
            ts(stats[:, 10:11], idxc, 1024.0)
            tt(gp_, gp_, stats[:, 10:11], ALU.add)
            vec.tensor_copy(icol[:, kk:kk + 1], gp_)
            X.tick(vec, "dv")
            X.wait(gps, "dv")
            X.inc(gps.indirect_dma_start(
                out=gstg[:, kk, :], out_offset=None,
                in_=eoall[:, :],
                in_offset=IOA(ap=icol[:, kk:kk + 1], axis=0)), "lgg", 16)
        X.wait(vec, "lgg")
        for kk in range(2):
            wc = tokr[:, tt_, 4 + kk:5 + kk]
            vec.tensor_copy(stg[:, :], gstg[:, kk, :])
            vec.scalar_tensor_tensor(x_res[:, tt_, :], stg[:, :], wc,
                                     x_res[:, tt_, :], op0=ALU.mult, op1=ALU.add)
        X.tick(vec, "dv")
    store_x()
    return nc


def rope_tabs_for_core(c):
    s0 = (c % 2) * 1024
    pos = np.maximum(np.arange(TE) + s0 - 512, 0).astype(np.float32)
    inv = (1.0 / (10000.0 ** (np.arange(0, DH, 2, dtype=np.float32) / np.float32(DH)))).astype(np.float32)
    fr = (pos[None, :] * inv[:, None]).astype(np.float32)
    cos, sin = np.cos(fr).astype(np.float32), np.sin(fr).astype(np.float32)
    cosI = np.empty((128, TE), np.float32)
    sinIS = np.empty((128, TE), np.float32)
    cosI[0::2] = cos
    cosI[1::2] = cos
    sinIS[0::2] = -sin
    sinIS[1::2] = sin
    return cosI, sinIS


def prep_inputs(L0_attn_x, L0_attn_Wq, L0_attn_Wk, L0_attn_Wv, L0_attn_Wo,
                L0_ffn_gate_w, L0_ffn_w_gate, L0_ffn_w_up, L0_ffn_w_down,
                L0_ffn_ws_gate, L0_ffn_ws_up, L0_ffn_ws_down):
    f = np.ascontiguousarray

    def pmaj(w, width):
        K = w.shape[0]
        return w.reshape(K // 128, 128, width).transpose(1, 0, 2)

    WqT = np.stack([pmaj(L0_attn_Wq.T[:, h * DH:(h + 1) * DH], DH) for h in range(NH)])
    WkT = np.stack([pmaj(L0_attn_Wk.T[:, h * DH:(h + 1) * DH], DH) for h in range(NH)])
    WvT = np.stack([pmaj(L0_attn_Wv.T[:, g * 512:(g + 1) * 512], 512) for g in range(4)])
    WoT = np.stack([pmaj(L0_attn_Wo.T[:, g * 512:(g + 1) * 512], 512) for g in range(4)])
    gT = pmaj(f(L0_ffn_gate_w.T), E)
    wsg = np.stack([pmaj(L0_ffn_ws_gate[:, i * 128:(i + 1) * 128], 128) for i in range(ISC)])
    wsu = np.stack([pmaj(L0_ffn_ws_up[:, i * 128:(i + 1) * 128], 128) for i in range(ISC)])
    wsdA = np.stack([pmaj(L0_ffn_ws_down[0:2048, g * 512:(g + 1) * 512], 512) for g in range(4)])
    wsdB = np.stack([pmaj(L0_ffn_ws_down[2048:ISH, g * 512:(g + 1) * 512], 512) for g in range(4)])
    wgp = np.zeros((E, D, IPAD), np.float32)
    wgp[:, :, :1408] = L0_ffn_w_gate
    wup = np.zeros((E, D, IPAD), np.float32)
    wup[:, :, :1408] = L0_ffn_w_up
    wdp = np.zeros((E, IPAD, D), np.float32)
    wdp[:, :1408, :] = L0_ffn_w_down
    wg_s = np.stack([np.stack([pmaj(wgp[e][:, g * 512:(g + 1) * 512], 512) for g in range(3)])
                     for e in range(E)])
    wu_s = np.stack([np.stack([pmaj(wup[e][:, g * 512:(g + 1) * 512], 512) for g in range(3)])
                     for e in range(E)])
    wd_s = np.stack([np.stack([pmaj(wdp[e][:, g * 512:(g + 1) * 512], 512) for g in range(4)])
                     for e in range(E)])

    in_maps = []
    for c in range(8):
        b, hf = c // 2, c % 2
        own = L0_attn_x[b, hf * 1024:(hf + 1) * 1024, :]
        if hf == 0:
            halo = np.zeros((HAL, D), np.float32)
        else:
            halo = L0_attn_x[b, 512:1024, :]
        xh = np.concatenate([halo, own], 0).astype(np.float32)
        cosI, sinIS = rope_tabs_for_core(c)
        km = np.zeros((128, NTE), np.float32)
        if hf == 0:
            km[:, 0:4] = BIGNEG
        cidv = np.zeros((1, 8), np.float32)
        cidv[0, 0], cidv[0, 1], cidv[0, 2] = c, 2 * c, 2 * c + 1
        cidohc = np.zeros((8, 1), np.float32)
        cidohc[c, 0] = 1
        eoh = np.zeros((E, 2), np.float32)
        eoh[2 * c, 0] = 1
        eoh[2 * c + 1, 1] = 1
        in_maps.append(dict(
            xh=xh, wqT=f(WqT), wkT=f(WkT), wvT=f(WvT), woT=f(WoT), gateT=f(gT),
            wsg=f(wsg), wsu=f(wsu), wsdA=f(wsdA), wsdB=f(wsdB),
            wg=f(wg_s[2 * c:2 * c + 2]), wu=f(wu_s[2 * c:2 * c + 2]),
            wd=f(wd_s[2 * c:2 * c + 2]),
            cosI=cosI, sinIS=sinIS, kmask=km, cidv=cidv, cidohc=cidohc, eoh=eoh))
    return in_maps
_NC_CACHE = {}


def kernel(**inputs):
    from concourse.bass_utils import run_bass_kernel_spmd
    in_maps = prep_inputs(**inputs)
    if "nc" not in _NC_CACHE:
        _NC_CACHE["nc"] = build_nc(stage="full")
    nc = _NC_CACHE["nc"]
    res = run_bass_kernel_spmd(nc, in_maps, core_ids=list(range(8)))
    x = inputs["L0_attn_x"]
    out = np.zeros((4, 2048, 2048), np.float32)
    for c in range(8):
        b, hf = c // 2, c % 2
        out[b, hf * 1024:(hf + 1) * 1024, :] = res.results[c]["y"]
    return out


# version 03 (final)
# speedup vs baseline: 1.2359x; 1.0859x over previous
import numpy as np
import ml_dtypes
import concourse.bass as bass
import concourse.mybir as mybir
from concourse.bass_core_rust import IndirectOffsetOnAxis as IOA

F32 = mybir.dt.float32
BF16 = mybir.dt.bfloat16
I32 = mybir.dt.int32
AF = mybir.ActivationFunctionType
ALU = mybir.AluOpType
AXX = mybir.AxisListType.X
BF = ml_dtypes.bfloat16

D, DC, NH, DH = 2048, 16, 16, 128
T, NT, HAL, TE, NTE = 1024, 8, 512, 1536, 12
SCALE = float(np.float32(DH) ** -0.5)
BIGNEG = -1e9
E, CAP = 16, 1024
ISH, ISC = 2816, 22
IPAD = 1536
GUARD = float(2 ** 20)
LISTW = 16400


class EngProxy:

    CHAIN = {"tensor_scalar", "tensor_tensor", "tensor_copy", "tensor_reduce",
             "reduce_max", "reduce_sum", "reciprocal", "memset", "activation",
             "iota", "scalar_tensor_tensor", "tensor_scalar_min",
             "tensor_scalar_add", "tensor_scalar_mul", "tensor_scalar_max",
             "tensor_single_scalar", "affine_select", "bn_stats", "bn_aggr",
             "tensor_add", "tensor_sub", "tensor_mul", "tensor_max",
             "partition_broadcast", "memzero"}

    def __init__(self, eng, X=None, chname=None):
        self._eng = eng
        self.last = None
        self._X = X
        self._chname = chname
        if X is not None and chname is not None:
            X.sem(chname)

    def __getattr__(self, name):
        f = getattr(self._eng, name)
        if not callable(f):
            return f
        chain = (name in EngProxy.CHAIN) and self._chname is not None

        def g(*a, **k):
            if chain and self._X.c[self._chname] > 0:
                self._eng.wait_ge(self._X.sem(self._chname), self._X.c[self._chname])
            r = f(*a, **k)
            if hasattr(r, "then_inc"):
                self.last = r
                if chain:
                    self._X.inc(r, self._chname, 1)
            return r
        return g


class Ctx:
    def __init__(self, nc):
        self.nc = nc
        self.c = {}
        self.h = {}

    def sem(self, name):
        if name not in self.h:
            self.h[name] = self.nc.alloc_semaphore(name)
            self.c[name] = 0
        return self.h[name]

    def inc(self, inst, name, n=1):
        inst.then_inc(self.sem(name), n)
        self.c[name] += n

    def tick(self, eng, name):
        if getattr(eng, "_chname", None) == name:
            return
        self.inc(eng.last, name, 1)

    def dma(self, eng, out, in_, name):
        self.inc(eng.dma_start(out=out, in_=in_), name, 16)

    def dmat(self, out, in_, name):
        self.inc(self.nc.sync.dma_start(out=out, in_=in_, transpose=True), name, 16)

    def wait(self, eng, name):
        eng.wait_ge(self.sem(name), self.c[name])

    def wait_val(self, eng, name, val):
        if val > 0:
            eng.wait_ge(self.sem(name), val)


def build_nc(stage="full", dbg=()):
    nc = bass.Bass("TRN2", target_bir_lowering=False, num_devices=8,
                   detect_race_conditions=False)
    X = Ctx(nc)
    sync = EngProxy(nc.sync)
    gps = EngProxy(nc.gpsimd, X, "gp")
    vec = EngProxy(nc.vector, X, "dv")
    sca = EngProxy(nc.scalar, X, "ac")
    ten = EngProxy(nc.tensor)

    def di(name, shape, dtype=F32):
        return nc.dram_tensor(name, list(shape), dtype, kind="ExternalInput").ap()

    xh = di("xh", [TE, D])
    wqT = di("wqT", [NH, 128, DC, DH])
    wkT = di("wkT", [NH, 128, DC, DH])
    wvT = di("wvT", [4, 128, DC, 512])
    woT = di("woT", [4, 128, DC, 512])
    gateT = di("gateT", [128, DC, E])
    wsg = di("wsg", [ISC, 128, DC, 128])
    wsu = di("wsu", [ISC, 128, DC, 128])
    wsdA = di("wsdA", [4, 128, 16, 512])
    wsdB = di("wsdB", [4, 128, 6, 512])
    wg = di("wg", [2, 3, 128, DC, 512])
    wu = di("wu", [2, 3, 128, DC, 512])
    wd = di("wd", [2, 4, 128, 12, 512])
    cosI = di("cosI", [128, TE])
    sinIS = di("sinIS", [128, TE])
    kmaski = di("kmask", [128, NTE])
    cidv = di("cidv", [1, 8])
    cidohc = di("cidohc", [8, 1])
    eohi = di("eoh", [E, 2])

    y_out = nc.dram_tensor("y", [T, D], F32, kind="ExternalOutput").ap()
    dbg_out = {}
    dbg_shapes = {"dbg_xnT": [D, TE], "dbg_q": [DH, T], "dbg_k": [DH, TE],
                  "dbg_xn2T": [D, T], "dbg_gu": [128, T], "dbg_g0": [128, T],
                  "dbg_aout": [D, T], "dbg_route": [T, 16], "dbg_lists": [1, LISTW],
                  "dbg_slot": [128, 32], "dbg_eo": [2 * CAP, D], "dbg_cnt": [8, 16],
                  "dbg_v": [TE, 512], "dbg_gu_all": [22 * 128, 1024],
                  "dbg_wdv": [128, 22 * 512]}
    for nm in dbg:
        dbg_out[nm] = nc.dram_tensor(nm, dbg_shapes[nm], F32, kind="ExternalOutput").ap()

    lists_pay = nc.dram_tensor("lists_pay", [1, LISTW], F32, kind="Internal").ap()
    lists_all = nc.dram_tensor("lists_all", [8, LISTW], F32, kind="Internal").ap()
    xn2_hbm = nc.dram_tensor("xn2_hbm", [T, D], BF16, kind="Internal").ap()
    xnall = nc.dram_tensor("xnall", [8 * T, D], BF16, kind="Internal").ap()
    eo_hbm = nc.dram_tensor("eo_hbm", [2 * CAP, D], BF16, kind="Internal").ap()
    eoall = nc.dram_tensor("eoall", [16 * CAP, D], BF16, kind="Internal").ap()

    swp_np = np.zeros((128, 128), BF)
    for m in range(128):
        swp_np[m ^ 1, m] = 1
    tq_ = np.arange(128)[None, :]
    tk_ = np.arange(128)[:, None]
    cN = {
        "idc": np.eye(128, dtype=BF),
        "swp": swp_np,
        "mdiag": np.where(tk_ <= tq_, 0, BIGNEG).astype(BF),
        "mfar": np.where(tk_ > tq_, 0, BIGNEG).astype(BF),
        "triu": np.triu(np.ones((128, 128)), 1).astype(BF),
    }
    cN_ap = {k: nc.inline_tensor(v, k).ap() for k, v in cN.items()}
    onescol_b = nc.inline_tensor(np.ones((128, 1), BF), "onescol_b").ap()
    onesrow_f = nc.inline_tensor(np.ones((1, 128), np.float32), "onesrow_f").ap()
    iot16_c = nc.inline_tensor(
        np.broadcast_to(np.arange(16, dtype=np.float32), (128, 16)).copy(), "iot16").ap()
    iotmB_c = nc.inline_tensor(
        (np.broadcast_to(np.arange(16, dtype=np.float32), (128, 16)) - 4e4).copy(), "iotmB").ap()
    iotcol_c = nc.inline_tensor(np.arange(128, dtype=np.float32)[:, None].copy(), "iotcol").ap()
    eye8_f = nc.inline_tensor(np.eye(8, dtype=np.float32), "eye8").ap()
    _rb = np.array([(e % 2) * 8192 + (e // 2) * 1024 for e in range(16)], np.float32)
    rbase_f = nc.inline_tensor(np.broadcast_to(_rb, (128, 16)).copy(), "rbasec").ap()
    eps_f = nc.inline_tensor(np.full((128, 1), 1e-5, np.float32), "epsc").ap()
    triu8_f = nc.inline_tensor(np.triu(np.ones((8, 8), np.float32), 1), "triu8").ap()

    A = nc.alloc_sbuf_tensor
    AT = nc.alloc_sbuf_tensor_at

    def addr(h):
        return nc.lookup_mloc(h).addr

    x_res = A("x_res", [128, NT, D], F32)
    fmA = A("fmA", [128, DC, TE], BF16)
    fmB = A("fmB", [128, 22, 1024], BF16)
    wslA = A("wslA", [128, 8192], BF16)
    wslB = A("wslB", [128, 8192], BF16)
    pool = A("pool", [128, 19040], mybir.dt.uint8)
    pb_ = addr(pool)
    fb_ = addr(fmB)
    vgrp = AT("vgrp", [128, NTE, 512], BF16, offset=fb_ + 16 * 2048)
    gstg = AT("gstg", [128, 2, D], BF16, offset=fb_ + 16 * 2048)
    rbase = AT("rbase", [128, 16], F32, offset=fb_ + 20 * 2048)
    idn = AT("idn", [128, 128], BF16, offset=fb_ + 21 * 2048)
    tabs = AT("tabs", [128, 2, TE], BF16, offset=addr(wslB) + 8704)
    consts = AT("consts", [128, 5, 128], BF16, offset=addr(wslB) + 14848)
    stg = AT("stg", [128, D], F32, offset=pb_ + 0)
    stgB = AT("stgB", [128, 2, D], BF16, offset=pb_ + 0)
    qkraw = AT("qkraw", [128, 2560], BF16, offset=pb_ + 0)
    zrow = AT("zrow", [1, 512], F32, offset=pb_ + 5120)
    stgb = AT("stgb", [128, 2, D], BF16, offset=pb_ + 8192)
    ropeA = AT("ropeA", [128, 512], F32, offset=pb_ + 8192)
    ropeB = AT("ropeB", [128, 512], F32, offset=pb_ + 10240)
    Pt = AT("Pt", [128, 2, 512], BF16, offset=pb_ + 12288)
    bcz = AT("bcz", [128, 512], F32, offset=pb_ + 14336)
    sm = pb_ + 16384
    stats = AT("stats", [128, 16], F32, offset=sm + 0)
    kmk = AT("kmk", [128, NTE], F32, offset=sm + 64)
    rt = AT("rt", [128, 8, 16], F32, offset=sm + 128)
    tokr = AT("tokr", [128, NT, 8], F32, offset=sm + 640)
    slotm = AT("slotm", [128, 40], F32, offset=sm + 896)
    icol = AT("icol", [128, 8], I32, offset=sm + 1056)
    sids = AT("sids", [128, 8], F32, offset=sm + 1088)
    onesc = AT("onesc", [128, 1], BF16, offset=sm + 1120)
    onesrf = AT("onesrf", [1, 128], F32, offset=sm + 1152)
    iot16 = AT("iot16s", [128, 16], F32, offset=sm + 1664)
    iotmB = AT("iotmBs", [128, 16], F32, offset=sm + 1728)
    iotcol = AT("iotcols", [128, 1], F32, offset=sm + 1792)
    eye8 = AT("eye8s", [8, 8], F32, offset=sm + 1824)
    triu8 = AT("triu8s", [8, 8], F32, offset=sm + 1856)
    cid8 = AT("cid8s", [1, 8], F32, offset=sm + 1888)
    cidoh = AT("cidohs", [8, 1], F32, offset=sm + 1920)
    eoh = AT("eohs", [E, 2], F32, offset=sm + 1952)
    ccols = AT("ccolss", [128, 4], F32, offset=sm + 1984)
    crow = AT("crows", [1, 4, 16], F32, offset=sm + 2016)
    cnt8 = AT("cnt8s", [8, 16], F32, offset=sm + 2272)
    cnt8T = AT("cnt8Ts", [16, 8], F32, offset=sm + 2336)
    cume = AT("cumes", [8, 16], F32, offset=sm + 2368)
    col8 = AT("col8s", [8, 1], F32, offset=sm + 2432)
    row8 = AT("row8s", [1, 8], F32, offset=sm + 2464)
    row16 = AT("row16s", [1, 16], F32, offset=sm + 2496)
    rts = AT("rtss", [128, 16], BF16, offset=sm + 2560)
    epsc = AT("epscs", [128, 1], F32, offset=sm + 2592)
    icol4 = AT("icol4", [128, 4], I32, offset=sm + 2624)

    PB = [nc.alloc_psum_tensor(f"pb{i}", [128, 512], F32) for i in range(6)]
    TBa = nc.alloc_psum_tensor("tba", [128, 512], BF16)
    TBb = nc.alloc_psum_tensor("tbb", [128, 512], BF16)

    IDC = consts[:, 0, :]
    SWP = consts[:, 1, :]
    MDIAG = consts[:, 2, :]
    MFAR = consts[:, 3, :]
    TRIU = consts[:, 4, :]

    def slab(dst, src_ap, name="lw", rel="peW"):
        X.wait(gps, rel)
        X.dma(gps, dst, src_ap, name)

    def ts(out, in0, s1, s2=None, op0=ALU.mult, op1=None, eng=None):
        e = eng or vec
        if s2 is None and op1 is None:
            e.tensor_scalar(out, in0, s1, None, op0=op0)
        else:
            e.tensor_scalar(out, in0, s1, s2, op0=op0, op1=op1)

    def tt(out, a, b, op=ALU.mult, eng=None):
        (eng or vec).tensor_tensor(out, a, b, op=op)

    def ln_tile(xt, out_bf, sq_scratch):
        mneg, ss, s0, rstd = (stats[:, i:i + 1] for i in range(4))
        vec.reduce_sum(mneg, xt, axis=AXX, negate=True)
        ts(mneg, mneg, 1.0 / D)
        X.wait(sca, "dv")
        sca.activation(out=sq_scratch, in_=xt, func=AF.Square, bias=mneg, scale=1.0,
                       accum_out=ss)
        sca.activation(out=s0, in_=ss, func=AF.Sqrt, bias=epsc[:, :], scale=1.0 / D)
        X.wait(vec, "ac")
        vec.reciprocal(rstd, s0)
        ts(out_bf, xt, mneg, rstd, op0=ALU.add, op1=ALU.mult)

    def store_x(tag="st"):
        X.wait(sync, "dv")
        for tt_ in range(NT):
            X.dma(sync, y_out[tt_ * 128:(tt_ + 1) * 128, :], x_res[:, tt_, :], tag)
        X.wait(sync, tag)

    X.dma(sync, consts[:, 0, :], cN_ap["idc"][:, :], "lc")
    X.dma(sync, consts[:, 1, :], cN_ap["swp"][:, :], "lc")
    X.dma(sync, consts[:, 2, :], cN_ap["mdiag"][:, :], "lc")
    X.dma(sync, consts[:, 3, :], cN_ap["mfar"][:, :], "lc")
    X.dma(sync, consts[:, 4, :], cN_ap["triu"][:, :], "lc")
    X.dma(sync, onesc[:, :], onescol_b[:, :], "lc")
    X.dma(sync, onesrf[:, :], onesrow_f[:, :], "lc")
    X.dma(sync, iot16[:, :], iot16_c[:, :], "lc")
    X.dma(sync, iotmB[:, :], iotmB_c[:, :], "lc")
    X.dma(sync, iotcol[:, :], iotcol_c[:, :], "lc")
    X.dma(sync, eye8[:, :], eye8_f[:, :], "lc")
    X.dma(sync, triu8[:, :], triu8_f[:, :], "lc")
    X.dma(sync, cid8[:, :], cidv[:, :], "lc")
    X.dma(sync, cidoh[:, :], cidohc[:, :], "lc")
    X.dma(sync, eoh[:, :], eohi[:, :], "lc")
    X.dma(sync, kmk[:, :], kmaski[:, :], "lc")
    X.dma(sync, epsc[:, :], eps_f[:, :], "lc")
    X.dma(gps, tabs[:, 0, :], cosI[:, :], "lcg")
    X.dma(gps, tabs[:, 1, :], sinIS[:, :], "lcg")
    X.wait(ten, "lc")
    ten.matmul(PB[0][:, 0:3], onesrf[:, :], cid8[0:1, 0:3], start=True, stop=True)
    X.tick(ten, "pe")
    X.wait(vec, "pe")
    ts(ccols[:, 0:3], PB[0][:, 0:3], 1024.0)
    X.tick(vec, "dv")

    for te in range(NTE):
        own = te >= 4
        xt = x_res[:, te - 4, :] if own else stg[:, :]
        X.dma(sync, xt, xh[te * 128:(te + 1) * 128, :], "lx")
        X.wait(vec, "lx")
        X.wait(sca, "lx")
        X.wait(vec, "tr")
        X.wait(sca, "tr")
        xnst = stgb[:, te % 2, :]
        ln_tile(xt, xnst, stgb[:, (te + 1) % 2, :])
        X.wait(sync, "dv")
        for dch in range(DC):
            X.dmat(fmA[:, dch, te * 128:(te + 1) * 128],
                   xnst[:, dch * 128:(dch + 1) * 128], "tr")
        X.wait(sync, "tr")

    if "dbg_xnT" in dbg_out:
        X.wait(sync, "tr")
        X.wait(sync, "dv")
        X.wait(vec, "tr")
        for dch in range(DC):
            vec.tensor_copy(stg[:, 0:TE], fmA[:, dch, :])
            X.tick(vec, "dv")
            X.wait(sync, "dv")
            X.dma(sync, dbg_out["dbg_xnT"][dch * 128:(dch + 1) * 128, :], stg[:, 0:TE], "st")
            X.wait(vec, "st")

    if stage == "p1":
        store_x()
        return nc

    aout = fmB
    X.wait(ten, "tr")
    X.wait(ten, "lc")
    X.wait(vec, "lcg")
    X.wait(vec, "lc")
    X.wait(sca, "lc")
    wv_sb = wslA[:, 0:8192].rearrange("p (a b) -> p a b", a=DC)
    wq_sb = wslB[:, 0:2048].rearrange("p (a b) -> p a b", a=DC)
    wk_sb = wslB[:, 2048:4096].rearrange("p (a b) -> p a b", a=DC)
    X.dma(gps, wv_sb[:, :, :], wvT[0, :, :, :], "lwv")
    X.dma(gps, wq_sb[:, :, :], wqT[0, :, :, :], "lwq")
    X.dma(gps, wk_sb[:, :, :], wkT[0, :, :, :], "lwq")
    for hg in range(4):
        X.wait(ten, "lwv")
        dvmarks = []
        for te in range(NTE):
            ps = PB[te % 2]
            if te >= 2:
                X.wait_val(ten, "dv", dvmarks[te - 2])
            for dch in range(DC):
                ten.matmul(ps[:, :], fmA[:, dch, te * 128:(te + 1) * 128],
                           wv_sb[:, dch, :], start=(dch == 0), stop=(dch == DC - 1))
            X.tick(ten, "pe")
            X.wait(vec, "pe")
            vec.tensor_copy(vgrp[:, te, :], ps[:, :])
            dvmarks.append(X.c["dv"])
        if hg < 3:
            X.wait_val(gps, "pe", X.c["pe"])
            X.dma(gps, wv_sb[:, :, :], wvT[hg + 1, :, :, :], "lwv")
        X.wait(ten, "dv")
        if stage == "p2v" and hg == 0:
            store_x()
            return nc
        for hh in range(4):
            h = hg * 4 + hh
            X.wait(ten, "lwq")
            X.wait(ten, "dv")
            X.wait(ten, "ac")
            for tg in range(2):
                for dch in range(DC):
                    ten.matmul(PB[tg][:, :], wq_sb[:, dch, :],
                               fmA[:, dch, 512 + tg * 512:1024 + tg * 512],
                               start=(dch == 0), stop=(dch == DC - 1))
            for sg in range(3):
                for dch in range(DC):
                    ten.matmul(PB[2 + sg][:, :], wk_sb[:, dch, :],
                               fmA[:, dch, sg * 512:(sg + 1) * 512],
                               start=(dch == 0), stop=(dch == DC - 1))
            X.tick(ten, "pe")
            if h < 15:
                X.wait_val(gps, "pe", X.c["pe"])
                X.dma(gps, wq_sb[:, :, :], wqT[h + 1, :, :, :], "lwq")
                X.dma(gps, wk_sb[:, :, :], wkT[h + 1, :, :, :], "lwq")
            X.wait(vec, "pe")
            for tg in range(2):
                vec.tensor_copy(qkraw[:, tg * 512:(tg + 1) * 512], PB[tg][:, :])
            for sg in range(3):
                vec.tensor_copy(qkraw[:, 1024 + sg * 512:1024 + (sg + 1) * 512],
                                PB[2 + sg][:, :])
            chunks = [(512 + c * 512, qkraw[:, c * 512:(c + 1) * 512]) for c in range(2)]                 + [(c * 512, qkraw[:, 1024 + c * 512:1024 + (c + 1) * 512]) for c in range(3)]
            pemarks = []
            dvm = []
            X.wait(ten, "dv")
            for ch, (lo, src) in enumerate(chunks):
                ps = PB[4 + ch % 2]
                if ch >= 2:
                    X.wait_val(ten, "dv", dvm[ch - 2])
                ten.matmul(ps[:, :], SWP, src, start=True, stop=True)
                X.tick(ten, "pe")
                pemarks.append(X.c["pe"])
                X.wait_val(vec, "pe", pemarks[ch])
                tt(ropeA[:, :], src, tabs[:, 0, lo:lo + 512], ALU.mult)
                tt(ropeB[:, :], ps[:, :], tabs[:, 1, lo:lo + 512], ALU.mult)
                tt(src, ropeA[:, :], ropeB[:, :], ALU.add)
                dvm.append(X.c["dv"])
            if stage == "p2q" and h == 0:
                store_x()
                return nc
            X.wait(ten, "dv")
            for g in range(2):
                pvps, zps, bcps = PB[2], PB[3], PB[4]
                rorder = [3, 4, 0, 1, 2, 5, 6, 7]
                geom = []
                for r in rorder:
                    s0, s1 = max(0, r - 4), min(3, r) + 1
                    geom.append((4 * g + r, s0 * 128, (min(3, r) + 1) * 128, r))
                pemarks = []
                acmarks = []
                for ri in range(len(rorder) + 1):
                    if ri < len(rorder):
                        jj, lo, hi, r = geom[ri]
                        st_ = PB[ri % 2]
                        has_far, has_diag = r <= 3, r >= 4
                        ten.matmul(st_[:, lo:hi],
                                   qkraw[:, 1024 + jj * 128:1024 + (jj + 1) * 128],
                                   qkraw[:, g * 512 + lo:g * 512 + hi],
                                   start=True, stop=not (has_far or has_diag))
                        if has_far:
                            ten.matmul(st_[:, r * 128:(r + 1) * 128], IDC, MFAR,
                                       start=False, stop=not has_diag)
                        if has_diag:
                            ten.matmul(st_[:, (r - 4) * 128:(r - 3) * 128], IDC, MDIAG,
                                       start=False, stop=True)
                        X.tick(ten, "pe")
                        pemarks.append(X.c["pe"])
                        X.wait_val(sca, "pe", pemarks[ri])
                        sca.activation(out=Pt[:, ri % 2, lo:hi], in_=st_[:, lo:hi],
                                       func=AF.Exp, bias=kmk[:, jj:jj + 1], scale=SCALE)
                        acmarks.append(X.c["ac"])
                    if ri >= 1:
                        pj = ri - 1
                        jj, lo, hi, r = geom[pj]
                        X.wait_val(ten, "ac", acmarks[pj])
                        last = pj == len(rorder) - 1
                        ten.matmul(zps[0:1, lo:hi], onesc[:, :], Pt[:, pj % 2, lo:hi],
                                   start=(pj == 0), stop=last)
                        ten.matmul(pvps[:, lo:hi],
                                   vgrp[:, jj, hh * 128:(hh + 1) * 128],
                                   Pt[:, pj % 2, lo:hi], start=(pj == 0), stop=last)
                X.tick(ten, "pe")
                X.wait(vec, "pe")
                vec.tensor_copy(zrow[0:1, :], zps[0:1, :])
                vec.reciprocal(zrow[0:1, :], zrow[0:1, :])
                X.tick(vec, "dv")
                X.wait(ten, "dv")
                ten.matmul(bcps[:, :], onesrf[:, :], zrow[0:1, :], start=True, stop=True)
                X.tick(ten, "pe")
                X.wait(vec, "pe")
                vec.tensor_copy(bcz[:, :], bcps[:, :])
                tt(aout[:, h, g * 512:(g + 1) * 512], pvps[:, :], bcz[:, :], ALU.mult)
                X.tick(vec, "dv")
                X.wait(ten, "dv")

    if stage == "p2a":
        store_x()
        return nc
    if "dbg_aout" in dbg_out:
        for dch in range(DC):
            vec.tensor_copy(stg[:, 0:1024], aout[:, dch, :])
            X.tick(vec, "dv")
            X.wait(sync, "dv")
            X.dma(sync, dbg_out["dbg_aout"][dch * 128:(dch + 1) * 128, :],
                  stg[:, 0:1024], "st")
        X.wait(vec, "st")

    wo_a = wslA[:, 0:8192].rearrange("p (a b) -> p a b", a=DC)
    wo_b = wslB[:, 0:8192].rearrange("p (a b) -> p a b", a=DC)
    X.wait(gps, "pe")
    X.wait(gps, "dv")
    X.dma(gps, wo_a[:, :, :], woT[0, :, :, :], "lwoA")
    X.dma(gps, wo_b[:, :, :], woT[1, :, :, :], "lwoB")
    lwo_need = [("lwoA", 16), ("lwoB", 16), None, None]
    for ocg in range(4):
        wo_sb = wo_a if ocg % 2 == 0 else wo_b
        X.wait_val(ten, *lwo_need[ocg])
        X.wait(ten, "dv")
        dvmarks = []
        for tt_ in range(NT):
            ps = PB[tt_ % 2]
            if tt_ >= 2:
                X.wait_val(ten, "dv", dvmarks[tt_ - 2])
            for dch in range(DC):
                ten.matmul(ps[:, :], aout[:, dch, tt_ * 128:(tt_ + 1) * 128],
                           wo_sb[:, dch, :], start=(dch == 0), stop=(dch == DC - 1))
            X.tick(ten, "pe")
            if tt_ == NT - 1 and ocg + 2 < 4:
                nm2 = "lwoA" if ocg % 2 == 0 else "lwoB"
                X.wait_val(gps, "pe", X.c["pe"])
                X.dma(gps, wo_sb[:, :, :], woT[ocg + 2, :, :, :], nm2)
                lwo_need[ocg + 2] = (nm2, X.c[nm2])
            X.wait(vec, "pe")
            xs = x_res[:, tt_, ocg * 512:(ocg + 1) * 512]
            tt(xs, xs, ps[:, :], ALU.add)
            dvmarks.append(X.c["dv"])
        X.wait(ten, "dv")

    if stage == "attn":
        store_x()
        return nc

    for tt_ in range(NT):
        X.wait(vec, "dv")
        X.wait(vec, "sx")
        X.wait(vec, "tr2")
        X.wait(sca, "sx")
        X.wait(sca, "tr2")
        xnst = stgb[:, tt_ % 2, :]
        ln_tile(x_res[:, tt_, :], xnst, stgb[:, (tt_ + 1) % 2, :])
        X.wait(sync, "dv")
        X.dma(sync, xn2_hbm[tt_ * 128:(tt_ + 1) * 128, :], xnst, "sx")
        for dch in range(DC):
            X.dmat(fmA[:, dch, tt_ * 128:(tt_ + 1) * 128],
                   xnst[:, dch * 128:(dch + 1) * 128], "tr2")


    gt_sb = wslB[:, 4096:4352].rearrange("p (a b) -> p a b", a=DC)
    X.wait(gps, "pe")
    X.dma(gps, gt_sb[:, :, :], gateT[:, :, :], "lw")
    X.dma(gps, consts[:, 4, :], cN_ap["triu"][:, :], "lcg")
    vec.memset(stg[:, 0:128], 0.0)
    vec.memset(crow[0:1, 0, :], 0.0)
    X.tick(vec, "dv")
    X.wait(sync, "dv")
    X.dma(sync, lists_pay[0:1, 0:16384].rearrange("a (p f) -> (a p) f", p=128),
          stg[:, 0:128], "sl")
    X.wait(ten, "lw")
    X.wait(ten, "lcg")
    X.wait(ten, "tr2")
    X.wait(ten, "dv")
    X.wait(gps, "lc")
    X.wait(vec, "lc")
    for tt_ in range(NT):
        X.wait(vec, "slg")
        X.wait(ten, "dv")
        lps = PB[2][:, 0:16]
        for dch in range(DC):
            ten.matmul(lps, fmA[:, dch, tt_ * 128:(tt_ + 1) * 128],
                       gt_sb[:, dch, :], start=(dch == 0), stop=(dch == DC - 1))
        X.tick(ten, "pe")
        X.wait(vec, "pe")
        lg = rt[:, 0, :]
        vec.tensor_copy(lg, lps)
        m1, idx1, m2, idx2, w1c, w2c, p1c, p2c = (tokr[:, tt_, i:i + 1] for i in range(8))
        sc1, sc2 = stats[:, 8:9], stats[:, 9:10]
        vec.reduce_max(m1, lg, axis=AXX)
        E1, E2, scr = rt[:, 1, :], rt[:, 2, :], rt[:, 3, :]
        ts(scr, lg, m1, None, op0=ALU.is_ge)
        tt(scr, scr, iotmB[:, :], ALU.mult)
        ts(scr, scr, 4e4, op0=ALU.add)
        vec.tensor_reduce(idx1, scr, axis=AXX, op=ALU.min)
        ts(E1, iot16[:, :], idx1, None, op0=ALU.is_equal)
        ts(scr, E1, 1e6)
        tt(scr, lg, scr, ALU.subtract)
        vec.reduce_max(m2, scr, axis=AXX)
        ts(rt[:, 4, :], scr, m2, None, op0=ALU.is_ge)
        tt(rt[:, 4, :], rt[:, 4, :], iotmB[:, :], ALU.mult)
        ts(rt[:, 4, :], rt[:, 4, :], 4e4, op0=ALU.add)
        vec.tensor_reduce(idx2, rt[:, 4, :], axis=AXX, op=ALU.min)
        ts(E2, iot16[:, :], idx2, None, op0=ALU.is_equal)
        ts(sc1, m1, -1.0)
        X.tick(vec, "dv")
        X.wait(sca, "dv")
        sca.activation(out=rt[:, 4, :], in_=lg, func=AF.Exp, bias=sc1, scale=1.0,
                       accum_out=sc2)
        X.tick(sca, "ac")
        X.wait(vec, "ac")
        vec.reciprocal(w1c, sc2)
        tt(sc1, m2, m1, ALU.subtract)
        X.tick(vec, "dv")
        X.wait(sca, "dv")
        sca.activation(out=sc2, in_=sc1, func=AF.Exp)
        X.tick(sca, "ac")
        X.wait(vec, "ac")
        tt(w2c, sc2, w1c, ALU.mult)
        tt(rts[:, :], E1, E2, ALU.add)
        X.tick(vec, "dv")
        X.wait(ten, "dv")
        cps = PB[3][:, 0:16]
        ten.matmul(cps, TRIU, rts[:, :], start=True, stop=False)
        ten.matmul(cps, onesrf[:, :], crow[0:1, 0, :], start=False, stop=True)
        ten.matmul(PB[4][0:1, 0:16], onesc[:, :], rts[:, :], start=True, stop=True)
        X.tick(ten, "pe")
        X.wait(vec, "pe")
        Cc = rt[:, 5, :]
        vec.tensor_copy(Cc, cps)
        vec.tensor_copy(crow[0:1, 1, :], PB[4][0:1, 0:16])
        vec.tensor_tensor(crow[0:1, 0, :], crow[0:1, 0, :], crow[0:1, 1, :],
                          op=ALU.add)
        tt(rt[:, 6, :], Cc, E1, ALU.mult)
        vec.reduce_sum(p1c, rt[:, 6, :], axis=AXX)
        tt(rt[:, 6, :], Cc, E2, ALU.mult)
        vec.reduce_sum(p2c, rt[:, 6, :], axis=AXX)
        gidc = stats[:, 10:11]
        ts(gidc, ccols[:, 0:1], 1.0, float(128 * tt_), op0=ALU.mult, op1=ALU.add)
        tt(gidc, gidc, iotcol[:, :], ALU.add)
        for kk, (idxc, posc) in enumerate(((idx1, p1c), (idx2, p2c))):
            sp = stats[:, 11 + kk:12 + kk]
            ts(sp, idxc, 1024.0)
            tt(sp, sp, posc, ALU.add)
            ovg = stats[:, 13 + kk:14 + kk]
            ts(ovg, posc, 1024.0, None, op0=ALU.is_ge)
            ts(ovg, ovg, GUARD)
            tt(sp, sp, ovg, ALU.add)
            vec.tensor_copy(icol[:, kk:kk + 1], sp)
        X.tick(vec, "dv")
        X.wait(gps, "dv")
        X.wait(gps, "sl")
        for kk in range(2):
            X.inc(gps.indirect_dma_start(
                out=lists_pay[:, :], out_offset=IOA(ap=icol[:, kk:kk + 1], axis=1),
                in_=gidc, in_offset=None,
                bounds_check=16383, oob_is_err=False), "slg", 16)
        if "dbg_route" in dbg_out:
            vec.tensor_copy(stg[:, 0:8], tokr[:, tt_, :])
            X.tick(vec, "dv")
            X.wait(sync, "dv")
            X.dma(sync, dbg_out["dbg_route"][tt_ * 128:(tt_ + 1) * 128, 0:8],
                  stg[:, 0:8], "st")
    X.wait(vec, "pe")
    X.wait(sync, "dv")
    X.dma(sync, lists_pay[0:1, 16384:16400], crow[0:1, 0, :], "sl")
    if "dbg_lists" in dbg_out:
        X.wait(sync, "sl")
        X.wait(vec, "sl")
        vec.memset(stats[:, 15:16], 0.0)
        X.dma(sync, dbg_out["dbg_lists"][:, :], lists_pay[:, :], "st")

    X.wait(gps, "sl")
    X.wait(gps, "slg")
    X.wait(gps, "sx")
    X.inc(gps.collective_compute(
        "AllGather", ALU.bypass, ins=[lists_pay[:, :]], outs=[lists_all[:, :]],
        replica_groups=[list(range(8))]), "cc1", 1)
    X.inc(gps.collective_compute(
        "AllGather", ALU.bypass, ins=[xn2_hbm[:, :]], outs=[xnall[:, :]],
        replica_groups=[list(range(8))]), "cc2", 1)

    wsv = [(wslA[:, 0:2048].rearrange("p (a b) -> p a b", a=DC),
            wslA[:, 2048:4096].rearrange("p (a b) -> p a b", a=DC)),
           (wslA[:, 4096:6144].rearrange("p (a b) -> p a b", a=DC),
            wslA[:, 6144:8192].rearrange("p (a b) -> p a b", a=DC))]
    X.wait(gps, "pe")
    X.dma(gps, wsv[0][0][:, :, :], wsg[0, :, :, :], "lwshA")
    X.dma(gps, wsv[0][1][:, :, :], wsu[0, :, :, :], "lwshA")
    lwsh_need = [("lwshA", 32)]
    pewm = [0] * ISC
    for is_ in range(ISC):
        wsgv, wsuv = wsv[is_ % 2]
        X.wait_val(ten, *lwsh_need[is_])
        X.wait(ten, "ac")
        X.wait(ten, "dv")
        for tg in range(2):
            for dch in range(DC):
                ten.matmul(PB[0 + tg][:, :], wsgv[:, dch, :],
                           fmA[:, dch, tg * 512:(tg + 1) * 512],
                           start=(dch == 0), stop=(dch == DC - 1))
                ten.matmul(PB[2 + tg][:, :], wsuv[:, dch, :],
                           fmA[:, dch, tg * 512:(tg + 1) * 512],
                           start=(dch == 0), stop=(dch == DC - 1))
        X.tick(ten, "pe")
        pewm[is_] = X.c["pe"]
        if is_ + 1 < ISC:
            j = is_ + 1
            if j >= 2:
                X.wait_val(gps, "pe", pewm[j - 2])
            nm2 = "lwshA" if j % 2 == 0 else "lwshB"
            X.dma(gps, wsv[j % 2][0][:, :, :], wsg[j, :, :, :], nm2)
            X.dma(gps, wsv[j % 2][1][:, :, :], wsu[j, :, :, :], nm2)
            lwsh_need.append((nm2, X.c[nm2]))
        X.wait(sca, "pe")
        X.wait(vec, "pe")
        if "dbg_g0" in dbg_out and is_ == 0:
            vec.tensor_copy(stg[:, 0:512], PB[0][:, :])
            vec.tensor_copy(stg[:, 512:1024], PB[1][:, :])
            X.wait(sync, "dv")
            X.dma(sync, dbg_out["dbg_g0"][:, :], stg[:, 0:1024], "st")
            X.wait(vec, "st")
        if "dbg_slot" in dbg_out and is_ == 0:
            vec.tensor_copy(stg[:, 0:16], wsgv[:, 0, 0:16])
            vec.tensor_copy(stg[:, 16:32], wsgv[:, 1, 0:16])
            X.wait(sync, "dv")
            X.dma(sync, dbg_out["dbg_slot"][:, :], stg[:, 0:32], "st")
            X.wait(vec, "st")
        for tg in range(2):
            sca.activation(out=stgb[:, 0, tg * 512:(tg + 1) * 512], in_=PB[tg][:, :],
                           func=AF.Silu)
            vec.tensor_copy(stgb[:, 1, tg * 512:(tg + 1) * 512], PB[2 + tg][:, :])
        X.tick(sca, "ac")
        X.wait(vec, "ac")
        tt(fmB[:, is_, :], stgb[:, 0, 0:1024], stgb[:, 1, 0:1024], ALU.mult)
        X.tick(vec, "dv")
        if stage.startswith("p7a") and is_ == int(stage[3:] or 0):
            vec.tensor_copy(stg[:, 0:1024], fmB[:, 0, :])
            vec.tensor_copy(stg[:, 1024:2048], fmB[:, min(is_, 1), :])
            X.wait(sync, "dv")
            for tt_ in range(NT):
                X.dma(sync, y_out[tt_ * 128:(tt_ + 1) * 128, :],
                      stg[:, :] if tt_ == 0 else x_res[:, tt_, :], "st")
            X.wait(sync, "st")
            return nc
    if "dbg_xn2T" in dbg_out:
        for dch in range(DC):
            vec.tensor_copy(stg[:, 0:1024], fmA[:, dch, 0:1024])
            X.wait(sync, "dv")
            X.dma(sync, dbg_out["dbg_xn2T"][dch * 128:(dch + 1) * 128, :],
                  stg[:, 0:1024], "st")
            X.wait(vec, "st")
    if "dbg_gu" in dbg_out:
        vec.tensor_copy(stg[:, 0:1024], fmB[:, 0, :])
        X.wait(sync, "dv")
        X.dma(sync, dbg_out["dbg_gu"][:, :], stg[:, 0:1024], "st")
        X.wait(vec, "st")
    if "dbg_gu_all" in dbg_out:
        for ch in range(22):
            X.wait(vec, "dv")
            vec.tensor_copy(stg[:, 0:1024], fmB[:, ch, :])
            X.wait(sync, "dv")
            X.dma(sync, dbg_out["dbg_gu_all"][ch * 128:(ch + 1) * 128, :],
                  stg[:, 0:1024], "st")
            X.wait(vec, "st")
    if stage == "full":
        X.wait(sync, "cc1")
        X.dma(sync, cnt8[:, :], lists_all[:, 16384:16400], "lg")
        X.wait(ten, "lg")
        X.wait(ten, "ac")
        X.wait(ten, "dv")
        ten.transpose(PB[0][0:16, 0:8], cnt8[:, :], eye8[:, :])
        ten.matmul(PB[1][0:8, 0:16], triu8[:, :], cnt8[:, :], start=True, stop=True)
        X.tick(ten, "pe")
        X.wait(vec, "pe")
        vec.tensor_copy(cnt8T[:, :], PB[0][0:16, 0:8])
        vec.tensor_copy(cume[:, :], PB[1][0:8, 0:16])
        X.tick(vec, "dv")
        X.wait(ten, "dv")
        ten.matmul(PB[2][0:1, 0:16], cidoh[:, :], cume[:, :], start=True, stop=True)
        X.tick(ten, "pe")
        X.wait(vec, "pe")
        vec.tensor_copy(row16[:, :], PB[2][0:1, 0:16])
        X.tick(vec, "dv")
        X.wait(ten, "dv")
        ten.matmul(PB[2][:, 0:16], onesrf[:, :], row16[:, :], start=True, stop=True)
        X.tick(ten, "pe")
        X.wait(vec, "pe")
        vec.tensor_copy(rt[:, 7, :], PB[2][:, 0:16])
        X.tick(vec, "dv")
        gps.iota(slotm[:, 8:16], pattern=[[128, 8]], base=0, channel_multiplier=1,
                 allow_small_or_imprecise_dtypes=True)
        X.tick(gps, "gp")

        def boundary_math(o):
            X.wait(ten, "dv")
            ten.matmul(PB[3][0:8, 0:1], cnt8T[:, :], eoh[:, o:o + 1], start=True,
                       stop=True)
            X.tick(ten, "pe")
            X.wait(vec, "pe")
            X.wait(vec, "gp")
            vec.tensor_copy(col8[:, :], PB[3][0:8, 0:1])
            X.tick(vec, "dv")
            X.wait(ten, "dv")
            ten.matmul(PB[3][0:1, 0:8], col8[:, :], triu8[:, :], start=True, stop=True)
            X.tick(ten, "pe")
            X.wait(vec, "pe")
            vec.tensor_copy(row8[:, :], PB[3][0:1, 0:8])
            X.tick(vec, "dv")
            X.wait(ten, "dv")
            ten.matmul(PB[3][:, 0:8], onesrf[:, :], row8[:, :], start=True, stop=True)
            X.tick(ten, "pe")
            X.wait(vec, "pe")
            bnd = slotm[:, 0:8]
            vec.tensor_copy(bnd, PB[3][:, 0:8])
            sio = slotm[:, 8:16]
            cidx = slotm[:, 16:24]
            base = slotm[:, 24:32]
            vec.memset(cidx, 0.0)
            vec.memset(base, 0.0)
            for c in range(1, 8):
                ge = slotm[:, 32:40]
                ts(ge, sio, bnd[:, c:c + 1], None, op0=ALU.is_ge)
                tt(cidx, cidx, ge, ALU.add)
                ts(ge, ge, bnd[:, c:c + 1], None, op0=ALU.mult)
                tt(base, base, ge, ALU.max)
            fl = slotm[:, 32:40]
            ts(fl, cidx, float(LISTW))
            tt(fl, fl, sio, ALU.add)
            tt(fl, fl, base, ALU.subtract)
            ts(fl, fl, ccols[:, 1 + o:2 + o], None, op0=ALU.add)
            vec.tensor_copy(icol[:, 0:8], fl)
            X.tick(vec, "dv")
            return X.c["dv"]

        def sids_gather(dvmark):
            X.wait_val(gps, "dv", dvmark)
            for col in range(8):
                X.inc(gps.indirect_dma_start(
                    out=sids[:, col:col + 1], out_offset=None,
                    in_=lists_all.rearrange("a b -> (a b)")[None, :],
                    in_offset=IOA(ap=icol[:, col:col + 1], axis=1)), "lgg", 16)
            X.wait(vec, "lgg")
            vec.tensor_copy(icol[:, 0:8], sids[:, :])
            X.tick(vec, "dv")
            return X.c["dv"]

        STGC = [stgb[:, 0, :], stgb[:, 1, :], stgB[:, 0, :], stgB[:, 1, :]]

        def stg_ap(i):
            if i < 4:
                return STGC[i]
            q = i - 4
            return fmA[:, 4 * q:4 * q + 4, 1024:1536]

        def stg_in(i, dch):
            if i < 4:
                return STGC[i][:, dch * 128:(dch + 1) * 128]
            q = i - 4
            c0 = 1024 + (dch % 4) * 128
            return fmA[:, 4 * q + dch // 4, c0:c0 + 128]

        xnflat512 = xnall.rearrange("a (c w) -> (a c) w", w=512)

        def xn_gather(col, buf, pem=None, dvm=None):
            if pem:
                X.wait_val(gps, "pe", pem)
            if dvm:
                X.wait_val(gps, "dv", dvm)
            if buf < 4:
                X.inc(gps.indirect_dma_start(
                    out=stg_ap(buf), out_offset=None, in_=xnall[:, :],
                    in_offset=IOA(ap=icol[:, col:col + 1], axis=0)), "lgg", 16)
            else:
                q = buf - 4
                X.wait(vec, "lgg")
                for a in range(4):
                    ts(rt[:, 3, a:a + 1], sids[:, col:col + 1], 4.0, float(a),
                       op0=ALU.mult, op1=ALU.add)
                vec.tensor_copy(icol4[:, :], rt[:, 3, 0:4])
                X.tick(vec, "dv")
                X.wait_val(gps, "dv", X.c["dv"])
                for a in range(4):
                    X.inc(gps.indirect_dma_start(
                        out=fmA[:, 4 * q + a, 1024:1536], out_offset=None,
                        in_=xnflat512,
                        in_offset=IOA(ap=icol4[:, a:a + 1], axis=0)), "lgg", 16)

        X.wait(gps, "cc1")
        dvm_ids0 = sids_gather(boundary_math(0))

    wdvA = wslA[:, 0:16 * 512].rearrange("p (a b) -> p a b", a=16)
    wdvB = wslB[:, 0:6 * 512].rearrange("p (a b) -> p a b", a=6)
    for ocg in range(4):
        X.wait(gps, "pe")
        X.dma(gps, wdvA[:, :, :], wsdA[ocg, :, :, :], "lw")
        X.dma(gps, wdvB[:, :, :], wsdB[ocg, :, :, :], "lw")
        if "dbg_wdv" in dbg_out and ocg == 0:
            X.wait(vec, "lw")
            for ii in range(22):
                wt = wdvA[:, ii, :] if ii < 16 else wdvB[:, ii - 16, :]
                vec.tensor_copy(stg[:, 0:512], wt)
                X.wait(sync, "dv")
                X.dma(sync, dbg_out["dbg_wdv"][:, ii * 512:(ii + 1) * 512],
                      stg[:, 0:512], "st")
                X.wait(vec, "st")
        if stage == "full":
            if ocg == 0:
                X.wait(gps, "cc2")
                X.wait(gps, "ac")
            for cb in (2 * ocg, 2 * ocg + 1):
                xn_gather(cb, cb, dvm=dvm_ids0 if cb == 0 else None)
        X.wait(ten, "lw")
        X.wait(ten, "dv")
        for tt_ in range(NT):
            ps = PB[4 + tt_ % 2]
            X.wait(ten, "dv")
            for is_ in range(ISC):
                wtile = wdvA[:, is_, :] if is_ < 16 else wdvB[:, is_ - 16, :]
                ten.matmul(ps[:, :], fmB[:, is_, tt_ * 128:(tt_ + 1) * 128],
                           wtile, start=(is_ == 0), stop=(is_ == ISC - 1))
            X.tick(ten, "pe")
            X.wait(vec, "pe")
            xs = x_res[:, tt_, ocg * 512:(ocg + 1) * 512]
            tt(xs, xs, ps[:, :], ALU.add)
            X.tick(vec, "dv")
        X.wait(ten, "dv")
    pe_downend = X.c["pe"]

    if stage == "shared":
        store_x()
        return nc

    X.wait(vec, "lgg")
    dvm_ids1 = sids_gather(boundary_math(1))
    X.wait(sync, "pe")
    X.dma(sync, idn[:, :], cN_ap["idc"][:, :], "lc2")
    X.dma(sync, rbase[:, :], rbase_f[:, :], "lc2")

    tp_dvm = []

    def transpose_col(col, buf):
        X.wait(ten, "lgg")
        X.wait_val(ten, "lc2", 32)
        for q in range(4):
            k4 = len(tp_dvm)
            TB = TBa if k4 % 2 == 0 else TBb
            if k4 >= 2:
                X.wait_val(ten, "dv", tp_dvm[k4 - 2])
            else:
                X.wait(ten, "dv")
            for k in range(4):
                ten.transpose(TB[:, k * 128:(k + 1) * 128], stg_in(buf, q * 4 + k),
                              idn[:, :])
            X.tick(ten, "pe")
            X.wait(vec, "pe")
            for k in range(4):
                vec.tensor_copy(fmA[:, q * 4 + k, col * 128:(col + 1) * 128],
                                TB[:, k * 128:(k + 1) * 128])
            tp_dvm.append(X.c["dv"])
        return X.c["pe"]

    for col in range(8):
        pe_tp0 = transpose_col(col, col)

    def expert_gu(o, pem0=None):
        wg_sb = wslA[:, 0:8192].rearrange("p (a b) -> p a b", a=DC)
        wu_sb = wslB[:, 0:8192].rearrange("p (a b) -> p a b", a=DC)
        for igrp in range(3):
            if igrp == 0 and pem0:
                X.wait_val(gps, "pe", pem0)
            else:
                X.wait(gps, "pe")
            X.dma(gps, wg_sb[:, :, :], wg[o, igrp, :, :, :], "lw")
            X.dma(gps, wu_sb[:, :, :], wu[o, igrp, :, :, :], "lw")
            X.wait(ten, "lw")
            for ic in range(4):
                X.wait(ten, "ac")
                X.wait(ten, "dv")
                for tg in range(2):
                    for dch in range(DC):
                        ten.matmul(PB[tg][:, :], wg_sb[:, dch, ic * 128:(ic + 1) * 128],
                                   fmA[:, dch, tg * 512:(tg + 1) * 512],
                                   start=(dch == 0), stop=(dch == DC - 1))
                        ten.matmul(PB[2 + tg][:, :],
                                   wu_sb[:, dch, ic * 128:(ic + 1) * 128],
                                   fmA[:, dch, tg * 512:(tg + 1) * 512],
                                   start=(dch == 0), stop=(dch == DC - 1))
                X.tick(ten, "pe")
                X.wait(sca, "pe")
                X.wait(vec, "pe")
                for tg in range(2):
                    sca.activation(out=stgb[:, 0, tg * 512:(tg + 1) * 512],
                                   in_=PB[tg][:, :], func=AF.Silu)
                    vec.tensor_copy(stgb[:, 1, tg * 512:(tg + 1) * 512], PB[2 + tg][:, :])
                X.tick(sca, "ac")
                X.wait(vec, "ac")
                tt(fmB[:, igrp * 4 + ic, 0:1024], stgb[:, 0, 0:1024], stgb[:, 1, 0:1024],
                   ALU.mult)
                X.tick(vec, "dv")
        return X.c["pe"]

    def expert_down(o):
        wdx = wslA[:, 0:12 * 512].rearrange("p (a b) -> p a b", a=12)
        for ocg in range(4):
            X.wait(gps, "pe")
            X.dma(gps, wdx[:, :, :], wd[o, ocg, :, :, :], "lw")
            X.wait(ten, "lw")
            X.wait(ten, "dv")
            for rs in range(8):
                ps = PB[4 + rs % 2]
                X.wait(ten, "dv")
                for is_ in range(12):
                    ten.matmul(ps[:, :], fmB[:, is_, rs * 128:(rs + 1) * 128],
                               wdx[:, is_, :], start=(is_ == 0), stop=(is_ == 11))
                X.tick(ten, "pe")
                X.wait(vec, "pe")
                vec.tensor_copy(stgb[:, rs % 2, 0:512], ps[:, :])
                X.tick(vec, "dv")
                X.wait(sync, "dv")
                X.dma(sync, eo_hbm[o * 1024 + rs * 128:o * 1024 + (rs + 1) * 128,
                                   ocg * 512:(ocg + 1) * 512], stgb[:, rs % 2, 0:512],
                      "se")
            X.wait(vec, "se")
        return X.c["pe"]

    expert_gu(0, pem0=pe_downend)
    for c in range(6):
        xn_gather(c, 2 + c, pem=pe_tp0, dvm=dvm_ids1 if c == 0 else None)
    pe_down0 = expert_down(0)
    X.wait(gps, "se")
    X.inc(gps.collective_compute(
        "AllGather", ALU.bypass, ins=[eo_hbm[0:1024, :]], outs=[eoall[0:8192, :]],
        replica_groups=[list(range(8))]), "cc3", 1)
    for c in range(6):
        pe_tp1 = transpose_col(c, 2 + c)
        if c == 1:
            xn_gather(6, 2, pem=pe_tp1)
            xn_gather(7, 3, pem=pe_tp1)
    for c in (6, 7):
        transpose_col(c, c - 4)
    expert_gu(1, pem0=pe_down0)
    expert_down(1)
    if "dbg_eo" in dbg_out:
        X.wait(sync, "se")
        for rr in range(16):
            X.dma(sync, gstg[:, 0, :], eo_hbm[rr * 128:(rr + 1) * 128, :], "lg")
            X.wait(vec, "lg")
            vec.tensor_copy(stg[:, :], gstg[:, 0, :])
            X.tick(vec, "dv")
            X.wait(sync, "dv")
            X.dma(sync, dbg_out["dbg_eo"][rr * 128:(rr + 1) * 128, :], stg[:, :], "st")
        X.wait(vec, "st")
    if "dbg_cnt" in dbg_out:
        X.wait(sync, "dv")
        vec.tensor_copy(stg[0:8, 0:16], cnt8[:, :])
        X.tick(vec, "dv")
        X.wait(sync, "dv")
        X.dma(sync, dbg_out["dbg_cnt"][:, :], stg[0:8, 0:16], "st")
        X.wait(vec, "st")

    X.wait(gps, "se")
    X.inc(gps.collective_compute(
        "AllGather", ALU.bypass, ins=[eo_hbm[1024:2048, :]], outs=[eoall[8192:16384, :]],
        replica_groups=[list(range(8))]), "cc3", 1)

    X.wait(gps, "cc3")
    X.wait(vec, "cc3")
    X.wait(vec, "lc2")
    for tt_ in range(NT):
        for kk in range(2):
            idxc = tokr[:, tt_, 1 + 2 * kk:2 + 2 * kk]
            posc = tokr[:, tt_, 6 + kk:7 + kk]
            ek = rt[:, 0, :]
            ts(ek, iot16[:, :], idxc, None, op0=ALU.is_equal)
            tt(ek, ek, rt[:, 7, :], ALU.mult)
            offc = stats[:, 8:9]
            vec.reduce_sum(offc, ek, axis=AXX)
            gp_ = stats[:, 9:10]
            tt(gp_, posc, offc, ALU.add)
            vec.tensor_scalar_min(gp_, gp_, 1023.0)
            ts(rt[:, 3, :], iot16[:, :], idxc, None, op0=ALU.is_equal)
            tt(rt[:, 3, :], rt[:, 3, :], rbase[:, :], ALU.mult)
            vec.reduce_sum(stats[:, 10:11], rt[:, 3, :], axis=AXX)
            tt(gp_, gp_, stats[:, 10:11], ALU.add)
            vec.tensor_copy(icol[:, kk:kk + 1], gp_)
            X.tick(vec, "dv")
            X.wait(gps, "dv")
            X.inc(gps.indirect_dma_start(
                out=gstg[:, kk, :], out_offset=None,
                in_=eoall[:, :],
                in_offset=IOA(ap=icol[:, kk:kk + 1], axis=0)), "lgg", 16)
        X.wait(vec, "lgg")
        for kk in range(2):
            wc = tokr[:, tt_, 4 + kk:5 + kk]
            vec.tensor_copy(stg[:, :], gstg[:, kk, :])
            vec.scalar_tensor_tensor(x_res[:, tt_, :], stg[:, :], wc,
                                     x_res[:, tt_, :], op0=ALU.mult, op1=ALU.add)
        X.tick(vec, "dv")
    store_x()
    return nc


def rope_tabs_for_core(c):
    s0 = (c % 2) * 1024
    pos = np.maximum(np.arange(TE) + s0 - 512, 0).astype(np.float32)
    inv = (1.0 / (10000.0 ** (np.arange(0, DH, 2, dtype=np.float32) / np.float32(DH)))).astype(np.float32)
    fr = (pos[None, :] * inv[:, None]).astype(np.float32)
    cos, sin = np.cos(fr).astype(np.float32), np.sin(fr).astype(np.float32)
    cosI = np.empty((128, TE), np.float32)
    sinIS = np.empty((128, TE), np.float32)
    cosI[0::2] = cos
    cosI[1::2] = cos
    sinIS[0::2] = -sin
    sinIS[1::2] = sin
    return cosI, sinIS


def prep_inputs(L0_attn_x, L0_attn_Wq, L0_attn_Wk, L0_attn_Wv, L0_attn_Wo,
                L0_ffn_gate_w, L0_ffn_w_gate, L0_ffn_w_up, L0_ffn_w_down,
                L0_ffn_ws_gate, L0_ffn_ws_up, L0_ffn_ws_down):
    f = np.ascontiguousarray

    def pmaj(w, width):
        K = w.shape[0]
        return w.reshape(K // 128, 128, width).transpose(1, 0, 2)

    WqT = np.stack([pmaj(L0_attn_Wq.T[:, h * DH:(h + 1) * DH], DH) for h in range(NH)])
    WkT = np.stack([pmaj(L0_attn_Wk.T[:, h * DH:(h + 1) * DH], DH) for h in range(NH)])
    WvT = np.stack([pmaj(L0_attn_Wv.T[:, g * 512:(g + 1) * 512], 512) for g in range(4)])
    WoT = np.stack([pmaj(L0_attn_Wo.T[:, g * 512:(g + 1) * 512], 512) for g in range(4)])
    gT = pmaj(f(L0_ffn_gate_w.T), E)
    wsg = np.stack([pmaj(L0_ffn_ws_gate[:, i * 128:(i + 1) * 128], 128) for i in range(ISC)])
    wsu = np.stack([pmaj(L0_ffn_ws_up[:, i * 128:(i + 1) * 128], 128) for i in range(ISC)])
    wsdA = np.stack([pmaj(L0_ffn_ws_down[0:2048, g * 512:(g + 1) * 512], 512) for g in range(4)])
    wsdB = np.stack([pmaj(L0_ffn_ws_down[2048:ISH, g * 512:(g + 1) * 512], 512) for g in range(4)])
    wgp = np.zeros((E, D, IPAD), np.float32)
    wgp[:, :, :1408] = L0_ffn_w_gate
    wup = np.zeros((E, D, IPAD), np.float32)
    wup[:, :, :1408] = L0_ffn_w_up
    wdp = np.zeros((E, IPAD, D), np.float32)
    wdp[:, :1408, :] = L0_ffn_w_down
    wg_s = np.stack([np.stack([pmaj(wgp[e][:, g * 512:(g + 1) * 512], 512) for g in range(3)])
                     for e in range(E)])
    wu_s = np.stack([np.stack([pmaj(wup[e][:, g * 512:(g + 1) * 512], 512) for g in range(3)])
                     for e in range(E)])
    wd_s = np.stack([np.stack([pmaj(wdp[e][:, g * 512:(g + 1) * 512], 512) for g in range(4)])
                     for e in range(E)])

    in_maps = []
    for c in range(8):
        b, hf = c // 2, c % 2
        own = L0_attn_x[b, hf * 1024:(hf + 1) * 1024, :]
        if hf == 0:
            halo = np.zeros((HAL, D), np.float32)
        else:
            halo = L0_attn_x[b, 512:1024, :]
        xh = np.concatenate([halo, own], 0).astype(np.float32)
        cosI, sinIS = rope_tabs_for_core(c)
        km = np.zeros((128, NTE), np.float32)
        if hf == 0:
            km[:, 0:4] = BIGNEG
        cidv = np.zeros((1, 8), np.float32)
        cidv[0, 0], cidv[0, 1], cidv[0, 2] = c, 2 * c, 2 * c + 1
        cidohc = np.zeros((8, 1), np.float32)
        cidohc[c, 0] = 1
        eoh = np.zeros((E, 2), np.float32)
        eoh[2 * c, 0] = 1
        eoh[2 * c + 1, 1] = 1
        in_maps.append(dict(
            xh=xh, wqT=f(WqT), wkT=f(WkT), wvT=f(WvT), woT=f(WoT), gateT=f(gT),
            wsg=f(wsg), wsu=f(wsu), wsdA=f(wsdA), wsdB=f(wsdB),
            wg=f(wg_s[2 * c:2 * c + 2]), wu=f(wu_s[2 * c:2 * c + 2]),
            wd=f(wd_s[2 * c:2 * c + 2]),
            cosI=cosI, sinIS=sinIS, kmask=km, cidv=cidv, cidohc=cidohc, eoh=eoh))
    return in_maps
_NC_CACHE = {}


def kernel(**inputs):
    from concourse.bass_utils import run_bass_kernel_spmd
    in_maps = prep_inputs(**inputs)
    if "nc" not in _NC_CACHE:
        _NC_CACHE["nc"] = build_nc(stage="full")
    nc = _NC_CACHE["nc"]
    res = run_bass_kernel_spmd(nc, in_maps, core_ids=list(range(8)))
    x = inputs["L0_attn_x"]
    out = np.zeros((4, 2048, 2048), np.float32)
    for c in range(8):
        b, hf = c // 2, c % 2
        out[b, hf * 1024:(hf + 1) * 1024, :] = res.results[c]["y"]
    return out


# version 05
# speedup vs baseline: 1.2626x; 1.0216x over previous
import numpy as np
import ml_dtypes
import concourse.bass as bass
import concourse.mybir as mybir
from concourse.bass_core_rust import IndirectOffsetOnAxis as IOA

F32 = mybir.dt.float32
BF16 = mybir.dt.bfloat16
I32 = mybir.dt.int32
AF = mybir.ActivationFunctionType
ALU = mybir.AluOpType
AXX = mybir.AxisListType.X
BF = ml_dtypes.bfloat16

D, DC, NH, DH = 2048, 16, 16, 128
T, NT, HAL, TE, NTE = 1024, 8, 512, 1536, 12
SCALE = float(np.float32(DH) ** -0.5)
BIGNEG = -1e9
E, CAP = 16, 1024
ISH, ISC = 2816, 22
IPAD = 1536
GUARD = float(2 ** 20)
LISTW = 16400


class EngProxy:

    CHAIN = {"tensor_scalar", "tensor_tensor", "tensor_copy", "tensor_reduce",
             "reduce_max", "reduce_sum", "reciprocal", "memset", "activation",
             "iota", "scalar_tensor_tensor", "tensor_scalar_min",
             "tensor_scalar_add", "tensor_scalar_mul", "tensor_scalar_max",
             "tensor_single_scalar", "affine_select", "bn_stats", "bn_aggr",
             "tensor_add", "tensor_sub", "tensor_mul", "tensor_max",
             "partition_broadcast", "memzero"}

    def __init__(self, eng, X=None, chname=None):
        self._eng = eng
        self.last = None
        self._X = X
        self._chname = chname
        if X is not None and chname is not None:
            X.sem(chname)

    def __getattr__(self, name):
        f = getattr(self._eng, name)
        if not callable(f):
            return f
        chain = (name in EngProxy.CHAIN) and self._chname is not None

        def g(*a, **k):
            if chain and self._X.c[self._chname] > 0:
                self._eng.wait_ge(self._X.sem(self._chname), self._X.c[self._chname])
            r = f(*a, **k)
            if hasattr(r, "then_inc"):
                self.last = r
                if chain:
                    self._X.inc(r, self._chname, 1)
            return r
        return g


class Ctx:
    def __init__(self, nc):
        self.nc = nc
        self.c = {}
        self.h = {}

    def sem(self, name):
        if name not in self.h:
            self.h[name] = self.nc.alloc_semaphore(name)
            self.c[name] = 0
        return self.h[name]

    def inc(self, inst, name, n=1):
        inst.then_inc(self.sem(name), n)
        self.c[name] += n

    def tick(self, eng, name):
        if getattr(eng, "_chname", None) == name:
            return
        self.inc(eng.last, name, 1)

    def dma(self, eng, out, in_, name):
        self.inc(eng.dma_start(out=out, in_=in_), name, 16)

    def dmat(self, out, in_, name):
        self.inc(self.nc.sync.dma_start(out=out, in_=in_, transpose=True), name, 16)

    def wait(self, eng, name):
        eng.wait_ge(self.sem(name), self.c[name])

    def wait_val(self, eng, name, val):
        if val > 0:
            eng.wait_ge(self.sem(name), val)


def build_nc(stage="full", dbg=()):
    nc = bass.Bass("TRN2", target_bir_lowering=False, num_devices=8,
                   detect_race_conditions=False)
    X = Ctx(nc)
    sync = EngProxy(nc.sync)
    gps = EngProxy(nc.gpsimd, X, "gp")
    vec = EngProxy(nc.vector, X, "dv")
    sca = EngProxy(nc.scalar, X, "ac")
    ten = EngProxy(nc.tensor)

    def di(name, shape, dtype=F32):
        return nc.dram_tensor(name, list(shape), dtype, kind="ExternalInput").ap()

    xh = di("xh", [TE, D])
    wqT = di("wqT", [NH, 128, DC, DH])
    wkT = di("wkT", [NH, 128, DC, DH])
    wvT = di("wvT", [4, 128, DC, 512])
    woT = di("woT", [4, 128, DC, 512])
    gateT = di("gateT", [128, DC, E])
    wsg = di("wsg", [ISC, 128, DC, 128])
    wsu = di("wsu", [ISC, 128, DC, 128])
    wsdA = di("wsdA", [4, 128, 16, 512])
    wsdB = di("wsdB", [4, 128, 6, 512])
    wg = di("wg", [2, 3, 128, DC, 512])
    wu = di("wu", [2, 3, 128, DC, 512])
    wd = di("wd", [2, 4, 128, 12, 512])
    cosI = di("cosI", [128, TE])
    sinIS = di("sinIS", [128, TE])
    kmaski = di("kmask", [128, NTE])
    cidv = di("cidv", [1, 8])
    cidohc = di("cidohc", [8, 1])
    eohi = di("eoh", [E, 2])

    y_out = nc.dram_tensor("y", [T, D], F32, kind="ExternalOutput").ap()
    dbg_out = {}
    dbg_shapes = {"dbg_xnT": [D, TE], "dbg_q": [DH, T], "dbg_k": [DH, TE],
                  "dbg_xn2T": [D, T], "dbg_gu": [128, T], "dbg_g0": [128, T],
                  "dbg_aout": [D, T], "dbg_route": [T, 16], "dbg_lists": [1, LISTW],
                  "dbg_slot": [128, 32], "dbg_eo": [2 * CAP, D], "dbg_cnt": [8, 16],
                  "dbg_v": [TE, 512], "dbg_gu_all": [22 * 128, 1024],
                  "dbg_wdv": [128, 22 * 512]}
    for nm in dbg:
        dbg_out[nm] = nc.dram_tensor(nm, dbg_shapes[nm], F32, kind="ExternalOutput").ap()

    lists_pay = nc.dram_tensor("lists_pay", [1, LISTW], F32, kind="Internal").ap()
    lists_all = nc.dram_tensor("lists_all", [8, LISTW], F32, kind="Internal").ap()
    xn2_hbm = nc.dram_tensor("xn2_hbm", [T, D], BF16, kind="Internal").ap()
    xnall = nc.dram_tensor("xnall", [8 * T, D], BF16, kind="Internal").ap()
    eo_hbm = nc.dram_tensor("eo_hbm", [2 * CAP, D], BF16, kind="Internal").ap()
    eoall = nc.dram_tensor("eoall", [16 * CAP, D], BF16, kind="Internal").ap()

    swp_np = np.zeros((128, 128), BF)
    for m in range(128):
        swp_np[m ^ 1, m] = 1
    tq_ = np.arange(128)[None, :]
    tk_ = np.arange(128)[:, None]
    cN = {
        "idc": np.eye(128, dtype=BF),
        "swp": swp_np,
        "mdiag": np.where(tk_ <= tq_, 0, BIGNEG).astype(BF),
        "mfar": np.where(tk_ > tq_, 0, BIGNEG).astype(BF),
        "triu": np.triu(np.ones((128, 128)), 1).astype(BF),
    }
    cN_ap = {k: nc.inline_tensor(v, k).ap() for k, v in cN.items()}
    onescol_b = nc.inline_tensor(np.ones((128, 1), BF), "onescol_b").ap()
    onesrow_f = nc.inline_tensor(np.ones((1, 128), np.float32), "onesrow_f").ap()
    iot16_c = nc.inline_tensor(
        np.broadcast_to(np.arange(16, dtype=np.float32), (128, 16)).copy(), "iot16").ap()
    iotmB_c = nc.inline_tensor(
        (np.broadcast_to(np.arange(16, dtype=np.float32), (128, 16)) - 4e4).copy(), "iotmB").ap()
    iotcol_c = nc.inline_tensor(np.arange(128, dtype=np.float32)[:, None].copy(), "iotcol").ap()
    eye8_f = nc.inline_tensor(np.eye(8, dtype=np.float32), "eye8").ap()
    _rb = np.array([(e % 2) * 8192 + (e // 2) * 1024 for e in range(16)], np.float32)
    rbase_f = nc.inline_tensor(np.broadcast_to(_rb, (128, 16)).copy(), "rbasec").ap()
    eps_f = nc.inline_tensor(np.full((128, 1), 1e-5, np.float32), "epsc").ap()
    triu8_f = nc.inline_tensor(np.triu(np.ones((8, 8), np.float32), 1), "triu8").ap()

    A = nc.alloc_sbuf_tensor
    AT = nc.alloc_sbuf_tensor_at

    def addr(h):
        return nc.lookup_mloc(h).addr

    x_res = A("x_res", [128, NT, D], F32)
    fmA = A("fmA", [128, DC, TE], BF16)
    fmB = A("fmB", [128, 22, 1024], BF16)
    wslA = A("wslA", [128, 8192], BF16)
    wslB = A("wslB", [128, 8192], BF16)
    pool = A("pool", [128, 19552], mybir.dt.uint8)
    pb_ = addr(pool)
    fb_ = addr(fmB)
    vgrp = AT("vgrp", [128, NTE, 512], BF16, offset=fb_ + 16 * 2048)
    gstg = AT("gstg", [128, 2, D], BF16, offset=fb_ + 16 * 2048)
    rbase = AT("rbase", [128, 16], F32, offset=fb_ + 20 * 2048)
    idn = AT("idn", [128, 128], BF16, offset=fb_ + 21 * 2048)
    tabs = AT("tabs", [128, 2, TE], BF16, offset=addr(wslB) + 8704)
    consts = AT("consts", [128, 5, 128], BF16, offset=addr(wslB) + 14848)
    stg = AT("stg", [128, D], F32, offset=pb_ + 0)
    stgB = AT("stgB", [128, 2, D], BF16, offset=pb_ + 0)
    qkraw = AT("qkraw", [128, 2560], BF16, offset=pb_ + 0)
    zrow = AT("zrow", [1, 512], F32, offset=pb_ + 5120)
    stgb = AT("stgb", [128, 2, D], BF16, offset=pb_ + 8192)
    ropeA = AT("ropeA", [128, 512], F32, offset=pb_ + 8192)
    ropeB = AT("ropeB", [128, 512], F32, offset=pb_ + 10240)
    Pt = AT("Pt", [128, 2, 512], BF16, offset=pb_ + 12288)
    bcz = AT("bcz", [128, 512], F32, offset=pb_ + 14336)
    sm = pb_ + 16384
    stats = AT("stats", [128, 16], F32, offset=sm + 0)
    kmk = AT("kmk", [128, NTE], F32, offset=sm + 64)
    rt = AT("rt", [128, 8, 16], F32, offset=sm + 128)
    tokr = AT("tokr", [128, NT, 8], F32, offset=sm + 640)
    slotm = AT("slotm", [128, 40], F32, offset=sm + 896)
    icol = AT("icol", [128, 8], I32, offset=sm + 1056)
    sids = AT("sids", [128, 8], F32, offset=sm + 1088)
    onesc = AT("onesc", [128, 1], BF16, offset=sm + 1120)
    onesrf = AT("onesrf", [1, 128], F32, offset=sm + 1152)
    iot16 = AT("iot16s", [128, 16], F32, offset=sm + 1664)
    iotmB = AT("iotmBs", [128, 16], F32, offset=sm + 1728)
    iotcol = AT("iotcols", [128, 1], F32, offset=sm + 1792)
    eye8 = AT("eye8s", [8, 8], F32, offset=sm + 1824)
    triu8 = AT("triu8s", [8, 8], F32, offset=sm + 1856)
    cid8 = AT("cid8s", [1, 8], F32, offset=sm + 1888)
    cidoh = AT("cidohs", [8, 1], F32, offset=sm + 1920)
    eoh = AT("eohs", [E, 2], F32, offset=sm + 1952)
    ccols = AT("ccolss", [128, 4], F32, offset=sm + 1984)
    crow = AT("crows", [1, 4, 16], F32, offset=sm + 2016)
    cnt8 = AT("cnt8s", [8, 16], F32, offset=sm + 2272)
    cnt8T = AT("cnt8Ts", [16, 8], F32, offset=sm + 2336)
    cume = AT("cumes", [8, 16], F32, offset=sm + 2368)
    col8 = AT("col8s", [8, 1], F32, offset=sm + 2432)
    row8 = AT("row8s", [1, 8], F32, offset=sm + 2464)
    row16 = AT("row16s", [1, 16], F32, offset=sm + 2496)
    rts = AT("rtss", [128, 16], BF16, offset=sm + 2560)
    epsc = AT("epscs", [128, 1], F32, offset=sm + 2592)
    icol4 = AT("icol4", [128, 4], I32, offset=sm + 2624)
    lgbuf = AT("lgbuf", [128, 8, 16], F32, offset=sm + 2656)

    PB = [nc.alloc_psum_tensor(f"pb{i}", [128, 512], F32) for i in range(6)]
    TBa = nc.alloc_psum_tensor("tba", [128, 512], BF16)
    TBb = nc.alloc_psum_tensor("tbb", [128, 512], BF16)

    IDC = consts[:, 0, :]
    SWP = consts[:, 1, :]
    MDIAG = consts[:, 2, :]
    MFAR = consts[:, 3, :]
    TRIU = consts[:, 4, :]

    def slab(dst, src_ap, name="lw", rel="peW"):
        X.wait(gps, rel)
        X.dma(gps, dst, src_ap, name)

    def ts(out, in0, s1, s2=None, op0=ALU.mult, op1=None, eng=None):
        e = eng or vec
        if s2 is None and op1 is None:
            e.tensor_scalar(out, in0, s1, None, op0=op0)
        else:
            e.tensor_scalar(out, in0, s1, s2, op0=op0, op1=op1)

    def tt(out, a, b, op=ALU.mult, eng=None):
        (eng or vec).tensor_tensor(out, a, b, op=op)

    def ln_tile(xt, out_bf, sq_scratch):
        mneg, ss, s0, rstd = (stats[:, i:i + 1] for i in range(4))
        vec.reduce_sum(mneg, xt, axis=AXX, negate=True)
        ts(mneg, mneg, 1.0 / D)
        X.wait(sca, "dv")
        sca.activation(out=sq_scratch, in_=xt, func=AF.Square, bias=mneg, scale=1.0,
                       accum_out=ss)
        sca.activation(out=s0, in_=ss, func=AF.Sqrt, bias=epsc[:, :], scale=1.0 / D)
        X.wait(vec, "ac")
        vec.reciprocal(rstd, s0)
        ts(out_bf, xt, mneg, rstd, op0=ALU.add, op1=ALU.mult)

    def store_x(tag="st"):
        X.wait(sync, "dv")
        for tt_ in range(NT):
            X.dma(sync, y_out[tt_ * 128:(tt_ + 1) * 128, :], x_res[:, tt_, :], tag)
        X.wait(sync, tag)

    X.dma(sync, consts[:, 0, :], cN_ap["idc"][:, :], "lc")
    X.dma(sync, consts[:, 1, :], cN_ap["swp"][:, :], "lc")
    X.dma(sync, consts[:, 2, :], cN_ap["mdiag"][:, :], "lc")
    X.dma(sync, consts[:, 3, :], cN_ap["mfar"][:, :], "lc")
    X.dma(sync, consts[:, 4, :], cN_ap["triu"][:, :], "lc")
    X.dma(sync, onesc[:, :], onescol_b[:, :], "lc")
    X.dma(sync, onesrf[:, :], onesrow_f[:, :], "lc")
    X.dma(sync, iot16[:, :], iot16_c[:, :], "lc")
    X.dma(sync, iotmB[:, :], iotmB_c[:, :], "lc")
    X.dma(sync, iotcol[:, :], iotcol_c[:, :], "lc")
    X.dma(sync, eye8[:, :], eye8_f[:, :], "lc")
    X.dma(sync, triu8[:, :], triu8_f[:, :], "lc")
    X.dma(sync, cid8[:, :], cidv[:, :], "lc")
    X.dma(sync, cidoh[:, :], cidohc[:, :], "lc")
    X.dma(sync, eoh[:, :], eohi[:, :], "lc")
    X.dma(sync, kmk[:, :], kmaski[:, :], "lc")
    X.dma(sync, epsc[:, :], eps_f[:, :], "lc")
    X.dma(gps, tabs[:, 0, :], cosI[:, :], "lcg")
    X.dma(gps, tabs[:, 1, :], sinIS[:, :], "lcg")
    X.wait(ten, "lc")
    ten.matmul(PB[0][:, 0:3], onesrf[:, :], cid8[0:1, 0:3], start=True, stop=True)
    X.tick(ten, "pe")
    X.wait(vec, "pe")
    ts(ccols[:, 0:3], PB[0][:, 0:3], 1024.0)
    X.tick(vec, "dv")

    for te in range(NTE):
        own = te >= 4
        xt = x_res[:, te - 4, :] if own else stg[:, :]
        X.dma(sync, xt, xh[te * 128:(te + 1) * 128, :], "lx")
        X.wait(vec, "lx")
        X.wait(sca, "lx")
        X.wait(vec, "tr")
        X.wait(sca, "tr")
        xnst = stgb[:, te % 2, :]
        ln_tile(xt, xnst, stgb[:, (te + 1) % 2, :])
        X.wait(sync, "dv")
        for dch in range(DC):
            X.dmat(fmA[:, dch, te * 128:(te + 1) * 128],
                   xnst[:, dch * 128:(dch + 1) * 128], "tr")
        X.wait(sync, "tr")

    if "dbg_xnT" in dbg_out:
        X.wait(sync, "tr")
        X.wait(sync, "dv")
        X.wait(vec, "tr")
        for dch in range(DC):
            vec.tensor_copy(stg[:, 0:TE], fmA[:, dch, :])
            X.tick(vec, "dv")
            X.wait(sync, "dv")
            X.dma(sync, dbg_out["dbg_xnT"][dch * 128:(dch + 1) * 128, :], stg[:, 0:TE], "st")
            X.wait(vec, "st")

    if stage == "p1":
        store_x()
        return nc

    aout = fmB
    X.wait(ten, "tr")
    X.wait(ten, "lc")
    X.wait(vec, "lcg")
    X.wait(vec, "lc")
    X.wait(sca, "lc")
    wv_sb = wslA[:, 0:8192].rearrange("p (a b) -> p a b", a=DC)
    wq_sb = wslB[:, 0:2048].rearrange("p (a b) -> p a b", a=DC)
    wk_sb = wslB[:, 2048:4096].rearrange("p (a b) -> p a b", a=DC)
    X.dma(gps, wv_sb[:, :, :], wvT[0, :, :, :], "lwv")
    X.dma(gps, wq_sb[:, :, :], wqT[0, :, :, :], "lwq")
    X.dma(gps, wk_sb[:, :, :], wkT[0, :, :, :], "lwq")
    for hg in range(4):
        X.wait(ten, "lwv")
        dvmarks = []
        for te in range(NTE):
            ps = PB[te % 2]
            if te >= 2:
                X.wait_val(ten, "dv", dvmarks[te - 2])
            for dch in range(DC):
                ten.matmul(ps[:, :], fmA[:, dch, te * 128:(te + 1) * 128],
                           wv_sb[:, dch, :], start=(dch == 0), stop=(dch == DC - 1))
            X.tick(ten, "pe")
            X.wait(vec, "pe")
            vec.tensor_copy(vgrp[:, te, :], ps[:, :])
            dvmarks.append(X.c["dv"])
        if hg < 3:
            X.wait_val(gps, "pe", X.c["pe"])
            X.dma(gps, wv_sb[:, :, :], wvT[hg + 1, :, :, :], "lwv")
        X.wait(ten, "dv")
        if stage == "p2v" and hg == 0:
            store_x()
            return nc
        for hh in range(4):
            h = hg * 4 + hh
            X.wait(ten, "lwq")
            X.wait(ten, "dv")
            X.wait(ten, "ac")
            for tg in range(2):
                for dch in range(DC):
                    ten.matmul(PB[tg][:, :], wq_sb[:, dch, :],
                               fmA[:, dch, 512 + tg * 512:1024 + tg * 512],
                               start=(dch == 0), stop=(dch == DC - 1))
            for sg in range(3):
                for dch in range(DC):
                    ten.matmul(PB[2 + sg][:, :], wk_sb[:, dch, :],
                               fmA[:, dch, sg * 512:(sg + 1) * 512],
                               start=(dch == 0), stop=(dch == DC - 1))
            X.tick(ten, "pe")
            if h < 15:
                X.wait_val(gps, "pe", X.c["pe"])
                X.dma(gps, wq_sb[:, :, :], wqT[h + 1, :, :, :], "lwq")
                X.dma(gps, wk_sb[:, :, :], wkT[h + 1, :, :, :], "lwq")
            X.wait(vec, "pe")
            for tg in range(2):
                vec.tensor_copy(qkraw[:, tg * 512:(tg + 1) * 512], PB[tg][:, :])
            for sg in range(3):
                vec.tensor_copy(qkraw[:, 1024 + sg * 512:1024 + (sg + 1) * 512],
                                PB[2 + sg][:, :])
            chunks = [(512 + c * 512, qkraw[:, c * 512:(c + 1) * 512]) for c in range(2)]                 + [(c * 512, qkraw[:, 1024 + c * 512:1024 + (c + 1) * 512]) for c in range(3)]
            pemarks = []
            dvm = []
            X.wait(ten, "dv")
            for ch, (lo, src) in enumerate(chunks):
                ps = PB[4 + ch % 2]
                if ch >= 2:
                    X.wait_val(ten, "dv", dvm[ch - 2])
                ten.matmul(ps[:, :], SWP, src, start=True, stop=True)
                X.tick(ten, "pe")
                pemarks.append(X.c["pe"])
                X.wait_val(vec, "pe", pemarks[ch])
                tt(ropeA[:, :], src, tabs[:, 0, lo:lo + 512], ALU.mult)
                tt(ropeB[:, :], ps[:, :], tabs[:, 1, lo:lo + 512], ALU.mult)
                tt(src, ropeA[:, :], ropeB[:, :], ALU.add)
                dvm.append(X.c["dv"])
            if stage == "p2q" and h == 0:
                store_x()
                return nc
            X.wait(ten, "dv")
            for g in range(2):
                pvps, zps, bcps = PB[2], PB[3], PB[4]
                rorder = [3, 4, 0, 1, 2, 5, 6, 7]
                geom = []
                for r in rorder:
                    s0, s1 = max(0, r - 4), min(3, r) + 1
                    geom.append((4 * g + r, s0 * 128, (min(3, r) + 1) * 128, r))
                pemarks = []
                acmarks = []
                for ri in range(len(rorder) + 1):
                    if ri < len(rorder):
                        jj, lo, hi, r = geom[ri]
                        st_ = PB[ri % 2]
                        has_far, has_diag = r <= 3, r >= 4
                        ten.matmul(st_[:, lo:hi],
                                   qkraw[:, 1024 + jj * 128:1024 + (jj + 1) * 128],
                                   qkraw[:, g * 512 + lo:g * 512 + hi],
                                   start=True, stop=not (has_far or has_diag))
                        if has_far:
                            ten.matmul(st_[:, r * 128:(r + 1) * 128], IDC, MFAR,
                                       start=False, stop=not has_diag)
                        if has_diag:
                            ten.matmul(st_[:, (r - 4) * 128:(r - 3) * 128], IDC, MDIAG,
                                       start=False, stop=True)
                        X.tick(ten, "pe")
                        pemarks.append(X.c["pe"])
                        X.wait_val(sca, "pe", pemarks[ri])
                        sca.activation(out=Pt[:, ri % 2, lo:hi], in_=st_[:, lo:hi],
                                       func=AF.Exp, bias=kmk[:, jj:jj + 1], scale=SCALE)
                        acmarks.append(X.c["ac"])
                    if ri >= 1:
                        pj = ri - 1
                        jj, lo, hi, r = geom[pj]
                        X.wait_val(ten, "ac", acmarks[pj])
                        last = pj == len(rorder) - 1
                        ten.matmul(zps[0:1, lo:hi], onesc[:, :], Pt[:, pj % 2, lo:hi],
                                   start=(pj == 0), stop=last)
                        ten.matmul(pvps[:, lo:hi],
                                   vgrp[:, jj, hh * 128:(hh + 1) * 128],
                                   Pt[:, pj % 2, lo:hi], start=(pj == 0), stop=last)
                X.tick(ten, "pe")
                X.wait(vec, "pe")
                vec.tensor_copy(zrow[0:1, :], zps[0:1, :])
                vec.reciprocal(zrow[0:1, :], zrow[0:1, :])
                X.tick(vec, "dv")
                X.wait(ten, "dv")
                ten.matmul(bcps[:, :], onesrf[:, :], zrow[0:1, :], start=True, stop=True)
                X.tick(ten, "pe")
                X.wait(vec, "pe")
                vec.tensor_copy(bcz[:, :], bcps[:, :])
                tt(aout[:, h, g * 512:(g + 1) * 512], pvps[:, :], bcz[:, :], ALU.mult)
                X.tick(vec, "dv")
                X.wait(ten, "dv")

    if stage == "p2a":
        store_x()
        return nc
    if "dbg_aout" in dbg_out:
        for dch in range(DC):
            vec.tensor_copy(stg[:, 0:1024], aout[:, dch, :])
            X.tick(vec, "dv")
            X.wait(sync, "dv")
            X.dma(sync, dbg_out["dbg_aout"][dch * 128:(dch + 1) * 128, :],
                  stg[:, 0:1024], "st")
        X.wait(vec, "st")

    wo_a = wslA[:, 0:8192].rearrange("p (a b) -> p a b", a=DC)
    wo_b = wslB[:, 0:8192].rearrange("p (a b) -> p a b", a=DC)
    X.wait(gps, "pe")
    X.wait(gps, "dv")
    X.dma(gps, wo_a[:, :, :], woT[0, :, :, :], "lwoA")
    X.dma(gps, wo_b[:, :, :], woT[1, :, :, :], "lwoB")
    lwo_need = [("lwoA", 16), ("lwoB", 16), None, None]
    for ocg in range(4):
        wo_sb = wo_a if ocg % 2 == 0 else wo_b
        X.wait_val(ten, *lwo_need[ocg])
        X.wait(ten, "dv")
        dvmarks = []
        for tt_ in range(NT):
            ps = PB[tt_ % 2]
            if tt_ >= 2:
                X.wait_val(ten, "dv", dvmarks[tt_ - 2])
            for dch in range(DC):
                ten.matmul(ps[:, :], aout[:, dch, tt_ * 128:(tt_ + 1) * 128],
                           wo_sb[:, dch, :], start=(dch == 0), stop=(dch == DC - 1))
            X.tick(ten, "pe")
            if tt_ == NT - 1 and ocg + 2 < 4:
                nm2 = "lwoA" if ocg % 2 == 0 else "lwoB"
                X.wait_val(gps, "pe", X.c["pe"])
                X.dma(gps, wo_sb[:, :, :], woT[ocg + 2, :, :, :], nm2)
                lwo_need[ocg + 2] = (nm2, X.c[nm2])
            X.wait(vec, "pe")
            xs = x_res[:, tt_, ocg * 512:(ocg + 1) * 512]
            tt(xs, xs, ps[:, :], ALU.add)
            dvmarks.append(X.c["dv"])
        X.wait(ten, "dv")

    if stage == "attn":
        store_x()
        return nc

    gt_sb = wslB[:, 4096:4352].rearrange("p (a b) -> p a b", a=DC)
    wsv = [(wslA[:, 0:2048].rearrange("p (a b) -> p a b", a=DC),
            wslA[:, 2048:4096].rearrange("p (a b) -> p a b", a=DC)),
           (wslA[:, 4096:6144].rearrange("p (a b) -> p a b", a=DC),
            wslA[:, 6144:8192].rearrange("p (a b) -> p a b", a=DC))]
    X.wait(gps, "pe")
    X.dma(gps, gt_sb[:, :, :], gateT[:, :, :], "lw")
    X.dma(gps, consts[:, 4, :], cN_ap["triu"][:, :], "lcg")
    X.dma(gps, wsv[0][0][:, :, :], wsg[0, :, :, :], "lwshA")
    X.dma(gps, wsv[0][1][:, :, :], wsu[0, :, :, :], "lwshA")
    lwsh_need = [("lwshA", 32)]

    for tt_ in range(NT):
        X.wait(vec, "dv")
        X.wait(vec, "sx")
        X.wait(vec, "tr2")
        X.wait(sca, "sx")
        X.wait(sca, "tr2")
        xnst = stgb[:, tt_ % 2, :]
        ln_tile(x_res[:, tt_, :], xnst, stgb[:, (tt_ + 1) % 2, :])
        X.wait(sync, "dv")
        X.dma(sync, xn2_hbm[tt_ * 128:(tt_ + 1) * 128, :], xnst, "sx")
        for dch in range(DC):
            X.dmat(fmA[:, dch, tt_ * 128:(tt_ + 1) * 128],
                   xnst[:, dch * 128:(dch + 1) * 128], "tr2")


    X.wait(gps, "sx")
    X.inc(gps.collective_compute(
        "AllGather", ALU.bypass, ins=[xn2_hbm[:, :]], outs=[xnall[:, :]],
        replica_groups=[list(range(8))]), "cc2", 1)

    vec.memset(stg[:, 0:128], 0.0)
    vec.memset(crow[0:1, 0, :], 0.0)
    X.tick(vec, "dv")
    X.wait(sync, "dv")
    X.dma(sync, lists_pay[0:1, 0:16384].rearrange("a (p f) -> (a p) f", p=128),
          stg[:, 0:128], "sl")
    X.wait(ten, "lw")
    X.wait(ten, "lcg")
    X.wait(ten, "tr2")
    X.wait(ten, "dv")
    X.wait(gps, "lc")
    X.wait(vec, "lc")
    rt_dvm = [0]

    def router_tile(tt_):
        X.wait(vec, "slg")
        X.wait_val(ten, "dv", rt_dvm[-1])
        lps = PB[4][:, 0:16]
        for dch in range(DC):
            ten.matmul(lps, fmA[:, dch, tt_ * 128:(tt_ + 1) * 128],
                       gt_sb[:, dch, :], start=(dch == 0), stop=(dch == DC - 1))
        X.tick(ten, "pe")
        X.wait(vec, "pe")
        lg = rt[:, 0, :]
        vec.tensor_copy(lg, lps)
        vec.tensor_copy(lgbuf[:, tt_, :], lps)
        m1, idx1, m2, idx2, w1c, w2c, p1c, p2c = (tokr[:, tt_, i:i + 1] for i in range(8))
        vec.reduce_max(m1, lg, axis=AXX)
        E1, E2, scr = rt[:, 1, :], rt[:, 2, :], rt[:, 3, :]
        ts(scr, lg, m1, None, op0=ALU.is_ge)
        tt(scr, scr, iotmB[:, :], ALU.mult)
        ts(scr, scr, 4e4, op0=ALU.add)
        vec.tensor_reduce(idx1, scr, axis=AXX, op=ALU.min)
        ts(E1, iot16[:, :], idx1, None, op0=ALU.is_equal)
        ts(scr, E1, 1e6)
        tt(scr, lg, scr, ALU.subtract)
        vec.reduce_max(m2, scr, axis=AXX)
        ts(rt[:, 4, :], scr, m2, None, op0=ALU.is_ge)
        tt(rt[:, 4, :], rt[:, 4, :], iotmB[:, :], ALU.mult)
        ts(rt[:, 4, :], rt[:, 4, :], 4e4, op0=ALU.add)
        vec.tensor_reduce(idx2, rt[:, 4, :], axis=AXX, op=ALU.min)
        ts(E2, iot16[:, :], idx2, None, op0=ALU.is_equal)
        tt(rts[:, :], E1, E2, ALU.add)
        X.tick(vec, "dv")
        X.wait(ten, "dv")
        cps = PB[5][:, 0:16]
        ten.matmul(cps, TRIU, rts[:, :], start=True, stop=False)
        ten.matmul(cps, onesrf[:, :], crow[0:1, 0, :], start=False, stop=True)
        ten.matmul(PB[5][0:1, 16:32], onesc[:, :], rts[:, :], start=True, stop=True)
        X.tick(ten, "pe")
        X.wait(vec, "pe")
        Cc = rt[:, 5, :]
        vec.tensor_copy(Cc, cps)
        vec.tensor_copy(crow[0:1, 1, :], PB[5][0:1, 16:32])
        vec.tensor_tensor(crow[0:1, 0, :], crow[0:1, 0, :], crow[0:1, 1, :],
                          op=ALU.add)
        tt(rt[:, 6, :], Cc, E1, ALU.mult)
        vec.reduce_sum(p1c, rt[:, 6, :], axis=AXX)
        tt(rt[:, 6, :], Cc, E2, ALU.mult)
        vec.reduce_sum(p2c, rt[:, 6, :], axis=AXX)
        gidc = stats[:, 10:11]
        ts(gidc, ccols[:, 0:1], 1.0, float(128 * tt_), op0=ALU.mult, op1=ALU.add)
        tt(gidc, gidc, iotcol[:, :], ALU.add)
        for kk, (idxc, posc) in enumerate(((idx1, p1c), (idx2, p2c))):
            sp = stats[:, 11 + kk:12 + kk]
            ts(sp, idxc, 1024.0)
            tt(sp, sp, posc, ALU.add)
            ovg = stats[:, 13 + kk:14 + kk]
            ts(ovg, posc, 1024.0, None, op0=ALU.is_ge)
            ts(ovg, ovg, GUARD)
            tt(sp, sp, ovg, ALU.add)
            vec.tensor_copy(icol[:, kk:kk + 1], sp)
        X.tick(vec, "dv")
        rt_dvm.append(X.c["dv"])
        X.wait(gps, "dv")
        X.wait(gps, "sl")
        for kk in range(2):
            X.inc(gps.indirect_dma_start(
                out=lists_pay[:, :], out_offset=IOA(ap=icol[:, kk:kk + 1], axis=1),
                in_=gidc, in_offset=None,
                bounds_check=16383, oob_is_err=False), "slg", 16)
        if "dbg_route" in dbg_out:
            vec.tensor_copy(stg[:, 0:8], tokr[:, tt_, :])
            X.tick(vec, "dv")
            X.wait(sync, "dv")
            X.dma(sync, dbg_out["dbg_route"][tt_ * 128:(tt_ + 1) * 128, 0:8],
                  stg[:, 0:8], "st")
    pewm = [0] * ISC
    gu_acm = []
    gu_dvm = []
    for is_ in range(ISC):
        if is_ % 2 == 0 and is_ // 2 < NT:
            router_tile(is_ // 2)
        if is_ == 15:
            X.wait(sync, "dv")
            X.dma(sync, lists_pay[0:1, 16384:16400], crow[0:1, 0, :], "sl")
        wsgv, wsuv = wsv[is_ % 2]
        X.wait_val(ten, *lwsh_need[is_])
        if is_ >= 1:
            X.wait_val(ten, "ac", gu_acm[is_ - 1])
            X.wait_val(ten, "dv", gu_dvm[is_ - 1])
        for tg in range(2):
            for dch in range(DC):
                ten.matmul(PB[0 + tg][:, :], wsgv[:, dch, :],
                           fmA[:, dch, tg * 512:(tg + 1) * 512],
                           start=(dch == 0), stop=(dch == DC - 1))
                ten.matmul(PB[2 + tg][:, :], wsuv[:, dch, :],
                           fmA[:, dch, tg * 512:(tg + 1) * 512],
                           start=(dch == 0), stop=(dch == DC - 1))
        X.tick(ten, "pe")
        pewm[is_] = X.c["pe"]
        if is_ + 1 < ISC:
            j = is_ + 1
            if j >= 2:
                X.wait_val(gps, "pe", pewm[j - 2])
            nm2 = "lwshA" if j % 2 == 0 else "lwshB"
            X.dma(gps, wsv[j % 2][0][:, :, :], wsg[j, :, :, :], nm2)
            X.dma(gps, wsv[j % 2][1][:, :, :], wsu[j, :, :, :], nm2)
            lwsh_need.append((nm2, X.c[nm2]))
        X.wait(sca, "pe")
        X.wait(vec, "pe")
        if "dbg_g0" in dbg_out and is_ == 0:
            vec.tensor_copy(stg[:, 0:512], PB[0][:, :])
            vec.tensor_copy(stg[:, 512:1024], PB[1][:, :])
            X.wait(sync, "dv")
            X.dma(sync, dbg_out["dbg_g0"][:, :], stg[:, 0:1024], "st")
            X.wait(vec, "st")
        if "dbg_slot" in dbg_out and is_ == 0:
            vec.tensor_copy(stg[:, 0:16], wsgv[:, 0, 0:16])
            vec.tensor_copy(stg[:, 16:32], wsgv[:, 1, 0:16])
            X.wait(sync, "dv")
            X.dma(sync, dbg_out["dbg_slot"][:, :], stg[:, 0:32], "st")
            X.wait(vec, "st")
        for tg in range(2):
            sca.activation(out=stgb[:, 0, tg * 512:(tg + 1) * 512], in_=PB[tg][:, :],
                           func=AF.Silu)
            vec.tensor_copy(stgb[:, 1, tg * 512:(tg + 1) * 512], PB[2 + tg][:, :])
        X.tick(sca, "ac")
        gu_acm.append(X.c["ac"])
        X.wait(vec, "ac")
        tt(fmB[:, is_, :], stgb[:, 0, 0:1024], stgb[:, 1, 0:1024], ALU.mult)
        X.tick(vec, "dv")
        gu_dvm.append(X.c["dv"])
        if stage.startswith("p7a") and is_ == int(stage[3:] or 0):
            vec.tensor_copy(stg[:, 0:1024], fmB[:, 0, :])
            vec.tensor_copy(stg[:, 1024:2048], fmB[:, min(is_, 1), :])
            X.wait(sync, "dv")
            for tt_ in range(NT):
                X.dma(sync, y_out[tt_ * 128:(tt_ + 1) * 128, :],
                      stg[:, :] if tt_ == 0 else x_res[:, tt_, :], "st")
            X.wait(sync, "st")
            return nc
    X.wait(gps, "sl")
    X.wait(gps, "slg")
    X.inc(gps.collective_compute(
        "AllGather", ALU.bypass, ins=[lists_pay[:, :]], outs=[lists_all[:, :]],
        replica_groups=[list(range(8))]), "cc1", 1)
    if "dbg_lists" in dbg_out:
        X.wait(sync, "sl")
        X.wait(vec, "sl")
        vec.memset(stats[:, 15:16], 0.0)
        X.dma(sync, dbg_out["dbg_lists"][:, :], lists_pay[:, :], "st")
    if "dbg_xn2T" in dbg_out:
        for dch in range(DC):
            vec.tensor_copy(stg[:, 0:1024], fmA[:, dch, 0:1024])
            X.wait(sync, "dv")
            X.dma(sync, dbg_out["dbg_xn2T"][dch * 128:(dch + 1) * 128, :],
                  stg[:, 0:1024], "st")
            X.wait(vec, "st")
    if "dbg_gu" in dbg_out:
        vec.tensor_copy(stg[:, 0:1024], fmB[:, 0, :])
        X.wait(sync, "dv")
        X.dma(sync, dbg_out["dbg_gu"][:, :], stg[:, 0:1024], "st")
        X.wait(vec, "st")
    if "dbg_gu_all" in dbg_out:
        for ch in range(22):
            X.wait(vec, "dv")
            vec.tensor_copy(stg[:, 0:1024], fmB[:, ch, :])
            X.wait(sync, "dv")
            X.dma(sync, dbg_out["dbg_gu_all"][ch * 128:(ch + 1) * 128, :],
                  stg[:, 0:1024], "st")
            X.wait(vec, "st")
    if stage == "full":
      def p8_pre():
        X.wait(sync, "cc1")
        X.dma(sync, cnt8[:, :], lists_all[:, 16384:16400], "lg")
        X.wait(ten, "lg")
        X.wait(ten, "ac")
        X.wait(ten, "dv")
        ten.transpose(PB[0][0:16, 0:8], cnt8[:, :], eye8[:, :])
        ten.matmul(PB[1][0:8, 0:16], triu8[:, :], cnt8[:, :], start=True, stop=True)
        X.tick(ten, "pe")
        X.wait(vec, "pe")
        vec.tensor_copy(cnt8T[:, :], PB[0][0:16, 0:8])
        vec.tensor_copy(cume[:, :], PB[1][0:8, 0:16])
        X.tick(vec, "dv")
        X.wait(ten, "dv")
        ten.matmul(PB[2][0:1, 0:16], cidoh[:, :], cume[:, :], start=True, stop=True)
        X.tick(ten, "pe")
        X.wait(vec, "pe")
        vec.tensor_copy(row16[:, :], PB[2][0:1, 0:16])
        X.tick(vec, "dv")
        X.wait(ten, "dv")
        ten.matmul(PB[2][:, 0:16], onesrf[:, :], row16[:, :], start=True, stop=True)
        X.tick(ten, "pe")
        X.wait(vec, "pe")
        vec.tensor_copy(rt[:, 7, :], PB[2][:, 0:16])
        X.tick(vec, "dv")
        gps.iota(slotm[:, 8:16], pattern=[[128, 8]], base=0, channel_multiplier=1,
                 allow_small_or_imprecise_dtypes=True)
        X.tick(gps, "gp")
        X.wait(gps, "cc1")
        return sids_gather(boundary_math(0))
      if True:
        def boundary_math(o):
            X.wait(ten, "dv")
            ten.matmul(PB[3][0:8, 0:1], cnt8T[:, :], eoh[:, o:o + 1], start=True,
                       stop=True)
            X.tick(ten, "pe")
            X.wait(vec, "pe")
            X.wait(vec, "gp")
            vec.tensor_copy(col8[:, :], PB[3][0:8, 0:1])
            X.tick(vec, "dv")
            X.wait(ten, "dv")
            ten.matmul(PB[3][0:1, 0:8], col8[:, :], triu8[:, :], start=True, stop=True)
            X.tick(ten, "pe")
            X.wait(vec, "pe")
            vec.tensor_copy(row8[:, :], PB[3][0:1, 0:8])
            X.tick(vec, "dv")
            X.wait(ten, "dv")
            ten.matmul(PB[3][:, 0:8], onesrf[:, :], row8[:, :], start=True, stop=True)
            X.tick(ten, "pe")
            X.wait(vec, "pe")
            bnd = slotm[:, 0:8]
            vec.tensor_copy(bnd, PB[3][:, 0:8])
            sio = slotm[:, 8:16]
            cidx = slotm[:, 16:24]
            base = slotm[:, 24:32]
            vec.memset(cidx, 0.0)
            vec.memset(base, 0.0)
            for c in range(1, 8):
                ge = slotm[:, 32:40]
                ts(ge, sio, bnd[:, c:c + 1], None, op0=ALU.is_ge)
                tt(cidx, cidx, ge, ALU.add)
                ts(ge, ge, bnd[:, c:c + 1], None, op0=ALU.mult)
                tt(base, base, ge, ALU.max)
            fl = slotm[:, 32:40]
            ts(fl, cidx, float(LISTW))
            tt(fl, fl, sio, ALU.add)
            tt(fl, fl, base, ALU.subtract)
            ts(fl, fl, ccols[:, 1 + o:2 + o], None, op0=ALU.add)
            vec.tensor_copy(icol[:, 0:8], fl)
            X.tick(vec, "dv")
            return X.c["dv"]

        def sids_gather(dvmark):
            X.wait_val(gps, "dv", dvmark)
            for col in range(8):
                X.inc(gps.indirect_dma_start(
                    out=sids[:, col:col + 1], out_offset=None,
                    in_=lists_all.rearrange("a b -> (a b)")[None, :],
                    in_offset=IOA(ap=icol[:, col:col + 1], axis=1)), "lgg", 16)
            X.wait(vec, "lgg")
            vec.tensor_copy(icol[:, 0:8], sids[:, :])
            X.tick(vec, "dv")
            return X.c["dv"]

        STGC = [stgb[:, 0, :], stgb[:, 1, :], stgB[:, 0, :], stgB[:, 1, :]]

        def stg_ap(i):
            if i < 4:
                return STGC[i]
            q = i - 4
            return fmA[:, 4 * q:4 * q + 4, 1024:1536]

        def stg_in(i, dch):
            if i < 4:
                return STGC[i][:, dch * 128:(dch + 1) * 128]
            q = i - 4
            c0 = 1024 + (dch % 4) * 128
            return fmA[:, 4 * q + dch // 4, c0:c0 + 128]

        xnflat512 = xnall.rearrange("a (c w) -> (a c) w", w=512)

        def xn_gather(col, buf, pem=None, dvm=None):
            if pem:
                X.wait_val(gps, "pe", pem)
            if dvm:
                X.wait_val(gps, "dv", dvm)
            if buf < 4:
                X.inc(gps.indirect_dma_start(
                    out=stg_ap(buf), out_offset=None, in_=xnall[:, :],
                    in_offset=IOA(ap=icol[:, col:col + 1], axis=0)), "lgg", 16)
            else:
                q = buf - 4
                X.wait(vec, "lgg")
                for a in range(4):
                    ts(rt[:, 3, a:a + 1], sids[:, col:col + 1], 4.0, float(a),
                       op0=ALU.mult, op1=ALU.add)
                vec.tensor_copy(icol4[:, :], rt[:, 3, 0:4])
                X.tick(vec, "dv")
                X.wait_val(gps, "dv", X.c["dv"])
                for a in range(4):
                    X.inc(gps.indirect_dma_start(
                        out=fmA[:, 4 * q + a, 1024:1536], out_offset=None,
                        in_=xnflat512,
                        in_offset=IOA(ap=icol4[:, a:a + 1], axis=0)), "lgg", 16)

        pre_dv = [0]

    def weight_pass(t):
        m1, m2 = tokr[:, t, 0:1], tokr[:, t, 2:3]
        w1c, w2c = tokr[:, t, 4:5], tokr[:, t, 5:6]
        sc1, sc2 = stats[:, 8:9], stats[:, 9:10]
        ts(sc1, m1, -1.0)
        X.wait(sca, "dv")
        sca.activation(out=rt[:, 4, :], in_=lgbuf[:, t, :], func=AF.Exp, bias=sc1,
                       scale=1.0, accum_out=sc2)
        X.wait(vec, "ac")
        vec.reciprocal(w1c, sc2)
        tt(sc1, m2, m1, ALU.subtract)
        X.wait(sca, "dv")
        sca.activation(out=sc2, in_=sc1, func=AF.Exp)
        X.wait(vec, "ac")
        tt(w2c, sc2, w1c, ALU.mult)

    wdvA = wslA[:, 0:16 * 512].rearrange("p (a b) -> p a b", a=16)
    wdvB = wslB[:, 0:6 * 512].rearrange("p (a b) -> p a b", a=6)
    for ocg in range(4):
        X.wait(gps, "pe")
        X.dma(gps, wdvA[:, :, :], wsdA[ocg, :, :, :], "lw")
        X.dma(gps, wdvB[:, :, :], wsdB[ocg, :, :, :], "lw")
        if "dbg_wdv" in dbg_out and ocg == 0:
            X.wait(vec, "lw")
            for ii in range(22):
                wt = wdvA[:, ii, :] if ii < 16 else wdvB[:, ii - 16, :]
                vec.tensor_copy(stg[:, 0:512], wt)
                X.wait(sync, "dv")
                X.dma(sync, dbg_out["dbg_wdv"][:, ii * 512:(ii + 1) * 512],
                      stg[:, 0:512], "st")
                X.wait(vec, "st")
        if stage == "full" and ocg > 0:
            if ocg == 1:
                pre_dv[0] = p8_pre()
                X.wait(gps, "cc2")
                X.wait(gps, "ac")
            cols = {1: (0, 1, 2), 2: (3, 4, 5), 3: (6, 7)}[ocg]
            for cb in cols:
                xn_gather(cb, cb, dvm=pre_dv[0] if cb == 0 else None)
        X.wait(ten, "lw")
        X.wait(ten, "dv")
        for tt_ in range(NT):
            ps = PB[4 + tt_ % 2]
            X.wait(ten, "dv")
            for is_ in range(ISC):
                wtile = wdvA[:, is_, :] if is_ < 16 else wdvB[:, is_ - 16, :]
                ten.matmul(ps[:, :], fmB[:, is_, tt_ * 128:(tt_ + 1) * 128],
                           wtile, start=(is_ == 0), stop=(is_ == ISC - 1))
            X.tick(ten, "pe")
            X.wait(vec, "pe")
            xs = x_res[:, tt_, ocg * 512:(ocg + 1) * 512]
            tt(xs, xs, ps[:, :], ALU.add)
            X.tick(vec, "dv")
            gi = ocg * 8 + tt_
            if gi % 4 == 0:
                weight_pass(gi // 4)
        X.wait(ten, "dv")
    pe_downend = X.c["pe"]

    if stage == "shared":
        store_x()
        return nc

    X.wait(vec, "lgg")
    dvm_ids1 = sids_gather(boundary_math(1))
    X.wait(sync, "pe")
    X.dma(sync, idn[:, :], cN_ap["idc"][:, :], "lc2")
    X.dma(sync, rbase[:, :], rbase_f[:, :], "lc2")

    tp_dvm = []

    def transpose_col(col, buf):
        X.wait(ten, "lgg")
        X.wait_val(ten, "lc2", 32)
        for q in range(4):
            k4 = len(tp_dvm)
            TB = TBa if k4 % 2 == 0 else TBb
            if k4 >= 2:
                X.wait_val(ten, "dv", tp_dvm[k4 - 2])
            else:
                X.wait(ten, "dv")
            for k in range(4):
                ten.transpose(TB[:, k * 128:(k + 1) * 128], stg_in(buf, q * 4 + k),
                              idn[:, :])
            X.tick(ten, "pe")
            X.wait(vec, "pe")
            for k in range(4):
                vec.tensor_copy(fmA[:, q * 4 + k, col * 128:(col + 1) * 128],
                                TB[:, k * 128:(k + 1) * 128])
            tp_dvm.append(X.c["dv"])
        return X.c["pe"]

    for col in range(8):
        pe_tp0 = transpose_col(col, col)

    def expert_gu(o, pem0=None):
        wg_sb = wslA[:, 0:8192].rearrange("p (a b) -> p a b", a=DC)
        wu_sb = wslB[:, 0:8192].rearrange("p (a b) -> p a b", a=DC)
        for igrp in range(3):
            if igrp == 0 and pem0:
                X.wait_val(gps, "pe", pem0)
            else:
                X.wait(gps, "pe")
            X.dma(gps, wg_sb[:, :, :], wg[o, igrp, :, :, :], "lw")
            X.dma(gps, wu_sb[:, :, :], wu[o, igrp, :, :, :], "lw")
            X.wait(ten, "lw")
            for ic in range(4):
                X.wait(ten, "ac")
                X.wait(ten, "dv")
                for tg in range(2):
                    for dch in range(DC):
                        ten.matmul(PB[tg][:, :], wg_sb[:, dch, ic * 128:(ic + 1) * 128],
                                   fmA[:, dch, tg * 512:(tg + 1) * 512],
                                   start=(dch == 0), stop=(dch == DC - 1))
                        ten.matmul(PB[2 + tg][:, :],
                                   wu_sb[:, dch, ic * 128:(ic + 1) * 128],
                                   fmA[:, dch, tg * 512:(tg + 1) * 512],
                                   start=(dch == 0), stop=(dch == DC - 1))
                X.tick(ten, "pe")
                X.wait(sca, "pe")
                X.wait(vec, "pe")
                for tg in range(2):
                    sca.activation(out=stgb[:, 0, tg * 512:(tg + 1) * 512],
                                   in_=PB[tg][:, :], func=AF.Silu)
                    vec.tensor_copy(stgb[:, 1, tg * 512:(tg + 1) * 512], PB[2 + tg][:, :])
                X.tick(sca, "ac")
                X.wait(vec, "ac")
                tt(fmB[:, igrp * 4 + ic, 0:1024], stgb[:, 0, 0:1024], stgb[:, 1, 0:1024],
                   ALU.mult)
                X.tick(vec, "dv")
        return X.c["pe"]

    def expert_down(o):
        wdx = wslA[:, 0:12 * 512].rearrange("p (a b) -> p a b", a=12)
        for ocg in range(4):
            X.wait(gps, "pe")
            X.dma(gps, wdx[:, :, :], wd[o, ocg, :, :, :], "lw")
            X.wait(ten, "lw")
            X.wait(ten, "dv")
            for rs in range(8):
                ps = PB[4 + rs % 2]
                X.wait(ten, "dv")
                for is_ in range(12):
                    ten.matmul(ps[:, :], fmB[:, is_, rs * 128:(rs + 1) * 128],
                               wdx[:, is_, :], start=(is_ == 0), stop=(is_ == 11))
                X.tick(ten, "pe")
                X.wait(vec, "pe")
                vec.tensor_copy(stgb[:, rs % 2, 0:512], ps[:, :])
                X.tick(vec, "dv")
                X.wait(sync, "dv")
                X.dma(sync, eo_hbm[o * 1024 + rs * 128:o * 1024 + (rs + 1) * 128,
                                   ocg * 512:(ocg + 1) * 512], stgb[:, rs % 2, 0:512],
                      "se")
            X.wait(vec, "se")
        return X.c["pe"]

    expert_gu(0, pem0=pe_downend)
    for c in range(6):
        xn_gather(c, 2 + c, pem=pe_tp0, dvm=dvm_ids1 if c == 0 else None)
    pe_down0 = expert_down(0)
    X.wait(gps, "se")
    X.inc(gps.collective_compute(
        "AllGather", ALU.bypass, ins=[eo_hbm[0:1024, :]], outs=[eoall[0:8192, :]],
        replica_groups=[list(range(8))]), "cc3", 1)
    for c in range(6):
        pe_tp1 = transpose_col(c, 2 + c)
        if c == 1:
            xn_gather(6, 2, pem=pe_tp1)
            xn_gather(7, 3, pem=pe_tp1)
    for c in (6, 7):
        transpose_col(c, c - 4)
    expert_gu(1, pem0=pe_down0)
    expert_down(1)
    if "dbg_eo" in dbg_out:
        X.wait(sync, "se")
        for rr in range(16):
            X.dma(sync, gstg[:, 0, :], eo_hbm[rr * 128:(rr + 1) * 128, :], "lg")
            X.wait(vec, "lg")
            vec.tensor_copy(stg[:, :], gstg[:, 0, :])
            X.tick(vec, "dv")
            X.wait(sync, "dv")
            X.dma(sync, dbg_out["dbg_eo"][rr * 128:(rr + 1) * 128, :], stg[:, :], "st")
        X.wait(vec, "st")
    if "dbg_cnt" in dbg_out:
        X.wait(sync, "dv")
        vec.tensor_copy(stg[0:8, 0:16], cnt8[:, :])
        X.tick(vec, "dv")
        X.wait(sync, "dv")
        X.dma(sync, dbg_out["dbg_cnt"][:, :], stg[0:8, 0:16], "st")
        X.wait(vec, "st")

    X.wait(gps, "se")
    X.inc(gps.collective_compute(
        "AllGather", ALU.bypass, ins=[eo_hbm[1024:2048, :]], outs=[eoall[8192:16384, :]],
        replica_groups=[list(range(8))]), "cc3", 1)

    X.wait(gps, "cc3")
    X.wait(vec, "cc3")
    X.wait(vec, "lc2")
    for tt_ in range(NT):
        for kk in range(2):
            idxc = tokr[:, tt_, 1 + 2 * kk:2 + 2 * kk]
            posc = tokr[:, tt_, 6 + kk:7 + kk]
            ek = rt[:, 0, :]
            ts(ek, iot16[:, :], idxc, None, op0=ALU.is_equal)
            tt(ek, ek, rt[:, 7, :], ALU.mult)
            offc = stats[:, 8:9]
            vec.reduce_sum(offc, ek, axis=AXX)
            gp_ = stats[:, 9:10]
            tt(gp_, posc, offc, ALU.add)
            vec.tensor_scalar_min(gp_, gp_, 1023.0)
            ts(rt[:, 3, :], iot16[:, :], idxc, None, op0=ALU.is_equal)
            tt(rt[:, 3, :], rt[:, 3, :], rbase[:, :], ALU.mult)
            vec.reduce_sum(stats[:, 10:11], rt[:, 3, :], axis=AXX)
            tt(gp_, gp_, stats[:, 10:11], ALU.add)
            vec.tensor_copy(icol[:, kk:kk + 1], gp_)
            X.tick(vec, "dv")
            X.wait(gps, "dv")
            X.inc(gps.indirect_dma_start(
                out=gstg[:, kk, :], out_offset=None,
                in_=eoall[:, :],
                in_offset=IOA(ap=icol[:, kk:kk + 1], axis=0)), "lgg", 16)
        X.wait(vec, "lgg")
        for kk in range(2):
            wc = tokr[:, tt_, 4 + kk:5 + kk]
            vec.tensor_copy(stg[:, :], gstg[:, kk, :])
            vec.scalar_tensor_tensor(x_res[:, tt_, :], stg[:, :], wc,
                                     x_res[:, tt_, :], op0=ALU.mult, op1=ALU.add)
        X.tick(vec, "dv")
    store_x()
    return nc


def rope_tabs_for_core(c):
    s0 = (c % 2) * 1024
    pos = np.maximum(np.arange(TE) + s0 - 512, 0).astype(np.float32)
    inv = (1.0 / (10000.0 ** (np.arange(0, DH, 2, dtype=np.float32) / np.float32(DH)))).astype(np.float32)
    fr = (pos[None, :] * inv[:, None]).astype(np.float32)
    cos, sin = np.cos(fr).astype(np.float32), np.sin(fr).astype(np.float32)
    cosI = np.empty((128, TE), np.float32)
    sinIS = np.empty((128, TE), np.float32)
    cosI[0::2] = cos
    cosI[1::2] = cos
    sinIS[0::2] = -sin
    sinIS[1::2] = sin
    return cosI, sinIS


def prep_inputs(L0_attn_x, L0_attn_Wq, L0_attn_Wk, L0_attn_Wv, L0_attn_Wo,
                L0_ffn_gate_w, L0_ffn_w_gate, L0_ffn_w_up, L0_ffn_w_down,
                L0_ffn_ws_gate, L0_ffn_ws_up, L0_ffn_ws_down):
    f = np.ascontiguousarray

    def pmaj(w, width):
        K = w.shape[0]
        return w.reshape(K // 128, 128, width).transpose(1, 0, 2)

    WqT = np.stack([pmaj(L0_attn_Wq.T[:, h * DH:(h + 1) * DH], DH) for h in range(NH)])
    WkT = np.stack([pmaj(L0_attn_Wk.T[:, h * DH:(h + 1) * DH], DH) for h in range(NH)])
    WvT = np.stack([pmaj(L0_attn_Wv.T[:, g * 512:(g + 1) * 512], 512) for g in range(4)])
    WoT = np.stack([pmaj(L0_attn_Wo.T[:, g * 512:(g + 1) * 512], 512) for g in range(4)])
    gT = pmaj(f(L0_ffn_gate_w.T), E)
    wsg = np.stack([pmaj(L0_ffn_ws_gate[:, i * 128:(i + 1) * 128], 128) for i in range(ISC)])
    wsu = np.stack([pmaj(L0_ffn_ws_up[:, i * 128:(i + 1) * 128], 128) for i in range(ISC)])
    wsdA = np.stack([pmaj(L0_ffn_ws_down[0:2048, g * 512:(g + 1) * 512], 512) for g in range(4)])
    wsdB = np.stack([pmaj(L0_ffn_ws_down[2048:ISH, g * 512:(g + 1) * 512], 512) for g in range(4)])
    wgp = np.zeros((E, D, IPAD), np.float32)
    wgp[:, :, :1408] = L0_ffn_w_gate
    wup = np.zeros((E, D, IPAD), np.float32)
    wup[:, :, :1408] = L0_ffn_w_up
    wdp = np.zeros((E, IPAD, D), np.float32)
    wdp[:, :1408, :] = L0_ffn_w_down
    wg_s = np.stack([np.stack([pmaj(wgp[e][:, g * 512:(g + 1) * 512], 512) for g in range(3)])
                     for e in range(E)])
    wu_s = np.stack([np.stack([pmaj(wup[e][:, g * 512:(g + 1) * 512], 512) for g in range(3)])
                     for e in range(E)])
    wd_s = np.stack([np.stack([pmaj(wdp[e][:, g * 512:(g + 1) * 512], 512) for g in range(4)])
                     for e in range(E)])

    in_maps = []
    for c in range(8):
        b, hf = c // 2, c % 2
        own = L0_attn_x[b, hf * 1024:(hf + 1) * 1024, :]
        if hf == 0:
            halo = np.zeros((HAL, D), np.float32)
        else:
            halo = L0_attn_x[b, 512:1024, :]
        xh = np.concatenate([halo, own], 0).astype(np.float32)
        cosI, sinIS = rope_tabs_for_core(c)
        km = np.zeros((128, NTE), np.float32)
        if hf == 0:
            km[:, 0:4] = BIGNEG
        cidv = np.zeros((1, 8), np.float32)
        cidv[0, 0], cidv[0, 1], cidv[0, 2] = c, 2 * c, 2 * c + 1
        cidohc = np.zeros((8, 1), np.float32)
        cidohc[c, 0] = 1
        eoh = np.zeros((E, 2), np.float32)
        eoh[2 * c, 0] = 1
        eoh[2 * c + 1, 1] = 1
        in_maps.append(dict(
            xh=xh, wqT=f(WqT), wkT=f(WkT), wvT=f(WvT), woT=f(WoT), gateT=f(gT),
            wsg=f(wsg), wsu=f(wsu), wsdA=f(wsdA), wsdB=f(wsdB),
            wg=f(wg_s[2 * c:2 * c + 2]), wu=f(wu_s[2 * c:2 * c + 2]),
            wd=f(wd_s[2 * c:2 * c + 2]),
            cosI=cosI, sinIS=sinIS, kmask=km, cidv=cidv, cidohc=cidohc, eoh=eoh))
    return in_maps
_NC_CACHE = {}


def kernel(**inputs):
    from concourse.bass_utils import run_bass_kernel_spmd
    in_maps = prep_inputs(**inputs)
    if "nc" not in _NC_CACHE:
        _NC_CACHE["nc"] = build_nc(stage="full")
    nc = _NC_CACHE["nc"]
    res = run_bass_kernel_spmd(nc, in_maps, core_ids=list(range(8)))
    x = inputs["L0_attn_x"]
    out = np.zeros((4, 2048, 2048), np.float32)
    for c in range(8):
        b, hf = c // 2, c % 2
        out[b, hf * 1024:(hf + 1) * 1024, :] = res.results[c]["y"]
    return out
